# version 04
# baseline (speedup 1.0000x reference)
import numpy as np
from contextlib import ExitStack
import concourse.bass as bass
import concourse.mybir as mybir
from concourse.bass_utils import run_bass_kernel_spmd

F32 = mybir.dt.float32
BF16 = mybir.dt.bfloat16
I32 = mybir.dt.int32
AL = mybir.AluOpType
AF = mybir.ActivationFunctionType
AX = mybir.AxisListType

T, D, IN, NS, R, HF = 2048, 2048, 4096, 16, 128, 8192
Q = 256
NCH = T // Q
NDC = D // 128
NIC = IN // 128
NHC = HF // 128
NG = 4
GN = NS // NG
EPS = 1e-5
NCORES = 8

ENGS = ("sync", "tensor", "vector", "scalar", "gpsimd")
DMA_ENGS = {"sync"}


class Asm:

    def __init__(self):
        self.ops = []
        self.final_waits = []

    def emit(self, eng, build, reads=(), writes=(), sfence=False, is_dma=False):
        self.ops.append([eng, build, tuple(reads), tuple(writes), False, sfence,
                         is_dma or eng in DMA_ENGS])

    def assemble(self, nc, block):
        ops = self.ops
        n = len(ops)
        last_w = {}
        readers = {}
        deps = [None] * n
        for i, (eng, _b, rd, wr, _f, sfence, _d) in enumerate(ops):
            dd = set()
            for r in rd:
                j = last_w.get(r)
                if j is not None and (ops[j][0] != eng or sfence):
                    dd.add(j)
            for r in wr:
                j = last_w.get(r)
                if j is not None and (ops[j][0] != eng or sfence):
                    dd.add(j)
                for j in readers.get(r, ()):
                    if ops[j][0] != eng or sfence:
                        dd.add(j)
            for r in rd:
                readers.setdefault(r, []).append(i)
            for r in wr:
                last_w[r] = i
                readers[r] = []
            deps[i] = dd
        for i in range(n):
            for j in deps[i]:
                ops[j][4] = True
        def key_of(i):
            eng, _b, _r, wr, _m, _sf, isd = ops[i]
            if isd:
                assert wr, f"dma op {i} needs a write resource"
                return "d:" + wr[0]
            return eng
        for op in ops:
            if op[6]:
                op[4] = True
        last_of = {}
        for i, op in enumerate(ops):
            last_of[op[0]] = i
        for e in ENGS:
            if e in last_of:
                ops[last_of[e]][4] = True
        kcount = {}
        inc_val = [0] * n
        for i in range(n):
            k = key_of(i)
            if ops[i][4]:
                kcount[k] = kcount.get(k, 0) + (16 if ops[i][6] else 1)
            inc_val[i] = kcount.get(k, 0)
        need_keys = [None] * n
        for i in range(n):
            ks = set()
            for j in deps[i]:
                ks.add(key_of(j))
            need_keys[i] = ks
        run_cnt = {}
        wait_val = [None] * n
        for i in range(n):
            if need_keys[i]:
                wv = {}
                for j in deps[i]:
                    k = key_of(j)
                    if k.startswith("d:"):
                        wv[k] = run_cnt.get(k, 0)
                    else:
                        wv[k] = max(wv.get(k, 0), inc_val[j])
                wait_val[i] = wv
            k = key_of(i)
            if ops[i][4]:
                run_cnt[k] = run_cnt.get(k, 0) + (16 if ops[i][6] else 1)
        finals = dict(run_cnt)
        all_keys = sorted(finals)
        sems = {k: nc.alloc_semaphore(f"s_{k.replace(':', '_')}")
                for k in all_keys}
        watermark = {e: {} for e in ENGS}

        def run_engine(e_obj, eng):
            wm = watermark[eng]
            for i in range(n):
                oe, build, _r, _w, marked, _sf, isd = ops[i]
                if oe != eng:
                    continue
                if wait_val[i]:
                    for k in sorted(wait_val[i]):
                        v = wait_val[i][k]
                        if v > wm.get(k, 0):
                            e_obj.wait_ge(sems[k], v)
                            wm[k] = v
                ins = build(e_obj)
                if marked:
                    ins.then_inc(sems[key_of(i)], 16 if isd else 1)
            if eng in self.final_waits:
                for k in all_keys:
                    if k != eng and finals[k] > watermark[eng].get(k, 0):
                        e_obj.wait_ge(sems[k], finals[k])
                        watermark[eng][k] = finals[k]

        for eng, starter in (("sync", block.sync), ("tensor", block.tensor),
                             ("vector", block.vector), ("scalar", block.scalar),
                             ("gpsimd", block.gpsimd)):
            starter(lambda e, eng=eng: run_engine(e, eng))


def build_program(taps=(), nch=NCH, tapc=0):
    nc = bass.Bass("TRN2", target_bir_lowering=False,
                   detect_race_conditions=False)
    A = Asm()
    tap_names = []

    def din(name, shape, dt=F32):
        return nc.dram_tensor(name, list(shape), dt, kind="ExternalInput").ap()

    x_t = din("x_t", (D, T))
    w_in = din("w_in", (D, HF))
    conv_w = din("conv_w", (128, NIC * 4))
    conv_b = din("conv_b", (128, NIC))
    xp_dt = din("xp_dt", (IN, R))
    xp_b = din("xp_b", (IN, NS))
    xp_c = din("xp_c", (IN, NS))
    dtp_w = din("dtp_w", (R, IN))
    dtp_b = din("dtp_b", (128, NIC))
    a_log = din("a_log", (128, NIC * NS))
    d_vec = din("d_vec", (128, NIC))
    op_w = din("op_w", (IN, D))
    wup_q = din("wup_q", (D, HF), I32)
    wup_s = din("wup_s", (2 * NDC, HF))
    b_up = din("b_up", (128, NHC))
    lau = din("lau", (D, 16))
    lbu = din("lbu", (16, HF))
    wdn_q = din("wdn_q", (HF, D), I32)
    wdn_s = din("wdn_s", (2 * NHC, D))
    b_dn = din("b_dn", (128, NDC))
    lad = din("lad", (HF, 16))
    lbd = din("lbd", (16, D))
    sel16 = din("sel16", (NS, NS * 128))
    sel2 = din("sel2", (2, 128))
    ones_c = din("ones_c", (128, 1))

    out_t = nc.dram_tensor("out_t", [D, T], F32, kind="ExternalOutput").ap()

    def scr(name, shape, dt=BF16):
        return nc.dram_tensor(name, list(shape), dt, kind="Internal").ap()

    s_win = [scr("s_win_h", (2 * NIC, 128, NDC * 128)),
             scr("s_win_l", (2 * NIC, 128, NDC * 128))]
    s_opw = [scr("s_opw_h", (NIC, 128, NDC * 128)),
             scr("s_opw_l", (NIC, 128, NDC * 128))]
    s_dtp = [scr("s_dtp_h", (NIC, 128, 128)), scr("s_dtp_l", (NIC, 128, 128))]
    s_xp = [scr("s_xp_h", (NIC, 128, R + 2 * NS)),
            scr("s_xp_l", (NIC, 128, R + 2 * NS))]
    s_wup = [scr("s_wup_h", (NHC, 128, NDC * 128)),
             scr("s_wup_l", (NHC, 128, NDC * 128))]
    s_wdn = [scr("s_wdn_h", (NDC, 128, NHC * 128)),
             scr("s_wdn_l", (NDC, 128, NHC * 128))]
    s_lau = [scr("s_lau_h", (128, NDC * 16)), scr("s_lau_l", (128, NDC * 16))]
    s_lbu = [scr("s_lbu_h", (16, HF)), scr("s_lbu_l", (16, HF))]
    s_lad = [scr("s_lad_h", (128, NHC * 16)), scr("s_lad_l", (128, NHC * 16))]
    s_lbd = [scr("s_lbd_h", (16, D)), scr("s_lbd_l", (16, D))]

    es = ExitStack()

    def sbt(name, shape, dt=F32):
        return es.enter_context(nc.sbuf_tensor(name, list(shape), dt))

    with es:
        u_buf = sbt("u_buf", (128, NIC * Q))
        bb_buf = sbt("bb_buf", (128, NS * Q))
        cb_buf = sbt("cb_buf", (128, NS * Q))
        scanh = sbt("scanh", (128, GN * Q))
        tmpP = sbt("tmpP", (128, GN * Q))
        tmpQ = sbt("tmpQ", (128, GN * Q))
        res_buf = sbt("res_buf", (128, NDC * Q))
        sc0p = [res_buf[:, 0:GN * Q], res_buf[:, GN * Q:2 * GN * Q]]
        sc1p = [res_buf[:, 2 * GN * Q:3 * GN * Q],
                res_buf[:, 3 * GN * Q:4 * GN * Q]]
        dtt = [tmpP[:, 0:Q], tmpP[:, Q:2 * Q]]
        dtut = [tmpP[:, 2 * Q:3 * Q], tmpP[:, 3 * Q:4 * Q]]
        get_ = [tmpQ[:, 0:Q], tmpQ[:, Q:2 * Q]]
        ggt = [tmpQ[:, 2 * Q:3 * Q], tmpQ[:, 3 * Q:4 * Q]]
        y4 = sbt("y4", (128, NG * Q))
        st1 = y4[0:1, :]
        mb_t = y4[:, 0:Q]
        rb_t = y4[:, Q:2 * Q]
        hidt = [sbt(f"hid{i}", (128, Q + 4)) for i in range(2)]
        tA = [sbt(f"tA{i}", (128, Q)) for i in range(2)]
        tB = [sbt(f"tB{i}", (128, Q)) for i in range(2)]
        tC = [sbt(f"tC{i}", (128, Q)) for i in range(2)]
        tD = sbt("tD", (128, Q))
        tE = sbt("tE", (128, Q))
        tF = sbt("tF", (128, Q))
        tG = sbt("tG", (128, Q))
        tH = sbt("tH", (128, Q))
        tt1 = [tD, tE]
        tt2 = [tF, tH]
        a_buf = sbt("a_buf", (128, NIC * NS))
        state = sbt("state", (128, NIC * NS))
        tails = sbt("tails", (128, NIC * 4))
        sel16_t = sbt("sel16_t", (NS, NS * 128))
        sel2_t = sbt("sel2_t", (2, 128))
        bc1_t = sel16_t[0:1, 0:128]
        ones_t = sbt("ones_t", (128, 1))
        convw_t = sbt("convw_t", (128, NIC * 4))
        convb_t = sbt("convb_t", (128, NIC))
        dtpb_t = sbt("dtpb_t", (128, NIC))
        dvec_t = sbt("dvec_t", (128, NIC))
        bup_t = sbt("bup_t", (128, NHC))
        bdn_t = sbt("bdn_t", (128, NDC))
        bt_t = sbt("bt_t", (NS, Q))
        ct_t = sbt("ct_t", (NS, Q))
        xst = sbt("xst", (128, Q))
        xh_buf = sbt("xh_buf", (128, NDC * Q), BF16)
        xl_buf = sbt("xl_buf", (128, NDC * Q), BF16)
        x1h_buf = sbt("x1h_buf", (128, NDC * Q), BF16)
        x1l_buf = sbt("x1l_buf", (128, NDC * Q), BF16)
        x2h = sbt("x2h", (128, 16 * Q), BF16)
        x2l = sbt("x2l", (128, 16 * Q), BF16)
        wa_h = sbt("wa_h", (128, 2 * NDC * 128), BF16)
        wa_l = sbt("wa_l", (128, 2 * NDC * 128), BF16)
        wb_h = sbt("wb_h", (128, 2 * 8 * 128), BF16)
        wb_l = sbt("wb_l", (128, 2 * 8 * 128), BF16)
        dtih = sbt("dtih", (128, Q), BF16)
        dtil = sbt("dtil", (128, Q), BF16)
        uht = [sbt(f"uh{i}", (128, Q), BF16) for i in range(2)]
        ult = [sbt(f"ul{i}", (128, Q), BF16) for i in range(2)]
        yph = uht
        ypl = ult
        t1h = dtih[0:16, :]
        t1l = dtil[0:16, :]
        t2h = dtih[0:16, :]
        t2l = dtil[0:16, :]
        laut = [sbt("laut_h", (128, NDC * 16), BF16),
                sbt("laut_l", (128, NDC * 16), BF16)]
        lbut = [sbt("lbut_h", (16, 2 * 128), BF16),
                sbt("lbut_l", (16, 2 * 128), BF16)]
        ladt = [sbt("ladt_h", (128, 2 * 16), BF16),
                sbt("ladt_l", (128, 2 * 16), BF16)]
        lbdt = lbut
        pp_f = [res_buf[:, 0:2048], res_buf[:, 2048:4096]]
        pp_h = [x2h[:, 0:2048], x2h[:, 2048:4096]]
        pp_l = [x2l[:, 0:2048], x2l[:, 2048:4096]]
        pp_s = [u_buf[:, 0:1024], u_buf[:, 1024:2048]]
        pp_q = [sbt("pp_q0", (128, 512), I32), sbt("pp_q1", (128, 512), I32)]
        sc_r = [bb_buf[0:2, 0:1024], cb_buf[0:2, 0:1024]]
        ppfr = ["res", "res1"]
        pphr = ["x2p0", "x2p1"]
        ppsr = ["u", "u1"]
        ppqr = ["ppq0", "ppq1"]
        scrr = ["bb", "cb"]

        ps = [es.enter_context(nc.psum_tensor(f"pb{i}", [128, 512], F32))
              for i in range(8)]

        block = es.enter_context(nc.Block())

        def E(eng, build, reads=(), writes=(), sfence=False, is_dma=False):
            A.emit(eng, build, reads, writes, sfence, is_dma)

        def dma(dst, src, reads=(), writes=(), q="sync"):
            E(q, lambda e, d=dst, s=src: e.dma_start(out=d, in_=s),
              reads, writes, is_dma=True)

        def tap(name, ap, shape, res):
            if name not in taps:
                return
            h = nc.dram_tensor("tap_" + name, list(shape), ap.dtype,
                               kind="ExternalOutput").ap()
            tap_names.append(name)
            dma(h[:], ap, reads=(res,), writes=("tapdma",))

        def mm3(psum_ap, lhs, rhs, first, last, res_r, res_w, skip=True):
            lh, ll = lhs
            rh, rl = rhs
            E("tensor", lambda e, a=psum_ap, x=lh, y=rh, st=first:
              e.matmul(a, x, y, start=st, stop=False, skip_group_check=skip),
              res_r, res_w if not last else ())
            E("tensor", lambda e, a=psum_ap, x=lh, y=rl:
              e.matmul(a, x, y, start=False, stop=False, skip_group_check=skip),
              res_r, ())
            E("tensor", lambda e, a=psum_ap, x=ll, y=rh, sp=last:
              e.matmul(a, x, y, start=False, stop=sp, skip_group_check=skip),
              res_r, res_w if last else ())

        def split_pair(src_ap, dst_h, dst_l, rd, wr):
            E("vector", lambda e, s=src_ap, d=dst_h: e.tensor_copy(out=d, in_=s),
              rd, wr)
            E("vector", lambda e, s=src_ap, h=dst_h, d=dst_l: e.tensor_tensor(
                out=d, in0=s, in1=h, op=AL.subtract), rd + wr, wr)

        for dst, src, tg in ((sel16_t, sel16, "sel16"), (sel2_t, sel2, "sel2"),
                             (ones_t, ones_c, "ones"),
                             (convw_t, conv_w, "convw"), (convb_t, conv_b, "convb"),
                             (dtpb_t, dtp_b, "dtpb"), (dvec_t, d_vec, "dvec"),
                             (bup_t, b_up, "bup"), (bdn_t, b_dn, "bdn")):
            dma(dst[:], src[:], writes=(tg,))

        pp_ctr = [0]

        def pp_next():
            p = pp_ctr[0] & 1
            pp_ctr[0] += 1
            return p, ("sync", "scalar")[p]

        def pp_split(p, cols, rows=128):
            split_pair(pp_f[p][0:rows, 0:cols], pp_h[p][0:rows, 0:cols],
                       pp_l[p][0:rows, 0:cols], (ppfr[p],), (pphr[p],))

        for dc in range(NDC):
            for piece in range(4):
                p, q = pp_next()
                dma(pp_f[p][:, 0:2048],
                    w_in[dc * 128:(dc + 1) * 128, piece * 2048:(piece + 1) * 2048],
                    writes=(ppfr[p],), q=q)
                pp_split(p, 2048)
                for jj in range(16):
                    jc = piece * 16 + jj
                    for h in range(2):
                        dma(s_win[h][jc][:, dc * 128:(dc + 1) * 128],
                            [pp_h, pp_l][h][p][:, jj * 128:(jj + 1) * 128],
                            reads=(pphr[p],), writes=("s_win",), q=q)
        for ic in range(NIC):
            p, q = pp_next()
            dma(pp_f[p][:, 0:2048], op_w[ic * 128:(ic + 1) * 128, :],
                writes=(ppfr[p],), q=q)
            pp_split(p, 2048)
            for h in range(2):
                dma(s_opw[h][ic], [pp_h, pp_l][h][p][:, 0:2048],
                    reads=(pphr[p],), writes=("s_opw",), q=q)
        for piece in range(2):
            p, q = pp_next()
            dma(pp_f[p][:, 0:2048], dtp_w[:, piece * 2048:(piece + 1) * 2048],
                writes=(ppfr[p],), q=q)
            pp_split(p, 2048)
            for jj in range(16):
                ic = piece * 16 + jj
                for h in range(2):
                    dma(s_dtp[h][ic],
                        [pp_h, pp_l][h][p][:, jj * 128:(jj + 1) * 128],
                        reads=(pphr[p],), writes=("s_dtp",), q=q)
        for ic in range(NIC):
            p, q = pp_next()
            dma(pp_f[p][:, 0:R], xp_dt[ic * 128:(ic + 1) * 128, :],
                writes=(ppfr[p],), q=q)
            dma(pp_f[p][:, R:R + NS], xp_b[ic * 128:(ic + 1) * 128, :],
                writes=(ppfr[p],), q=q)
            dma(pp_f[p][:, R + NS:R + 2 * NS], xp_c[ic * 128:(ic + 1) * 128, :],
                writes=(ppfr[p],), q=q)
            pp_split(p, R + 2 * NS)
            for h in range(2):
                dma(s_xp[h][ic], [pp_h, pp_l][h][p][:, 0:R + 2 * NS],
                    reads=(pphr[p],), writes=("s_xp",), q=q)
        p, q = pp_next()
        for dc in range(NDC):
            dma(pp_f[p][:, dc * 16:(dc + 1) * 16],
                lau[dc * 128:(dc + 1) * 128, :], writes=(ppfr[p],), q=q)
        pp_split(p, NDC * 16)
        for h in range(2):
            dma(s_lau[h][:, :], [pp_h, pp_l][h][p][:, 0:NDC * 16],
                reads=(pphr[p],), writes=("s_lau",), q=q)
        for half in range(2):
            p, q = pp_next()
            for j in range(NHC // 2):
                hc = half * (NHC // 2) + j
                dma(pp_f[p][:, j * 16:(j + 1) * 16],
                    lad[hc * 128:(hc + 1) * 128, :], writes=(ppfr[p],), q=q)
            pp_split(p, (NHC // 2) * 16)
            for h in range(2):
                dma(s_lad[h][:, half * (NHC // 2) * 16:
                             (half + 1) * (NHC // 2) * 16],
                    [pp_h, pp_l][h][p][:, 0:(NHC // 2) * 16],
                    reads=(pphr[p],), writes=("s_lad",), q=q)
        for piece in range(4):
            p, q = pp_next()
            dma(pp_f[p][0:16, 0:2048], lbu[:, piece * 2048:(piece + 1) * 2048],
                writes=(ppfr[p],), q=q)
            pp_split(p, 2048, rows=16)
            for h in range(2):
                dma(s_lbu[h][:, piece * 2048:(piece + 1) * 2048],
                    [pp_h, pp_l][h][p][0:16, 0:2048],
                    reads=(pphr[p],), writes=("s_lbu",), q=q)
        p, q = pp_next()
        dma(pp_f[p][0:16, 0:2048], lbd[:, :], writes=(ppfr[p],), q=q)
        pp_split(p, 2048, rows=16)
        for h in range(2):
            dma(s_lbd[h][:, :], [pp_h, pp_l][h][p][0:16, 0:2048],
                reads=(pphr[p],), writes=("s_lbd",), q=q)

        def dequant(qsrc, ssrc, dst, n_stripes, n_cols, tile_res):
            for si in range(n_stripes):
                for half in range(n_cols // 512):
                    c0 = half * 512
                    p, q = pp_next()
                    dma(sc_r[p][:, 0:512],
                        ssrc[2 * si:2 * si + 2, c0:c0 + 512],
                        writes=(scrr[p],), q=q)
                    dma(pp_q[p][:, 0:512],
                        qsrc[si * 128:(si + 1) * 128, c0:c0 + 512],
                        writes=(ppqr[p],), q=q)
                    E("tensor", lambda e, p=p: e.matmul(
                        ps[6 + p][:, 0:512], sel2_t[:], sc_r[p][:, 0:512],
                        start=True, stop=True, skip_group_check=True),
                      (scrr[p], "sel2", f"ps{6 + p}"), (f"ps{6 + p}",))
                    E("scalar", lambda e, p=p: e.copy(
                        out=pp_s[p][:, 0:512], in_=ps[6 + p][:, 0:512]),
                      (f"ps{6 + p}",), (ppsr[p], f"ps{6 + p}"))
                    E("vector", lambda e, p=p: e.scalar_tensor_tensor(
                        out=pp_f[p][:, 0:512], in0=pp_q[p][:, 0:512],
                        scalar=-7.5, in1=pp_s[p][:, 0:512],
                        op0=AL.add, op1=AL.mult),
                      (ppqr[p], ppsr[p]), (ppfr[p],))
                    pp_split(p, 512)
                    for jj in range(4):
                        j = half * 4 + jj
                        for h in range(2):
                            dma(dst[h][j][:, si * 128:(si + 1) * 128],
                                [pp_h, pp_l][h][p][:, jj * 128:(jj + 1) * 128],
                                reads=(pphr[p],), writes=(tile_res,), q=q)
        dequant(wup_q, wup_s, s_wup, NDC, HF, "s_wup")
        dequant(wdn_q, wdn_s, s_wdn, NHC, D, "s_wdn")

        p, q = pp_next()
        dma(pp_f[p][:, 0:NIC * NS], a_log[:], writes=(ppfr[p],), q=q)
        E("scalar", lambda e, p=p: e.activation(out=pp_s[p][:, 0:NIC * NS],
                                                in_=pp_f[p][:, 0:NIC * NS],
                                                func=AF.Exp),
          (ppfr[p],), (ppsr[p],))
        E("vector", lambda e, p=p: e.tensor_scalar_mul(
            out=a_buf[:], in0=pp_s[p][:, 0:NIC * NS], scalar1=-1.0),
          (ppsr[p],), ("abuf",))
        E("vector", lambda e: e.memset(state[:], 0.0), (), ("state",))
        E("vector", lambda e: e.memset(tails[:], 0.0), (), ("tails",))
        E("vector", lambda e: e.engine_nop(),
          ("res", "res1", "u", "u1", "x2p0", "x2p1", "bb", "cb"),
          ("res", "res1", "u", "u1", "x2p0", "x2p1", "bb", "cb"))

        def wa_slot(par):
            o = par * NDC * 128
            return ((wa_h[:, o:o + NDC * 128], wa_l[:, o:o + NDC * 128]),
                    f"wa{par}")

        def wb_slot(par, cols=8 * 128):
            o = par * 8 * 128
            return ((wb_h[:, o:o + cols], wb_l[:, o:o + cols]), f"wb{par}")

        for c in range(nch):
            tc0 = c * Q

            for dc in range(NDC):
                dma(xst[:, 0:Q],
                    x_t[dc * 128:(dc + 1) * 128, tc0:tc0 + Q],
                    writes=("xst",))
                split_pair(xst[:, 0:Q],
                           xh_buf[:, dc * Q:(dc + 1) * Q],
                           xl_buf[:, dc * Q:(dc + 1) * Q],
                           ("xst",), ("xpair",))

            for ic in range(NIC):
                par = ic % 2
                (wtiles, wres) = wa_slot(par)
                for h in range(2):
                    dma(wtiles[h], s_win[h][ic], reads=("s_win",),
                        writes=(wres,), q=("sync", "scalar")[par])
                pb = ps[par][:, 0:Q]
                for dc in range(NDC):
                    mm3(pb, (wtiles[0][:, dc * 128:(dc + 1) * 128],
                             wtiles[1][:, dc * 128:(dc + 1) * 128]),
                        (xh_buf[:, dc * Q:(dc + 1) * Q],
                         xl_buf[:, dc * Q:(dc + 1) * Q]),
                        first=(dc == 0), last=(dc == NDC - 1),
                        res_r=(wres, "xpair", f"ps{par}"), res_w=(f"ps{par}",))
                hb = hidt[par]
                E("scalar", lambda e, ic=ic, hb=hb: e.copy(
                    out=hb[:, 0:3], in_=tails[:, ic * 4:ic * 4 + 3]),
                  ("tails",), (f"hid{par}",))
                E("scalar", lambda e, pb=pb, hb=hb: e.copy(
                    out=hb[:, 3:3 + Q], in_=pb),
                  (f"ps{par}",), (f"hid{par}", f"ps{par}"))
                E("scalar", lambda e, ic=ic, hb=hb: e.copy(
                    out=tails[:, ic * 4:ic * 4 + 3], in_=hb[:, Q:Q + 3]),
                  (f"hid{par}",), ("tails",))
                E("vector", lambda e, ic=ic, hb=hb, par=par: e.tensor_scalar(
                    out=tB[par][:], in0=hb[:, 0:Q],
                    scalar1=convw_t[:, ic * 4:ic * 4 + 1],
                    scalar2=convb_t[:, ic:ic + 1], op0=AL.mult, op1=AL.add),
                  (f"hid{par}", "convw", "convb", f"tB{par}"), (f"tB{par}",))
                E("vector", lambda e, ic=ic, hb=hb, par=par:
                  e.scalar_tensor_tensor(
                      out=tC[par][:], in0=hb[:, 1:1 + Q],
                      scalar=convw_t[:, ic * 4 + 1:ic * 4 + 2], in1=tB[par][:],
                      op0=AL.mult, op1=AL.add),
                  (f"hid{par}", f"tB{par}", "convw", f"tC{par}"), (f"tC{par}",))
                E("vector", lambda e, ic=ic, hb=hb, par=par:
                  e.scalar_tensor_tensor(
                      out=tB[par][:], in0=hb[:, 2:2 + Q],
                      scalar=convw_t[:, ic * 4 + 2:ic * 4 + 3], in1=tC[par][:],
                      op0=AL.mult, op1=AL.add),
                  (f"hid{par}", f"tC{par}", "convw", f"tB{par}"), (f"tB{par}",))
                E("vector", lambda e, ic=ic, hb=hb, par=par:
                  e.scalar_tensor_tensor(
                      out=tC[par][:], in0=hb[:, 3:3 + Q],
                      scalar=convw_t[:, ic * 4 + 3:ic * 4 + 4], in1=tB[par][:],
                      op0=AL.mult, op1=AL.add),
                  (f"hid{par}", f"tB{par}", "convw", f"tC{par}"), (f"tC{par}",))
                E("scalar", lambda e, ic=ic, par=par: e.activation(
                    out=u_buf[:, ic * Q:(ic + 1) * Q], in_=tC[par][:],
                    func=AF.Silu), (f"tC{par}",), ("u",))
            if c == tapc:
                tap("u", u_buf[:], (128, NIC * Q), "u")

            for ic in range(NIC):
                par = ic % 2
                (wtiles, wres) = wb_slot(par, cols=R + 2 * NS)
                for h in range(2):
                    dma(wtiles[h], s_xp[h][ic], reads=("s_xp",),
                        writes=(wres,), q=("sync", "scalar")[par])
                split_pair(u_buf[:, ic * Q:(ic + 1) * Q],
                           uht[par][:], ult[par][:], ("u",), (f"upair{par}",))
                upair = (uht[par][:], ult[par][:])
                mm3(ps[2][:, 0:Q], (wtiles[0][:, 0:R], wtiles[1][:, 0:R]),
                    upair, first=(ic == 0), last=(ic == NIC - 1),
                    res_r=(wres, f"upair{par}", "ps2"), res_w=("ps2",))
                mm3(ps[3][0:NS, 0:Q],
                    (wtiles[0][:, R:R + NS], wtiles[1][:, R:R + NS]),
                    upair, first=(ic == 0), last=(ic == NIC - 1),
                    res_r=(wres, f"upair{par}", "ps3"), res_w=("ps3",))
                mm3(ps[4][0:NS, 0:Q],
                    (wtiles[0][:, R + NS:R + 2 * NS],
                     wtiles[1][:, R + NS:R + 2 * NS]),
                    upair, first=(ic == 0), last=(ic == NIC - 1),
                    res_r=(wres, f"upair{par}", "ps4"), res_w=("ps4",))
            E("scalar", lambda e: e.copy(out=tG[:], in_=ps[2][:, 0:Q]),
              ("ps2",), ("tG", "ps2"))
            E("scalar", lambda e: e.copy(out=bt_t[:], in_=ps[3][0:NS, 0:Q]),
              ("ps3",), ("btct", "ps3"))
            E("scalar", lambda e: e.copy(out=ct_t[:], in_=ps[4][0:NS, 0:Q]),
              ("ps4",), ("btct", "ps4"))
            if c == tapc:
                tap("dtin", tG[:], (128, Q), "tG")
                tap("bt", bt_t[:], (NS, Q), "btct")
                tap("ct", ct_t[:], (NS, Q), "btct")
            split_pair(tG[:], dtih[:], dtil[:], ("tG",), ("dtipair",))

            for which, src_t, dst_buf, dres in ((0, bt_t, bb_buf, "bb"),
                                                (1, ct_t, cb_buf, "cb")):
                for n in range(NS):
                    p2 = n % 2
                    g, j = n // GN, n % GN
                    E("tensor", lambda e, n=n, p2=p2, s=src_t: e.matmul(
                        ps[5 + p2][:, 0:Q], sel16_t[:, n * 128:(n + 1) * 128],
                        s[:], start=True, stop=True, skip_group_check=True),
                      ("btct", "sel16", f"ps{5 + p2}"), (f"ps{5 + p2}",))
                    E("scalar", lambda e, g=g, j=j, p2=p2, d=dst_buf: e.copy(
                        out=d[:, g * GN * Q:(g + 1) * GN * Q].rearrange(
                            "p (q n) -> p n q", n=GN)[:, j, :],
                        in_=ps[5 + p2][:, 0:Q]),
                      (f"ps{5 + p2}",), (dres, f"ps{5 + p2}"))
            if c == tapc:
                tap("bb", bb_buf[:], (128, NS * Q), "bb")

            if c == tapc:
                tap("state_in", state[:], (128, NIC * NS), "state")
            E("vector", lambda e: e.engine_nop(),
              ("res", "z0", "z1", "e0", "e1"),
              ("res", "z0", "z1", "e0", "e1"))
            for ic in range(NIC):
                par = ic % 2
                (wtiles, wres) = wb_slot(par, cols=128)
                for h in range(2):
                    dma(wtiles[h], s_dtp[h][ic], reads=("s_dtp",),
                        writes=(wres,), q=("sync", "scalar")[par])
                pdt = [ps[7], ps[4]][par]
                pdtr = ["ps7", "ps4"][par]
                mm3(pdt[:, 0:Q], (wtiles[0], wtiles[1]), (dtih[:], dtil[:]),
                    first=True, last=True,
                    res_r=(wres, "dtipair", pdtr), res_w=(pdtr,))
                E("scalar", lambda e, ic=ic, pdt=pdt, par=par: e.activation(
                    out=tt1[par][:], in_=pdt[:, 0:Q], func=AF.Exp,
                    bias=dtpb_t[:, ic:ic + 1]),
                  (pdtr, "dtpb", f"tt1{par}"), (f"tt1{par}", pdtr))
                E("scalar", lambda e, par=par: e.activation(
                    out=dtt[par], in_=tt1[par][:], func=AF.Ln, bias=1.0),
                  (f"tt1{par}", f"dtt{par}"), (f"dtt{par}",))
                if c == tapc and ic == 0:
                    tap("dt0", dtt[0], (128, Q), "dtt0")
                E("vector", lambda e, ic=ic, par=par: e.tensor_tensor(
                    out=dtut[par], in0=dtt[par],
                    in1=u_buf[:, ic * Q:(ic + 1) * Q],
                    op=AL.mult), (f"dtt{par}", "u", f"dtu{par}"), (f"dtu{par}",))
                (gtiles, gres) = wa_slot(par)
                for h in range(2):
                    dma(gtiles[h], s_win[h][NIC + ic], reads=("s_win",),
                        writes=(gres,), q=("sync", "scalar")[par])
                pbg = ps[5 + par][:, 0:Q]
                for dc in range(NDC):
                    mm3(pbg, (gtiles[0][:, dc * 128:(dc + 1) * 128],
                              gtiles[1][:, dc * 128:(dc + 1) * 128]),
                        (xh_buf[:, dc * Q:(dc + 1) * Q],
                         xl_buf[:, dc * Q:(dc + 1) * Q]),
                        first=(dc == 0), last=(dc == NDC - 1),
                        res_r=(gres, "xpair", f"ps{5 + par}"),
                        res_w=(f"ps{5 + par}",))
                E("scalar", lambda e, p=pbg, par=par: e.activation(
                    out=get_[par], in_=p, func=AF.Exp, scale=-1.0),
                  (f"ps{5 + par}", f"ge{par}"), (f"ge{par}",))
                E("scalar", lambda e, p=pbg, par=par: e.copy(
                    out=ggt[par], in_=p),
                  (f"ps{5 + par}", f"gg{par}"), (f"gg{par}", f"ps{5 + par}"))

                gq = GN * Q

                def emit_zarg(g, par=par, ic=ic):
                    pg = g % 2
                    dt_b = dtt[par].unsqueeze(2).broadcast_to([128, Q, GN])
                    a_sl = a_buf[:, ic * NS + g * GN: ic * NS + (g + 1) * GN]
                    a_b = a_sl.unsqueeze(1).broadcast_to([128, Q, GN])
                    E("vector", lambda e, db=dt_b, ab=a_b, pg=pg:
                      e.tensor_tensor(out=sc0p[pg].rearrange(
                          "p (q n) -> p q n", n=GN), in0=db, in1=ab,
                          op=AL.mult),
                      (f"dtt{par}", "abuf", f"z{pg}"), (f"z{pg}",))
                    E("scalar", lambda e, pg=pg: e.activation(
                        out=sc1p[pg], in_=sc0p[pg], func=AF.Exp),
                      (f"z{pg}", f"e{pg}"), (f"e{pg}",))

                emit_zarg(0)
                emit_zarg(1)
                for g in range(NG):
                    pg = g % 2
                    bb_g = bb_buf[:, g * gq:(g + 1) * gq]
                    cb_g = cb_buf[:, g * gq:(g + 1) * gq]
                    dtu_b = dtut[par].unsqueeze(2).broadcast_to([128, Q, GN])
                    E("gpsimd", lambda e, db=dtu_b, b=bb_g, pg=pg:
                      e.tensor_tensor(out=sc0p[pg].rearrange(
                          "p (q n) -> p q n", n=GN), in0=db,
                          in1=b.rearrange("p (q n) -> p q n", n=GN),
                          op=AL.mult),
                      (f"dtu{par}", "bb", f"z{pg}", f"e{pg}"), (f"z{pg}",))
                    for j in range(GN):
                        n = g * GN + j
                        da_ap = sc1p[pg].rearrange("p (q n) -> p n q",
                                                   n=GN)[:, j, :]
                        w_ap = sc0p[pg].rearrange("p (q n) -> p n q",
                                                  n=GN)[:, j, :]
                        h_ap = scanh[:, 0:gq].rearrange("p (q n) -> p n q",
                                                        n=GN)[:, j, :]
                        E("vector", lambda e, a=da_ap, w=w_ap, o=h_ap,
                          ini=state[:, ic * NS + n:ic * NS + n + 1]:
                          e.tensor_tensor_scan(o, a, w, ini, AL.mult, AL.add),
                          (f"z{pg}", f"e{pg}", "state", "sc2"), ("sc2",))
                    E("vector", lambda e, g=g, ic=ic, gq=gq: e.tensor_copy(
                        out=state[:, ic * NS + g * GN:ic * NS + (g + 1) * GN],
                        in_=scanh[:, gq - GN:gq]), ("sc2", "state"), ("state",),
                      sfence=True)
                    E("gpsimd", lambda e, cb=cb_g, gq=gq, pg=pg: e.tensor_tensor(
                        out=sc1p[pg], in0=scanh[:, 0:gq], in1=cb,
                        op=AL.mult), ("sc2", "cb", f"e{pg}"), (f"e{pg}",))
                    E("vector", lambda e, g=g, gq=gq, pg=pg: e.tensor_reduce(
                        out=y4[:, g * Q:(g + 1) * Q],
                        in_=sc1p[pg].rearrange("p (q n) -> p q n", n=GN),
                        axis=AX.X, op=AL.add), (f"e{pg}", "y4"), ("y4",))
                    if g + 2 < NG:
                        emit_zarg(g + 2)
                E("vector", lambda e, par=par: e.tensor_tensor(
                    out=tt2[par][:], in0=y4[:, 0:Q], in1=y4[:, Q:2 * Q],
                    op=AL.add), ("y4", f"tt2{par}"), (f"tt2{par}",))
                E("vector", lambda e, par=par: e.tensor_tensor(
                    out=tt1[par][:], in0=y4[:, 2 * Q:3 * Q],
                    in1=y4[:, 3 * Q:4 * Q],
                    op=AL.add), ("y4", f"tt1{par}"), (f"tt1{par}",),
                  sfence=True)
                E("vector", lambda e, par=par: e.tensor_scalar_add(
                    out=dtt[par], in0=get_[par], scalar1=1.0),
                  (f"ge{par}", f"dtt{par}"), (f"dtt{par}",))
                E("vector", lambda e, par=par: e.reciprocal(
                    out=get_[par], in_=dtt[par]),
                  (f"dtt{par}", f"ge{par}"), (f"ge{par}",))
                E("vector", lambda e, par=par: e.tensor_tensor(
                    out=dtt[par], in0=ggt[par], in1=get_[par], op=AL.mult),
                  (f"gg{par}", f"ge{par}", f"dtt{par}"), (f"dtt{par}",))
                E("vector", lambda e, par=par: e.tensor_tensor(
                    out=dtut[par], in0=tt2[par][:], in1=tt1[par][:],
                    op=AL.add),
                  (f"tt2{par}", f"tt1{par}", f"dtu{par}"), (f"dtu{par}",))
                E("vector", lambda e, ic=ic, par=par: e.scalar_tensor_tensor(
                    out=tt1[par][:], in0=u_buf[:, ic * Q:(ic + 1) * Q],
                    scalar=dvec_t[:, ic:ic + 1], in1=dtut[par],
                    op0=AL.mult, op1=AL.add),
                  ("u", f"dtu{par}", "dvec", f"tt1{par}"), (f"tt1{par}",))
                E("vector", lambda e, ic=ic, par=par: e.tensor_tensor(
                    out=u_buf[:, ic * Q:(ic + 1) * Q], in0=tt1[par][:],
                    in1=dtt[par], op=AL.mult),
                  (f"tt1{par}", f"dtt{par}", "u"), ("u",))
                if c == tapc and ic == 0:
                    tap("yfin", u_buf[:, 0:Q], (128, Q), "u")
            if c == tapc:
                tap("ymall", u_buf[:], (128, NIC * Q), "u")
            E("vector", lambda e: e.engine_nop(),
              ("res", "z0", "z1", "e0", "e1"),
              ("res", "z0", "z1", "e0", "e1"))

            for grp in range(2):
                for ic in range(NIC):
                    par = ic % 2
                    split_pair(u_buf[:, ic * Q:(ic + 1) * Q],
                               yph[par][:], ypl[par][:], ("u",), (f"upair{par}",))
                    (wtiles, wres) = wb_slot(par)
                    for h in range(2):
                        dma(wtiles[h],
                            s_opw[h][ic][:, grp * 8 * 128:(grp + 1) * 8 * 128],
                            reads=("s_opw",), writes=(wres,),
                            q=("sync", "scalar")[par])
                    for j in range(8):
                        pa = ps[j][:, 0:Q]
                        mm3(pa, (wtiles[0][:, j * 128:(j + 1) * 128],
                                 wtiles[1][:, j * 128:(j + 1) * 128]),
                            (yph[par][:], ypl[par][:]),
                            first=(ic == 0), last=(ic == NIC - 1),
                            res_r=(wres, f"upair{par}", f"ps{j}"),
                            res_w=(f"ps{j}",))
                for j in range(8):
                    dc = grp * 8 + j
                    pa = ps[j][:, 0:Q]
                    E("vector", lambda e, pa=pa, dc=dc: e.tensor_tensor(
                        out=tA[0][:], in0=pa, in1=xh_buf[:, dc * Q:(dc + 1) * Q],
                        op=AL.add),
                      (f"ps{j}", "xpair", "tA0"),
                      ("tA0", f"ps{j}"))
                    E("vector", lambda e, dc=dc: e.tensor_tensor(
                        out=res_buf[:, dc * Q:(dc + 1) * Q], in0=tA[0][:],
                        in1=xl_buf[:, dc * Q:(dc + 1) * Q], op=AL.add),
                      ("tA0", "xpair"), ("res",))
            if c == tapc:
                tap("attnres", res_buf[:], (128, NDC * Q), "res")

            def layer_norm(src_buf, src_res, out_cb, psA, psB, psC):
                for dc in range(NDC):
                    dp = dc % 2
                    E("scalar", lambda e, dc=dc, dp=dp: e.activation(
                        out=tA[dp][:], in_=src_buf[:, dc * Q:(dc + 1) * Q],
                        func=AF.Square), (src_res, f"tA{dp}"), (f"tA{dp}",))
                    E("tensor", lambda e, dc=dc: e.matmul(
                        ps[psA][0:1, 0:Q], ones_t[:],
                        src_buf[:, dc * Q:(dc + 1) * Q],
                        start=(dc == 0), stop=(dc == NDC - 1),
                        skip_group_check=True),
                      (src_res, "ones", f"ps{psA}"), (f"ps{psA}",))
                    E("tensor", lambda e, dc=dc, dp=dp: e.matmul(
                        ps[psB][0:1, 0:Q], ones_t[:], tA[dp][:],
                        start=(dc == 0), stop=(dc == NDC - 1),
                        skip_group_check=True),
                      (f"tA{dp}", "ones", f"ps{psB}"), (f"ps{psB}",))
                E("scalar", lambda e: e.activation(
                    out=st1[0:1, 0:Q], in_=ps[psA][0:1, 0:Q], func=AF.Copy,
                    scale=1.0 / D), (f"ps{psA}", "y4"), ("y4", f"ps{psA}"))
                E("scalar", lambda e: e.activation(
                    out=st1[0:1, Q:2 * Q], in_=st1[0:1, 0:Q], func=AF.Square),
                  ("y4",), ("y4",))
                E("scalar", lambda e: e.activation(
                    out=st1[0:1, 2 * Q:3 * Q], in_=ps[psB][0:1, 0:Q],
                    func=AF.Copy, scale=1.0 / D),
                  (f"ps{psB}", "y4"), ("y4", f"ps{psB}"))
                E("vector", lambda e: e.tensor_tensor(
                    out=st1[0:1, 3 * Q:4 * Q], in0=st1[0:1, 2 * Q:3 * Q],
                    in1=st1[0:1, Q:2 * Q], op=AL.subtract), ("y4",), ("y4",))
                E("vector", lambda e: e.tensor_scalar_add(
                    out=st1[0:1, Q:2 * Q], in0=st1[0:1, 3 * Q:4 * Q],
                    scalar1=EPS), ("y4",), ("y4",))
                E("scalar", lambda e: e.activation(
                    out=st1[0:1, 2 * Q:3 * Q], in_=st1[0:1, Q:2 * Q],
                    func=AF.Ln), ("y4",), ("y4",))
                E("scalar", lambda e: e.activation(
                    out=st1[0:1, 3 * Q:4 * Q], in_=st1[0:1, 2 * Q:3 * Q],
                    func=AF.Exp, scale=-0.5), ("y4",), ("y4",))
                E("tensor", lambda e: e.matmul(
                    ps[psC][:, 0:Q], bc1_t[:], st1[0:1, 0:Q],
                    start=True, stop=True, skip_group_check=True),
                  ("y4", "sel16", f"ps{psC}"), (f"ps{psC}",))
                E("tensor", lambda e: e.matmul(
                    ps[psC][:, Q:2 * Q], bc1_t[:], st1[0:1, 3 * Q:4 * Q],
                    start=True, stop=True, skip_group_check=True),
                  ("y4", "sel16", f"ps{psC}"), (f"ps{psC}",))
                E("scalar", lambda e: e.copy(
                    out=mb_t, in_=ps[psC][:, 0:Q]),
                  (f"ps{psC}", "y4"), ("y4",))
                E("scalar", lambda e: e.copy(
                    out=rb_t, in_=ps[psC][:, Q:2 * Q]),
                  (f"ps{psC}", "y4"), ("y4", f"ps{psC}"))
                for dc in range(NDC):
                    dp = dc % 2
                    E("vector", lambda e, dc=dc, dp=dp: e.tensor_tensor(
                        out=tA[dp][:], in0=src_buf[:, dc * Q:(dc + 1) * Q],
                        in1=mb_t, op=AL.subtract),
                      (src_res, "y4", f"tA{dp}"), (f"tA{dp}",))
                    E("vector", lambda e, dc=dc, dp=dp: e.tensor_tensor(
                        out=tB[dp][:], in0=tA[dp][:], in1=rb_t, op=AL.mult),
                      (f"tA{dp}", "y4", f"tB{dp}"), (f"tB{dp}",))
                    out_cb(dc)

            def ln1_out(dc):
                split_pair(tB[dc % 2][:],
                           x1h_buf[:, dc * Q:(dc + 1) * Q],
                           x1l_buf[:, dc * Q:(dc + 1) * Q],
                           (f"tB{dc % 2}",), ("x1pair",))
            layer_norm(res_buf, "res", ln1_out, 4, 5, 6)
            if c == tapc:
                tap("x1", x1h_buf[:], (128, NDC * Q), "x1pair")

            if c == 0:
                for h in range(2):
                    dma(laut[h][:], s_lau[h][:, :], reads=("s_lau",),
                        writes=("laut",))

            for dc in range(NDC):
                mm3(ps[2][0:16, 0:Q],
                    (laut[0][:, dc * 16:(dc + 1) * 16],
                     laut[1][:, dc * 16:(dc + 1) * 16]),
                    (x1h_buf[:, dc * Q:(dc + 1) * Q],
                     x1l_buf[:, dc * Q:(dc + 1) * Q]),
                    first=(dc == 0), last=(dc == NDC - 1),
                    res_r=("laut", "x1pair", "ps2"), res_w=("ps2",))
            E("scalar", lambda e: e.copy(out=tG[0:16, :], in_=ps[2][0:16, 0:Q]),
              ("ps2", "tG"), ("tG", "ps2"))
            split_pair(tG[0:16, :], t1h[:], t1l[:], ("tG",), ("dtipair",))

            for hbk in range(8):
                hpar = hbk % 2
                for jj in range(8):
                    hc = hbk * 8 + jj
                    par = jj % 2
                    (wtiles, wres) = wa_slot(par)
                    for h in range(2):
                        dma(wtiles[h], s_wup[h][hc], reads=("s_wup",),
                            writes=(wres,), q=("sync", "scalar")[par])
                    pb = ps[par][:, 0:Q]
                    for dc in range(NDC):
                        mm3(pb, (wtiles[0][:, dc * 128:(dc + 1) * 128],
                                 wtiles[1][:, dc * 128:(dc + 1) * 128]),
                            (x1h_buf[:, dc * Q:(dc + 1) * Q],
                             x1l_buf[:, dc * Q:(dc + 1) * Q]),
                            first=(dc == 0), last=False,
                            res_r=(wres, "x1pair", f"ps{par}"),
                            res_w=(f"ps{par}",))
                    for h in range(2):
                        dma(lbut[h][:, par * 128:(par + 1) * 128],
                            s_lbu[h][:, hc * 128:(hc + 1) * 128],
                            reads=("s_lbu",), writes=(f"lbu{par}",),
                            q=("sync", "scalar")[par])
                    mm3(pb, (lbut[0][:, par * 128:(par + 1) * 128],
                             lbut[1][:, par * 128:(par + 1) * 128]),
                        (t1h[:], t1l[:]), first=False, last=True,
                        res_r=(f"lbu{par}", "dtipair", f"ps{par}"),
                        res_w=(f"ps{par}",))
                    E("scalar", lambda e, pb=pb, hc=hc, par=par: e.activation(
                        out=tA[par][:], in_=pb, func=AF.Relu,
                        bias=bup_t[:, hc:hc + 1]),
                      (f"ps{par}", "bup", f"tA{par}"), (f"tA{par}", f"ps{par}"))
                    o = (hpar * 8 + jj) * Q
                    split_pair(tA[par][:], x2h[:, o:o + Q], x2l[:, o:o + Q],
                               (f"tA{par}",), (f"x2p{hpar}",))
                    for h in range(2):
                        dma(ladt[h][:, par * 16:(par + 1) * 16],
                            s_lad[h][:, hc * 16:(hc + 1) * 16],
                            reads=("s_lad",), writes=(f"lad{par}",),
                            q=("sync", "scalar")[par])
                    mm3(ps[3][0:16, 0:Q],
                        (ladt[0][:, par * 16:(par + 1) * 16],
                         ladt[1][:, par * 16:(par + 1) * 16]),
                        (x2h[:, o:o + Q], x2l[:, o:o + Q]),
                        first=(hc == 0), last=(hc == NHC - 1),
                        res_r=(f"lad{par}", f"x2p{hpar}", "ps3"), res_w=("ps3",))
                for dc in range(NDC):
                    par = dc % 2
                    (wtiles, wres) = wb_slot(par)
                    for h in range(2):
                        dma(wtiles[h],
                            s_wdn[h][dc][:, hbk * 8 * 128:(hbk + 1) * 8 * 128],
                            reads=("s_wdn",), writes=(wres,),
                            q=("sync", "scalar")[par])
                    pb = ps[4 + par][:, 0:Q]
                    for jj in range(8):
                        o = (hpar * 8 + jj) * Q
                        mm3(pb, (wtiles[0][:, jj * 128:(jj + 1) * 128],
                                 wtiles[1][:, jj * 128:(jj + 1) * 128]),
                            (x2h[:, o:o + Q], x2l[:, o:o + Q]),
                            first=(jj == 0), last=(jj == 7 and hbk != 7),
                            res_r=(wres, f"x2p{hpar}", f"ps{4 + par}"),
                            res_w=(f"ps{4 + par}",))
                    if hbk == 7:
                        if dc == 0:
                            E("scalar", lambda e: e.copy(
                                out=tG[0:16, :], in_=ps[3][0:16, 0:Q]),
                              ("ps3", "tG"), ("tG", "ps3"))
                            split_pair(tG[0:16, :], t2h[:], t2l[:], ("tG",),
                                       ("dtipair",))
                        for h in range(2):
                            dma(lbdt[h][:, par * 128:(par + 1) * 128],
                                s_lbd[h][:, dc * 128:(dc + 1) * 128],
                                reads=("s_lbd",), writes=(f"lbu{par}",),
                                q=("sync", "scalar")[par])
                        mm3(pb, (lbdt[0][:, par * 128:(par + 1) * 128],
                                 lbdt[1][:, par * 128:(par + 1) * 128]),
                            (t2h[:], t2l[:]), first=False, last=True,
                            res_r=(f"lbu{par}", "dtipair", f"ps{4 + par}"),
                            res_w=(f"ps{4 + par}",))
                    cur = hbk % 2
                    prv = 1 - cur
                    dst = u_buf[:, (cur * NDC + dc) * Q:(cur * NDC + dc + 1) * Q]
                    srcp = u_buf[:, (prv * NDC + dc) * Q:(prv * NDC + dc + 1) * Q]
                    if hbk == 0:
                        E("scalar", lambda e, pb=pb, d=dst: e.copy(
                            out=d, in_=pb),
                          (f"ps{4 + par}", "u"), ("u", f"ps{4 + par}"))
                    else:
                        E("vector", lambda e, pb=pb, d=dst, s=srcp:
                          e.tensor_tensor(out=d, in0=pb, in1=s, op=AL.add),
                          (f"ps{4 + par}", "u"), ("u", f"ps{4 + par}"))
            for dc in range(NDC):
                y2fin = u_buf[:, (NDC + dc) * Q:(NDC + dc + 1) * Q]
                E("vector", lambda e, s=y2fin, dc=dc: e.tensor_scalar_add(
                    out=tA[0][:], in0=s, scalar1=bdn_t[:, dc:dc + 1]),
                  ("u", "bdn", "tA0"), ("tA0",))
                E("vector", lambda e, dc=dc: e.tensor_tensor(
                    out=tB[1][:], in0=tA[0][:],
                    in1=x1h_buf[:, dc * Q:(dc + 1) * Q], op=AL.add),
                  ("tA0", "x1pair", "tB1"), ("tB1",))
                E("vector", lambda e, dc=dc: e.tensor_tensor(
                    out=res_buf[:, dc * Q:(dc + 1) * Q], in0=tB[1][:],
                    in1=x1l_buf[:, dc * Q:(dc + 1) * Q], op=AL.add),
                  ("tB1", "x1pair"), ("res",))
            if c == tapc:
                tap("ffnres", res_buf[:], (128, NDC * Q), "res")

            def ln2_out(dc, c=c):
                E("sync", lambda e, dc=dc, c=c: e.dma_start(
                    out=out_t[dc * 128:(dc + 1) * 128, c * Q:(c + 1) * Q],
                    in_=tB[dc % 2][:]), (f"tB{dc % 2}",), ("outdma",))
            layer_norm(res_buf, "res", ln2_out, 4, 5, 6)

        A.final_waits = ["sync", "gpsimd"]
        A.assemble(nc, block)

    return nc, tap_names


def prep_inputs(inputs):
    f32 = np.float32
    g = {k: np.asarray(v) for k, v in inputs.items()}
    x = g["L0_attn_input_states"]
    base = {}
    base["w_in"] = np.ascontiguousarray(g["L0_attn_in_proj_w"], f32)
    cw = np.asarray(g["L0_attn_conv_w"], f32).reshape(IN, 4)
    base["conv_w"] = np.ascontiguousarray(
        cw.reshape(NIC, 128, 4).transpose(1, 0, 2).reshape(128, NIC * 4))
    base["conv_b"] = np.ascontiguousarray(
        np.asarray(g["L0_attn_conv_b"], f32).reshape(NIC, 128).T)
    xp = np.asarray(g["L0_attn_x_proj_w"], f32)
    base["xp_dt"] = np.ascontiguousarray(xp[:, 0:R])
    base["xp_b"] = np.ascontiguousarray(xp[:, R:R + NS])
    base["xp_c"] = np.ascontiguousarray(xp[:, R + NS:R + 2 * NS])
    base["dtp_w"] = np.ascontiguousarray(g["L0_attn_dt_proj_w"], f32)
    base["dtp_b"] = np.ascontiguousarray(
        np.asarray(g["L0_attn_dt_proj_b"], f32).reshape(NIC, 128).T)
    base["a_log"] = np.ascontiguousarray(
        np.asarray(g["L0_attn_A_log"], f32).reshape(NIC, 128, NS)
        .transpose(1, 0, 2).reshape(128, NIC * NS))
    base["d_vec"] = np.ascontiguousarray(
        np.asarray(g["L0_attn_D"], f32).reshape(NIC, 128).T)
    base["op_w"] = np.ascontiguousarray(g["L0_attn_out_proj_w"], f32)
    base["wup_q"] = np.ascontiguousarray(np.asarray(g["L0_ffn_w_up_q"]).T)
    base["wup_s"] = np.ascontiguousarray(np.asarray(g["L0_ffn_w_up_scale"], f32).T)
    base["b_up"] = np.ascontiguousarray(
        np.asarray(g["L0_ffn_b_up"], f32).reshape(NHC, 128).T)
    base["lau"] = np.ascontiguousarray(g["L0_ffn_w_up_lora_a"], f32)
    base["lbu"] = np.ascontiguousarray(g["L0_ffn_w_up_lora_b"], f32)
    base["wdn_q"] = np.ascontiguousarray(np.asarray(g["L0_ffn_w_down_q"]).T)
    base["wdn_s"] = np.ascontiguousarray(
        np.asarray(g["L0_ffn_w_down_scale"], f32).T)
    base["b_dn"] = np.ascontiguousarray(
        np.asarray(g["L0_ffn_b_down"], f32).reshape(NDC, 128).T)
    base["lad"] = np.ascontiguousarray(g["L0_ffn_w_down_lora_a"], f32)
    base["lbd"] = np.ascontiguousarray(g["L0_ffn_w_down_lora_b"], f32)
    sel16v = np.zeros((NS, NS * 128), f32)
    for n in range(NS):
        sel16v[n, n * 128:(n + 1) * 128] = 1.0
    base["sel16"] = sel16v
    sel2v = np.zeros((2, 128), f32)
    sel2v[0, 0:64] = 1.0
    sel2v[1, 64:128] = 1.0
    base["sel2"] = sel2v
    base["ones_c"] = np.ones((128, 1), f32)

    in_maps = []
    for cid in range(NCORES):
        m = dict(base)
        m["x_t"] = np.ascontiguousarray(x[cid].T.astype(f32))
        in_maps.append(m)
    return in_maps


def kernel(**inputs):
    in_maps = prep_inputs(inputs)
    nc, _ = build_program()
    res = run_bass_kernel_spmd(nc, in_maps, core_ids=list(range(NCORES)))
    out = np.stack([np.asarray(r["out_t"]).T for r in res.results])
    return np.ascontiguousarray(out).astype(np.float32)


# version 05
# speedup vs baseline: 1.0166x; 1.0166x over previous
import numpy as np
from contextlib import ExitStack
import concourse.bass as bass
import concourse.mybir as mybir
from concourse.bass_utils import run_bass_kernel_spmd

F32 = mybir.dt.float32
BF16 = mybir.dt.bfloat16
I32 = mybir.dt.int32
AL = mybir.AluOpType
AF = mybir.ActivationFunctionType
AX = mybir.AxisListType

T, D, IN, NS, R, HF = 2048, 2048, 4096, 16, 128, 8192
Q = 256
NCH = T // Q
NDC = D // 128
NIC = IN // 128
NHC = HF // 128
NG = 4
GN = NS // NG
EPS = 1e-5
NCORES = 8

ENGS = ("sync", "tensor", "vector", "scalar", "gpsimd")
DMA_ENGS = {"sync"}


class Asm:

    def __init__(self):
        self.ops = []
        self.final_waits = []

    def emit(self, eng, build, reads=(), writes=(), sfence=False, is_dma=False):
        self.ops.append([eng, build, tuple(reads), tuple(writes), False, sfence,
                         is_dma or eng in DMA_ENGS])

    def assemble(self, nc, block):
        ops = self.ops
        n = len(ops)
        last_w = {}
        readers = {}
        deps = [None] * n
        for i, (eng, _b, rd, wr, _f, sfence, _d) in enumerate(ops):
            dd = set()
            for r in rd:
                j = last_w.get(r)
                if j is not None and (ops[j][0] != eng or sfence):
                    dd.add(j)
            for r in wr:
                j = last_w.get(r)
                if j is not None and (ops[j][0] != eng or sfence):
                    dd.add(j)
                for j in readers.get(r, ()):
                    if ops[j][0] != eng or sfence:
                        dd.add(j)
            for r in rd:
                readers.setdefault(r, []).append(i)
            for r in wr:
                last_w[r] = i
                readers[r] = []
            deps[i] = dd
        for i in range(n):
            for j in deps[i]:
                ops[j][4] = True
        def key_of(i):
            eng, _b, _r, wr, _m, _sf, isd = ops[i]
            if isd:
                assert wr, f"dma op {i} needs a write resource"
                return "d:" + wr[0]
            return eng
        for op in ops:
            if op[6]:
                op[4] = True
        last_of = {}
        for i, op in enumerate(ops):
            last_of[op[0]] = i
        for e in ENGS:
            if e in last_of:
                ops[last_of[e]][4] = True
        kcount = {}
        inc_val = [0] * n
        for i in range(n):
            k = key_of(i)
            if ops[i][4]:
                kcount[k] = kcount.get(k, 0) + (16 if ops[i][6] else 1)
            inc_val[i] = kcount.get(k, 0)
        need_keys = [None] * n
        for i in range(n):
            ks = set()
            for j in deps[i]:
                ks.add(key_of(j))
            need_keys[i] = ks
        run_cnt = {}
        wait_val = [None] * n
        for i in range(n):
            if need_keys[i]:
                wv = {}
                for j in deps[i]:
                    k = key_of(j)
                    if k.startswith("d:"):
                        wv[k] = run_cnt.get(k, 0)
                    else:
                        wv[k] = max(wv.get(k, 0), inc_val[j])
                wait_val[i] = wv
            k = key_of(i)
            if ops[i][4]:
                run_cnt[k] = run_cnt.get(k, 0) + (16 if ops[i][6] else 1)
        finals = dict(run_cnt)
        all_keys = sorted(finals)
        sems = {k: nc.alloc_semaphore(f"s_{k.replace(':', '_')}")
                for k in all_keys}
        watermark = {e: {} for e in ENGS}

        def run_engine(e_obj, eng):
            wm = watermark[eng]
            for i in range(n):
                oe, build, _r, _w, marked, _sf, isd = ops[i]
                if oe != eng:
                    continue
                if wait_val[i]:
                    for k in sorted(wait_val[i]):
                        v = wait_val[i][k]
                        if v > wm.get(k, 0):
                            e_obj.wait_ge(sems[k], v)
                            wm[k] = v
                ins = build(e_obj)
                if marked:
                    ins.then_inc(sems[key_of(i)], 16 if isd else 1)
            if eng in self.final_waits:
                for k in all_keys:
                    if k != eng and finals[k] > watermark[eng].get(k, 0):
                        e_obj.wait_ge(sems[k], finals[k])
                        watermark[eng][k] = finals[k]

        for eng, starter in (("sync", block.sync), ("tensor", block.tensor),
                             ("vector", block.vector), ("scalar", block.scalar),
                             ("gpsimd", block.gpsimd)):
            starter(lambda e, eng=eng: run_engine(e, eng))


def build_program(taps=(), nch=NCH, tapc=0):
    nc = bass.Bass("TRN2", target_bir_lowering=False,
                   detect_race_conditions=False)
    A = Asm()
    tap_names = []

    def din(name, shape, dt=F32):
        return nc.dram_tensor(name, list(shape), dt, kind="ExternalInput").ap()

    x_t = din("x_t", (D, T))
    w_in = din("w_in", (D, HF))
    conv_w = din("conv_w", (128, NIC * 4))
    conv_b = din("conv_b", (128, NIC))
    xp_dt = din("xp_dt", (IN, R))
    xp_b = din("xp_b", (IN, NS))
    xp_c = din("xp_c", (IN, NS))
    dtp_w = din("dtp_w", (R, IN))
    dtp_b = din("dtp_b", (128, NIC))
    a_log = din("a_log", (128, NIC * NS))
    d_vec = din("d_vec", (128, NIC))
    op_w = din("op_w", (IN, D))
    wup_q = din("wup_q", (D, HF), I32)
    wup_s = din("wup_s", (2 * NDC, HF))
    b_up = din("b_up", (128, NHC))
    lau = din("lau", (D, 16))
    lbu = din("lbu", (16, HF))
    wdn_q = din("wdn_q", (HF, D), I32)
    wdn_s = din("wdn_s", (2 * NHC, D))
    b_dn = din("b_dn", (128, NDC))
    lad = din("lad", (HF, 16))
    lbd = din("lbd", (16, D))
    sel16 = din("sel16", (NS, NS * 128))
    sel2 = din("sel2", (2, 128))
    ones_c = din("ones_c", (128, 1))

    out_t = nc.dram_tensor("out_t", [D, T], F32, kind="ExternalOutput").ap()

    def scr(name, shape, dt=BF16):
        return nc.dram_tensor(name, list(shape), dt, kind="Internal").ap()

    s_win = [scr("s_win_h", (2 * NIC, 128, NDC * 128)),
             scr("s_win_l", (2 * NIC, 128, NDC * 128))]
    s_opw = [scr("s_opw_h", (NIC, 128, NDC * 128)),
             scr("s_opw_l", (NIC, 128, NDC * 128))]
    s_dtp = [scr("s_dtp_h", (NIC, 128, 128)), scr("s_dtp_l", (NIC, 128, 128))]
    s_xp = [scr("s_xp_h", (NIC, 128, R + 2 * NS)),
            scr("s_xp_l", (NIC, 128, R + 2 * NS))]
    s_wup = [scr("s_wup_h", (NHC, 128, NDC * 128)),
             scr("s_wup_l", (NHC, 128, NDC * 128))]
    s_wdn = [scr("s_wdn_h", (NDC, 128, NHC * 128)),
             scr("s_wdn_l", (NDC, 128, NHC * 128))]
    s_lau = [scr("s_lau_h", (128, NDC * 16)), scr("s_lau_l", (128, NDC * 16))]
    s_lbu = [scr("s_lbu_h", (16, HF)), scr("s_lbu_l", (16, HF))]
    s_lad = [scr("s_lad_h", (128, NHC * 16)), scr("s_lad_l", (128, NHC * 16))]
    s_lbd = [scr("s_lbd_h", (16, D)), scr("s_lbd_l", (16, D))]

    es = ExitStack()

    def sbt(name, shape, dt=F32):
        return es.enter_context(nc.sbuf_tensor(name, list(shape), dt))

    with es:
        u_buf = sbt("u_buf", (128, NIC * Q))
        bb_buf = sbt("bb_buf", (128, NS * Q))
        cb_buf = sbt("cb_buf", (128, NS * Q))
        scanh = sbt("scanh", (128, GN * Q))
        tmpP = sbt("tmpP", (128, GN * Q))
        tmpQ = sbt("tmpQ", (128, GN * Q))
        res_buf = sbt("res_buf", (128, NDC * Q))
        sc0p = [res_buf[:, 0:GN * Q], res_buf[:, GN * Q:2 * GN * Q]]
        sc1p = [res_buf[:, 2 * GN * Q:3 * GN * Q],
                res_buf[:, 3 * GN * Q:4 * GN * Q]]
        dtt = [tmpP[:, 0:Q], tmpP[:, Q:2 * Q]]
        dtut = [tmpP[:, 2 * Q:3 * Q], tmpP[:, 3 * Q:4 * Q]]
        get_ = [tmpQ[:, 0:Q], tmpQ[:, Q:2 * Q]]
        ggt = [tmpQ[:, 2 * Q:3 * Q], tmpQ[:, 3 * Q:4 * Q]]
        y4 = sbt("y4", (128, NG * Q))
        st1 = y4[0:1, :]
        mb_t = y4[:, 0:Q]
        rb_t = y4[:, Q:2 * Q]
        hidt = [sbt(f"hid{i}", (128, Q + 4)) for i in range(2)]
        tA = [sbt(f"tA{i}", (128, Q)) for i in range(2)]
        tB = [sbt(f"tB{i}", (128, Q)) for i in range(2)]
        tC = [sbt(f"tC{i}", (128, Q)) for i in range(2)]
        tD = sbt("tD", (128, Q))
        tE = sbt("tE", (128, Q))
        tF = sbt("tF", (128, Q))
        tG = sbt("tG", (128, Q))
        tH = sbt("tH", (128, Q))
        tt1 = [tD, tE]
        tt2 = [tF, tH]
        a_buf = sbt("a_buf", (128, NIC * NS))
        state = sbt("state", (128, NIC * NS))
        tails = sbt("tails", (128, NIC * 4))
        sel16_t = sbt("sel16_t", (NS, NS * 128))
        sel2_t = sbt("sel2_t", (2, 128))
        bc1_t = sel16_t[0:1, 0:128]
        ones_t = sbt("ones_t", (128, 1))
        convw_t = sbt("convw_t", (128, NIC * 4))
        convb_t = sbt("convb_t", (128, NIC))
        dtpb_t = sbt("dtpb_t", (128, NIC))
        dvec_t = sbt("dvec_t", (128, NIC))
        bup_t = sbt("bup_t", (128, NHC))
        bdn_t = sbt("bdn_t", (128, NDC))
        bt_t = sbt("bt_t", (NS, Q))
        ct_t = sbt("ct_t", (NS, Q))
        xst = sbt("xst", (128, Q))
        xh_buf = sbt("xh_buf", (128, NDC * Q), BF16)
        xl_buf = sbt("xl_buf", (128, NDC * Q), BF16)
        x1h_buf = sbt("x1h_buf", (128, NDC * Q), BF16)
        x1l_buf = sbt("x1l_buf", (128, NDC * Q), BF16)
        x2h = sbt("x2h", (128, 16 * Q), BF16)
        x2l = sbt("x2l", (128, 16 * Q), BF16)
        wa_h = sbt("wa_h", (128, 2 * NDC * 128), BF16)
        wa_l = sbt("wa_l", (128, 2 * NDC * 128), BF16)
        wb_h = sbt("wb_h", (128, 2 * 8 * 128), BF16)
        wb_l = sbt("wb_l", (128, 2 * 8 * 128), BF16)
        dtih = sbt("dtih", (128, Q), BF16)
        dtil = sbt("dtil", (128, Q), BF16)
        uht = [sbt(f"uh{i}", (128, Q), BF16) for i in range(2)]
        ult = [sbt(f"ul{i}", (128, Q), BF16) for i in range(2)]
        yph = uht
        ypl = ult
        t1h = dtih[0:16, :]
        t1l = dtil[0:16, :]
        t2h = dtih[0:16, :]
        t2l = dtil[0:16, :]
        laut = [sbt("laut_h", (128, NDC * 16), BF16),
                sbt("laut_l", (128, NDC * 16), BF16)]
        lbut = [sbt("lbut_h", (16, 2 * 128), BF16),
                sbt("lbut_l", (16, 2 * 128), BF16)]
        ladt = [sbt("ladt_h", (128, 2 * 16), BF16),
                sbt("ladt_l", (128, 2 * 16), BF16)]
        lbdt = lbut
        pp_f = [res_buf[:, 0:2048], res_buf[:, 2048:4096]]
        pp_h = [x2h[:, 0:2048], x2h[:, 2048:4096]]
        pp_l = [x2l[:, 0:2048], x2l[:, 2048:4096]]
        pp_s = [u_buf[:, 0:1024], u_buf[:, 1024:2048]]
        pp_q = [sbt("pp_q0", (128, 512), I32), sbt("pp_q1", (128, 512), I32)]
        sc_r = [bb_buf[0:2, 0:1024], cb_buf[0:2, 0:1024]]
        ppfr = ["res", "res1"]
        pphr = ["x2p0", "x2p1"]
        ppsr = ["u", "u1"]
        ppqr = ["ppq0", "ppq1"]
        scrr = ["bb", "cb"]

        ps = [es.enter_context(nc.psum_tensor(f"pb{i}", [128, 512], F32))
              for i in range(8)]

        block = es.enter_context(nc.Block())

        def E(eng, build, reads=(), writes=(), sfence=False, is_dma=False):
            A.emit(eng, build, reads, writes, sfence, is_dma)

        def dma(dst, src, reads=(), writes=(), q="sync"):
            E(q, lambda e, d=dst, s=src: e.dma_start(out=d, in_=s),
              reads, writes, is_dma=True)

        def tap(name, ap, shape, res):
            if name not in taps:
                return
            h = nc.dram_tensor("tap_" + name, list(shape), ap.dtype,
                               kind="ExternalOutput").ap()
            tap_names.append(name)
            dma(h[:], ap, reads=(res,), writes=("tapdma",))

        def mm3(psum_ap, lhs, rhs, first, last, res_r, res_w, skip=True):
            lh, ll = lhs
            rh, rl = rhs
            E("tensor", lambda e, a=psum_ap, x=lh, y=rh, st=first:
              e.matmul(a, x, y, start=st, stop=False, skip_group_check=skip),
              res_r, res_w if not last else ())
            E("tensor", lambda e, a=psum_ap, x=lh, y=rl:
              e.matmul(a, x, y, start=False, stop=False, skip_group_check=skip),
              res_r, ())
            E("tensor", lambda e, a=psum_ap, x=ll, y=rh, sp=last:
              e.matmul(a, x, y, start=False, stop=sp, skip_group_check=skip),
              res_r, res_w if last else ())

        def split_pair(src_ap, dst_h, dst_l, rd, wr):
            E("vector", lambda e, s=src_ap, d=dst_h: e.tensor_copy(out=d, in_=s),
              rd, wr)
            E("vector", lambda e, s=src_ap, h=dst_h, d=dst_l: e.tensor_tensor(
                out=d, in0=s, in1=h, op=AL.subtract), rd + wr, wr)

        for dst, src, tg in ((sel16_t, sel16, "sel16"), (sel2_t, sel2, "sel2"),
                             (ones_t, ones_c, "ones"),
                             (convw_t, conv_w, "convw"), (convb_t, conv_b, "convb"),
                             (dtpb_t, dtp_b, "dtpb"), (dvec_t, d_vec, "dvec"),
                             (bup_t, b_up, "bup"), (bdn_t, b_dn, "bdn")):
            dma(dst[:], src[:], writes=(tg,))

        pp_ctr = [0]

        def pp_next():
            p = pp_ctr[0] & 1
            pp_ctr[0] += 1
            return p, ("sync", "scalar")[p]

        def pp_split(p, cols, rows=128):
            split_pair(pp_f[p][0:rows, 0:cols], pp_h[p][0:rows, 0:cols],
                       pp_l[p][0:rows, 0:cols], (ppfr[p],), (pphr[p],))

        for dc in range(NDC):
            for piece in range(4):
                p, q = pp_next()
                dma(pp_f[p][:, 0:2048],
                    w_in[dc * 128:(dc + 1) * 128, piece * 2048:(piece + 1) * 2048],
                    writes=(ppfr[p],), q=q)
                pp_split(p, 2048)
                for jj in range(16):
                    jc = piece * 16 + jj
                    for h in range(2):
                        dma(s_win[h][jc][:, dc * 128:(dc + 1) * 128],
                            [pp_h, pp_l][h][p][:, jj * 128:(jj + 1) * 128],
                            reads=(pphr[p],), writes=("s_win",), q=q)
        for ic in range(NIC):
            p, q = pp_next()
            dma(pp_f[p][:, 0:2048], op_w[ic * 128:(ic + 1) * 128, :],
                writes=(ppfr[p],), q=q)
            pp_split(p, 2048)
            for h in range(2):
                dma(s_opw[h][ic], [pp_h, pp_l][h][p][:, 0:2048],
                    reads=(pphr[p],), writes=("s_opw",), q=q)
        for piece in range(2):
            p, q = pp_next()
            dma(pp_f[p][:, 0:2048], dtp_w[:, piece * 2048:(piece + 1) * 2048],
                writes=(ppfr[p],), q=q)
            pp_split(p, 2048)
            for jj in range(16):
                ic = piece * 16 + jj
                for h in range(2):
                    dma(s_dtp[h][ic],
                        [pp_h, pp_l][h][p][:, jj * 128:(jj + 1) * 128],
                        reads=(pphr[p],), writes=("s_dtp",), q=q)
        for ic in range(NIC):
            p, q = pp_next()
            dma(pp_f[p][:, 0:R], xp_dt[ic * 128:(ic + 1) * 128, :],
                writes=(ppfr[p],), q=q)
            dma(pp_f[p][:, R:R + NS], xp_b[ic * 128:(ic + 1) * 128, :],
                writes=(ppfr[p],), q=q)
            dma(pp_f[p][:, R + NS:R + 2 * NS], xp_c[ic * 128:(ic + 1) * 128, :],
                writes=(ppfr[p],), q=q)
            pp_split(p, R + 2 * NS)
            for h in range(2):
                dma(s_xp[h][ic], [pp_h, pp_l][h][p][:, 0:R + 2 * NS],
                    reads=(pphr[p],), writes=("s_xp",), q=q)
        p, q = pp_next()
        for dc in range(NDC):
            dma(pp_f[p][:, dc * 16:(dc + 1) * 16],
                lau[dc * 128:(dc + 1) * 128, :], writes=(ppfr[p],), q=q)
        pp_split(p, NDC * 16)
        for h in range(2):
            dma(s_lau[h][:, :], [pp_h, pp_l][h][p][:, 0:NDC * 16],
                reads=(pphr[p],), writes=("s_lau",), q=q)
        for half in range(2):
            p, q = pp_next()
            for j in range(NHC // 2):
                hc = half * (NHC // 2) + j
                dma(pp_f[p][:, j * 16:(j + 1) * 16],
                    lad[hc * 128:(hc + 1) * 128, :], writes=(ppfr[p],), q=q)
            pp_split(p, (NHC // 2) * 16)
            for h in range(2):
                dma(s_lad[h][:, half * (NHC // 2) * 16:
                             (half + 1) * (NHC // 2) * 16],
                    [pp_h, pp_l][h][p][:, 0:(NHC // 2) * 16],
                    reads=(pphr[p],), writes=("s_lad",), q=q)
        for piece in range(4):
            p, q = pp_next()
            dma(pp_f[p][0:16, 0:2048], lbu[:, piece * 2048:(piece + 1) * 2048],
                writes=(ppfr[p],), q=q)
            pp_split(p, 2048, rows=16)
            for h in range(2):
                dma(s_lbu[h][:, piece * 2048:(piece + 1) * 2048],
                    [pp_h, pp_l][h][p][0:16, 0:2048],
                    reads=(pphr[p],), writes=("s_lbu",), q=q)
        p, q = pp_next()
        dma(pp_f[p][0:16, 0:2048], lbd[:, :], writes=(ppfr[p],), q=q)
        pp_split(p, 2048, rows=16)
        for h in range(2):
            dma(s_lbd[h][:, :], [pp_h, pp_l][h][p][0:16, 0:2048],
                reads=(pphr[p],), writes=("s_lbd",), q=q)

        def dequant(qsrc, ssrc, dst, n_stripes, n_cols, tile_res):
            for si in range(n_stripes):
                for half in range(n_cols // 512):
                    c0 = half * 512
                    p, q = pp_next()
                    dma(sc_r[p][:, 0:512],
                        ssrc[2 * si:2 * si + 2, c0:c0 + 512],
                        writes=(scrr[p],), q=q)
                    dma(pp_q[p][:, 0:512],
                        qsrc[si * 128:(si + 1) * 128, c0:c0 + 512],
                        writes=(ppqr[p],), q=q)
                    E("tensor", lambda e, p=p: e.matmul(
                        ps[6 + p][:, 0:512], sel2_t[:], sc_r[p][:, 0:512],
                        start=True, stop=True, skip_group_check=True),
                      (scrr[p], "sel2", f"ps{6 + p}"), (f"ps{6 + p}",))
                    E("scalar", lambda e, p=p: e.copy(
                        out=pp_s[p][:, 0:512], in_=ps[6 + p][:, 0:512]),
                      (f"ps{6 + p}",), (ppsr[p], f"ps{6 + p}"))
                    E("vector", lambda e, p=p: e.scalar_tensor_tensor(
                        out=pp_f[p][:, 0:512], in0=pp_q[p][:, 0:512],
                        scalar=-7.5, in1=pp_s[p][:, 0:512],
                        op0=AL.add, op1=AL.mult),
                      (ppqr[p], ppsr[p]), (ppfr[p],))
                    pp_split(p, 512)
                    for jj in range(4):
                        j = half * 4 + jj
                        for h in range(2):
                            dma(dst[h][j][:, si * 128:(si + 1) * 128],
                                [pp_h, pp_l][h][p][:, jj * 128:(jj + 1) * 128],
                                reads=(pphr[p],), writes=(tile_res,), q=q)
        dequant(wup_q, wup_s, s_wup, NDC, HF, "s_wup")
        dequant(wdn_q, wdn_s, s_wdn, NHC, D, "s_wdn")

        p, q = pp_next()
        dma(pp_f[p][:, 0:NIC * NS], a_log[:], writes=(ppfr[p],), q=q)
        E("scalar", lambda e, p=p: e.activation(out=pp_s[p][:, 0:NIC * NS],
                                                in_=pp_f[p][:, 0:NIC * NS],
                                                func=AF.Exp),
          (ppfr[p],), (ppsr[p],))
        E("vector", lambda e, p=p: e.tensor_scalar_mul(
            out=a_buf[:], in0=pp_s[p][:, 0:NIC * NS], scalar1=-1.0),
          (ppsr[p],), ("abuf",))
        E("vector", lambda e: e.memset(state[:], 0.0), (), ("state",))
        E("vector", lambda e: e.memset(tails[:], 0.0), (), ("tails",))
        E("vector", lambda e: e.engine_nop(),
          ("res", "res1", "u", "u1", "x2p0", "x2p1", "bb", "cb"),
          ("res", "res1", "u", "u1", "x2p0", "x2p1", "bb", "cb"))

        def wa_slot(par):
            o = par * NDC * 128
            return ((wa_h[:, o:o + NDC * 128], wa_l[:, o:o + NDC * 128]),
                    f"wa{par}")

        def wb_slot(par, cols=8 * 128):
            o = par * 8 * 128
            return ((wb_h[:, o:o + cols], wb_l[:, o:o + cols]), f"wb{par}")

        for c in range(nch):
            tc0 = c * Q

            for dc in range(NDC):
                dma(xst[:, 0:Q],
                    x_t[dc * 128:(dc + 1) * 128, tc0:tc0 + Q],
                    writes=("xst",))
                split_pair(xst[:, 0:Q],
                           xh_buf[:, dc * Q:(dc + 1) * Q],
                           xl_buf[:, dc * Q:(dc + 1) * Q],
                           ("xst",), ("xpair",))

            for ic in range(NIC):
                par = ic % 2
                (wtiles, wres) = wa_slot(par)
                for h in range(2):
                    dma(wtiles[h], s_win[h][ic], reads=("s_win",),
                        writes=(wres,), q=("sync", "scalar")[par])
                pb = ps[par][:, 0:Q]
                for dc in range(NDC):
                    mm3(pb, (wtiles[0][:, dc * 128:(dc + 1) * 128],
                             wtiles[1][:, dc * 128:(dc + 1) * 128]),
                        (xh_buf[:, dc * Q:(dc + 1) * Q],
                         xl_buf[:, dc * Q:(dc + 1) * Q]),
                        first=(dc == 0), last=(dc == NDC - 1),
                        res_r=(wres, "xpair", f"ps{par}"), res_w=(f"ps{par}",))
                hb = hidt[par]
                E("scalar", lambda e, ic=ic, hb=hb: e.copy(
                    out=hb[:, 0:3], in_=tails[:, ic * 4:ic * 4 + 3]),
                  ("tails",), (f"hid{par}",))
                E("scalar", lambda e, pb=pb, hb=hb: e.copy(
                    out=hb[:, 3:3 + Q], in_=pb),
                  (f"ps{par}",), (f"hid{par}", f"ps{par}"))
                E("scalar", lambda e, ic=ic, hb=hb: e.copy(
                    out=tails[:, ic * 4:ic * 4 + 3], in_=hb[:, Q:Q + 3]),
                  (f"hid{par}",), ("tails",))
                E("vector", lambda e, ic=ic, hb=hb, par=par: e.tensor_scalar(
                    out=tB[par][:], in0=hb[:, 0:Q],
                    scalar1=convw_t[:, ic * 4:ic * 4 + 1],
                    scalar2=convb_t[:, ic:ic + 1], op0=AL.mult, op1=AL.add),
                  (f"hid{par}", "convw", "convb", f"tB{par}"), (f"tB{par}",))
                E("vector", lambda e, ic=ic, hb=hb, par=par:
                  e.scalar_tensor_tensor(
                      out=tC[par][:], in0=hb[:, 1:1 + Q],
                      scalar=convw_t[:, ic * 4 + 1:ic * 4 + 2], in1=tB[par][:],
                      op0=AL.mult, op1=AL.add),
                  (f"hid{par}", f"tB{par}", "convw", f"tC{par}"), (f"tC{par}",))
                E("vector", lambda e, ic=ic, hb=hb, par=par:
                  e.scalar_tensor_tensor(
                      out=tB[par][:], in0=hb[:, 2:2 + Q],
                      scalar=convw_t[:, ic * 4 + 2:ic * 4 + 3], in1=tC[par][:],
                      op0=AL.mult, op1=AL.add),
                  (f"hid{par}", f"tC{par}", "convw", f"tB{par}"), (f"tB{par}",))
                E("vector", lambda e, ic=ic, hb=hb, par=par:
                  e.scalar_tensor_tensor(
                      out=tC[par][:], in0=hb[:, 3:3 + Q],
                      scalar=convw_t[:, ic * 4 + 3:ic * 4 + 4], in1=tB[par][:],
                      op0=AL.mult, op1=AL.add),
                  (f"hid{par}", f"tB{par}", "convw", f"tC{par}"), (f"tC{par}",))
                E("scalar", lambda e, ic=ic, par=par: e.activation(
                    out=u_buf[:, ic * Q:(ic + 1) * Q], in_=tC[par][:],
                    func=AF.Silu), (f"tC{par}",), ("u",))
            if c == tapc:
                tap("u", u_buf[:], (128, NIC * Q), "u")

            for ic in range(NIC):
                par = ic % 2
                (wtiles, wres) = wb_slot(par, cols=R + 2 * NS)
                for h in range(2):
                    dma(wtiles[h], s_xp[h][ic], reads=("s_xp",),
                        writes=(wres,), q=("sync", "scalar")[par])
                split_pair(u_buf[:, ic * Q:(ic + 1) * Q],
                           uht[par][:], ult[par][:], ("u",), (f"upair{par}",))
                upair = (uht[par][:], ult[par][:])
                mm3(ps[2][:, 0:Q], (wtiles[0][:, 0:R], wtiles[1][:, 0:R]),
                    upair, first=(ic == 0), last=(ic == NIC - 1),
                    res_r=(wres, f"upair{par}", "ps2"), res_w=("ps2",))
                mm3(ps[3][0:NS, 0:Q],
                    (wtiles[0][:, R:R + NS], wtiles[1][:, R:R + NS]),
                    upair, first=(ic == 0), last=(ic == NIC - 1),
                    res_r=(wres, f"upair{par}", "ps3"), res_w=("ps3",))
                mm3(ps[4][0:NS, 0:Q],
                    (wtiles[0][:, R + NS:R + 2 * NS],
                     wtiles[1][:, R + NS:R + 2 * NS]),
                    upair, first=(ic == 0), last=(ic == NIC - 1),
                    res_r=(wres, f"upair{par}", "ps4"), res_w=("ps4",))
            E("scalar", lambda e: e.copy(out=tG[:], in_=ps[2][:, 0:Q]),
              ("ps2",), ("tG", "ps2"))
            E("scalar", lambda e: e.copy(out=bt_t[:], in_=ps[3][0:NS, 0:Q]),
              ("ps3",), ("btct", "ps3"))
            E("scalar", lambda e: e.copy(out=ct_t[:], in_=ps[4][0:NS, 0:Q]),
              ("ps4",), ("btct", "ps4"))
            if c == tapc:
                tap("dtin", tG[:], (128, Q), "tG")
                tap("bt", bt_t[:], (NS, Q), "btct")
                tap("ct", ct_t[:], (NS, Q), "btct")
            split_pair(tG[:], dtih[:], dtil[:], ("tG",), ("dtipair",))

            for which, src_t, dst_buf, dres in ((0, bt_t, bb_buf, "bb"),
                                                (1, ct_t, cb_buf, "cb")):
                for n in range(NS):
                    p2 = n % 2
                    g, j = n // GN, n % GN
                    E("tensor", lambda e, n=n, p2=p2, s=src_t: e.matmul(
                        ps[5 + p2][:, 0:Q], sel16_t[:, n * 128:(n + 1) * 128],
                        s[:], start=True, stop=True, skip_group_check=True),
                      ("btct", "sel16", f"ps{5 + p2}"), (f"ps{5 + p2}",))
                    E("scalar", lambda e, g=g, j=j, p2=p2, d=dst_buf: e.copy(
                        out=d[:, g * GN * Q:(g + 1) * GN * Q].rearrange(
                            "p (q n) -> p n q", n=GN)[:, j, :],
                        in_=ps[5 + p2][:, 0:Q]),
                      (f"ps{5 + p2}",), (dres, f"ps{5 + p2}"))
            if c == tapc:
                tap("bb", bb_buf[:], (128, NS * Q), "bb")

            if c == tapc:
                tap("state_in", state[:], (128, NIC * NS), "state")
            E("vector", lambda e: e.engine_nop(),
              ("res", "z0", "z1", "e0", "e1"),
              ("res", "z0", "z1", "e0", "e1"))
            for ic in range(NIC):
                par = ic % 2
                (wtiles, wres) = wb_slot(par, cols=128)
                for h in range(2):
                    dma(wtiles[h], s_dtp[h][ic], reads=("s_dtp",),
                        writes=(wres,), q=("sync", "scalar")[par])
                pdt = [ps[7], ps[4]][par]
                pdtr = ["ps7", "ps4"][par]
                mm3(pdt[:, 0:Q], (wtiles[0], wtiles[1]), (dtih[:], dtil[:]),
                    first=True, last=True,
                    res_r=(wres, "dtipair", pdtr), res_w=(pdtr,))
                E("scalar", lambda e, ic=ic, pdt=pdt, par=par: e.activation(
                    out=tt1[par][:], in_=pdt[:, 0:Q], func=AF.Exp,
                    bias=dtpb_t[:, ic:ic + 1]),
                  (pdtr, "dtpb", f"tt1{par}"), (f"tt1{par}", pdtr))
                E("scalar", lambda e, par=par: e.activation(
                    out=dtt[par], in_=tt1[par][:], func=AF.Ln, bias=1.0),
                  (f"tt1{par}", f"dtt{par}"), (f"dtt{par}",))
                if c == tapc and ic == 0:
                    tap("dt0", dtt[0], (128, Q), "dtt0")
                E("vector", lambda e, ic=ic, par=par: e.tensor_tensor(
                    out=dtut[par], in0=dtt[par],
                    in1=u_buf[:, ic * Q:(ic + 1) * Q],
                    op=AL.mult), (f"dtt{par}", "u", f"dtu{par}"), (f"dtu{par}",))
                (gtiles, gres) = wa_slot(par)
                for h in range(2):
                    dma(gtiles[h], s_win[h][NIC + ic], reads=("s_win",),
                        writes=(gres,), q=("sync", "scalar")[par])
                pbg = ps[5 + par][:, 0:Q]
                for dc in range(NDC):
                    mm3(pbg, (gtiles[0][:, dc * 128:(dc + 1) * 128],
                              gtiles[1][:, dc * 128:(dc + 1) * 128]),
                        (xh_buf[:, dc * Q:(dc + 1) * Q],
                         xl_buf[:, dc * Q:(dc + 1) * Q]),
                        first=(dc == 0), last=(dc == NDC - 1),
                        res_r=(gres, "xpair", f"ps{5 + par}"),
                        res_w=(f"ps{5 + par}",))
                E("scalar", lambda e, p=pbg, par=par: e.activation(
                    out=get_[par], in_=p, func=AF.Exp, scale=-1.0),
                  (f"ps{5 + par}", f"ge{par}"), (f"ge{par}",))
                E("scalar", lambda e, p=pbg, par=par: e.copy(
                    out=ggt[par], in_=p),
                  (f"ps{5 + par}", f"gg{par}"), (f"gg{par}", f"ps{5 + par}"))

                gq = GN * Q

                def emit_zarg(g, par=par, ic=ic):
                    pg = g % 2
                    dt_b = dtt[par].unsqueeze(2).broadcast_to([128, Q, GN])
                    a_sl = a_buf[:, ic * NS + g * GN: ic * NS + (g + 1) * GN]
                    a_b = a_sl.unsqueeze(1).broadcast_to([128, Q, GN])
                    E("vector", lambda e, db=dt_b, ab=a_b, pg=pg:
                      e.tensor_tensor(out=sc0p[pg].rearrange(
                          "p (q n) -> p q n", n=GN), in0=db, in1=ab,
                          op=AL.mult),
                      (f"dtt{par}", "abuf", f"z{pg}"), (f"z{pg}",))
                    E("scalar", lambda e, pg=pg: e.activation(
                        out=sc1p[pg], in_=sc0p[pg], func=AF.Exp),
                      (f"z{pg}", f"e{pg}"), (f"e{pg}",))

                emit_zarg(0)
                emit_zarg(1)
                for g in range(NG):
                    pg = g % 2
                    bb_g = bb_buf[:, g * gq:(g + 1) * gq]
                    cb_g = cb_buf[:, g * gq:(g + 1) * gq]
                    dtu_b = dtut[par].unsqueeze(2).broadcast_to([128, Q, GN])
                    E("vector", lambda e, db=dtu_b, b=bb_g, pg=pg:
                      e.tensor_tensor(out=sc0p[pg].rearrange(
                          "p (q n) -> p q n", n=GN), in0=db,
                          in1=b.rearrange("p (q n) -> p q n", n=GN),
                          op=AL.mult),
                      (f"dtu{par}", "bb", f"z{pg}", f"e{pg}"), (f"z{pg}",))
                    for j in range(GN):
                        n = g * GN + j
                        da_ap = sc1p[pg].rearrange("p (q n) -> p n q",
                                                   n=GN)[:, j, :]
                        w_ap = sc0p[pg].rearrange("p (q n) -> p n q",
                                                  n=GN)[:, j, :]
                        h_ap = scanh[:, 0:gq].rearrange("p (q n) -> p n q",
                                                        n=GN)[:, j, :]
                        E("vector", lambda e, a=da_ap, w=w_ap, o=h_ap,
                          ini=state[:, ic * NS + n:ic * NS + n + 1]:
                          e.tensor_tensor_scan(o, a, w, ini, AL.mult, AL.add),
                          (f"z{pg}", f"e{pg}", "state", "sc2"), ("sc2",))
                    E("vector", lambda e, g=g, ic=ic, gq=gq: e.tensor_copy(
                        out=state[:, ic * NS + g * GN:ic * NS + (g + 1) * GN],
                        in_=scanh[:, gq - GN:gq]), ("sc2", "state"), ("state",),
                      sfence=True)
                    E("vector", lambda e, cb=cb_g, gq=gq, pg=pg: e.tensor_tensor(
                        out=sc1p[pg], in0=scanh[:, 0:gq], in1=cb,
                        op=AL.mult), ("sc2", "cb", f"e{pg}"), (f"e{pg}",))
                    E("vector", lambda e, g=g, gq=gq, pg=pg: e.tensor_reduce(
                        out=y4[:, g * Q:(g + 1) * Q],
                        in_=sc1p[pg].rearrange("p (q n) -> p q n", n=GN),
                        axis=AX.X, op=AL.add), (f"e{pg}", "y4"), ("y4",))
                    if g + 2 < NG:
                        emit_zarg(g + 2)
                E("vector", lambda e, par=par: e.tensor_tensor(
                    out=tt2[par][:], in0=y4[:, 0:Q], in1=y4[:, Q:2 * Q],
                    op=AL.add), ("y4", f"tt2{par}"), (f"tt2{par}",))
                E("vector", lambda e, par=par: e.tensor_tensor(
                    out=tt1[par][:], in0=y4[:, 2 * Q:3 * Q],
                    in1=y4[:, 3 * Q:4 * Q],
                    op=AL.add), ("y4", f"tt1{par}"), (f"tt1{par}",),
                  sfence=True)
                E("vector", lambda e, par=par: e.tensor_scalar_add(
                    out=dtt[par], in0=get_[par], scalar1=1.0),
                  (f"ge{par}", f"dtt{par}"), (f"dtt{par}",))
                E("vector", lambda e, par=par: e.reciprocal(
                    out=get_[par], in_=dtt[par]),
                  (f"dtt{par}", f"ge{par}"), (f"ge{par}",))
                E("vector", lambda e, par=par: e.tensor_tensor(
                    out=dtt[par], in0=ggt[par], in1=get_[par], op=AL.mult),
                  (f"gg{par}", f"ge{par}", f"dtt{par}"), (f"dtt{par}",))
                E("vector", lambda e, par=par: e.tensor_tensor(
                    out=dtut[par], in0=tt2[par][:], in1=tt1[par][:],
                    op=AL.add),
                  (f"tt2{par}", f"tt1{par}", f"dtu{par}"), (f"dtu{par}",))
                E("vector", lambda e, ic=ic, par=par: e.scalar_tensor_tensor(
                    out=tt1[par][:], in0=u_buf[:, ic * Q:(ic + 1) * Q],
                    scalar=dvec_t[:, ic:ic + 1], in1=dtut[par],
                    op0=AL.mult, op1=AL.add),
                  ("u", f"dtu{par}", "dvec", f"tt1{par}"), (f"tt1{par}",))
                E("vector", lambda e, ic=ic, par=par: e.tensor_tensor(
                    out=u_buf[:, ic * Q:(ic + 1) * Q], in0=tt1[par][:],
                    in1=dtt[par], op=AL.mult),
                  (f"tt1{par}", f"dtt{par}", "u"), ("u",))
                if c == tapc and ic == 0:
                    tap("yfin", u_buf[:, 0:Q], (128, Q), "u")
            if c == tapc:
                tap("ymall", u_buf[:], (128, NIC * Q), "u")
            E("vector", lambda e: e.engine_nop(),
              ("res", "z0", "z1", "e0", "e1"),
              ("res", "z0", "z1", "e0", "e1"))

            for grp in range(2):
                for ic in range(NIC):
                    par = ic % 2
                    split_pair(u_buf[:, ic * Q:(ic + 1) * Q],
                               yph[par][:], ypl[par][:], ("u",), (f"upair{par}",))
                    (wtiles, wres) = wb_slot(par)
                    for h in range(2):
                        dma(wtiles[h],
                            s_opw[h][ic][:, grp * 8 * 128:(grp + 1) * 8 * 128],
                            reads=("s_opw",), writes=(wres,),
                            q=("sync", "scalar")[par])
                    for j in range(8):
                        pa = ps[j][:, 0:Q]
                        mm3(pa, (wtiles[0][:, j * 128:(j + 1) * 128],
                                 wtiles[1][:, j * 128:(j + 1) * 128]),
                            (yph[par][:], ypl[par][:]),
                            first=(ic == 0), last=(ic == NIC - 1),
                            res_r=(wres, f"upair{par}", f"ps{j}"),
                            res_w=(f"ps{j}",))
                for j in range(8):
                    dc = grp * 8 + j
                    pa = ps[j][:, 0:Q]
                    E("vector", lambda e, pa=pa, dc=dc: e.tensor_tensor(
                        out=tA[0][:], in0=pa, in1=xh_buf[:, dc * Q:(dc + 1) * Q],
                        op=AL.add),
                      (f"ps{j}", "xpair", "tA0"),
                      ("tA0", f"ps{j}"))
                    E("vector", lambda e, dc=dc: e.tensor_tensor(
                        out=res_buf[:, dc * Q:(dc + 1) * Q], in0=tA[0][:],
                        in1=xl_buf[:, dc * Q:(dc + 1) * Q], op=AL.add),
                      ("tA0", "xpair"), ("res",))
            if c == tapc:
                tap("attnres", res_buf[:], (128, NDC * Q), "res")

            def layer_norm(src_buf, src_res, out_cb, psA, psB, psC):
                for dc in range(NDC):
                    dp = dc % 2
                    E("scalar", lambda e, dc=dc, dp=dp: e.activation(
                        out=tA[dp][:], in_=src_buf[:, dc * Q:(dc + 1) * Q],
                        func=AF.Square), (src_res, f"tA{dp}"), (f"tA{dp}",))
                    E("tensor", lambda e, dc=dc: e.matmul(
                        ps[psA][0:1, 0:Q], ones_t[:],
                        src_buf[:, dc * Q:(dc + 1) * Q],
                        start=(dc == 0), stop=(dc == NDC - 1),
                        skip_group_check=True),
                      (src_res, "ones", f"ps{psA}"), (f"ps{psA}",))
                    E("tensor", lambda e, dc=dc, dp=dp: e.matmul(
                        ps[psB][0:1, 0:Q], ones_t[:], tA[dp][:],
                        start=(dc == 0), stop=(dc == NDC - 1),
                        skip_group_check=True),
                      (f"tA{dp}", "ones", f"ps{psB}"), (f"ps{psB}",))
                E("scalar", lambda e: e.activation(
                    out=st1[0:1, 0:Q], in_=ps[psA][0:1, 0:Q], func=AF.Copy,
                    scale=1.0 / D), (f"ps{psA}", "y4"), ("y4", f"ps{psA}"))
                E("scalar", lambda e: e.activation(
                    out=st1[0:1, Q:2 * Q], in_=st1[0:1, 0:Q], func=AF.Square),
                  ("y4",), ("y4",))
                E("scalar", lambda e: e.activation(
                    out=st1[0:1, 2 * Q:3 * Q], in_=ps[psB][0:1, 0:Q],
                    func=AF.Copy, scale=1.0 / D),
                  (f"ps{psB}", "y4"), ("y4", f"ps{psB}"))
                E("vector", lambda e: e.tensor_tensor(
                    out=st1[0:1, 3 * Q:4 * Q], in0=st1[0:1, 2 * Q:3 * Q],
                    in1=st1[0:1, Q:2 * Q], op=AL.subtract), ("y4",), ("y4",))
                E("vector", lambda e: e.tensor_scalar_add(
                    out=st1[0:1, Q:2 * Q], in0=st1[0:1, 3 * Q:4 * Q],
                    scalar1=EPS), ("y4",), ("y4",))
                E("scalar", lambda e: e.activation(
                    out=st1[0:1, 2 * Q:3 * Q], in_=st1[0:1, Q:2 * Q],
                    func=AF.Ln), ("y4",), ("y4",))
                E("scalar", lambda e: e.activation(
                    out=st1[0:1, 3 * Q:4 * Q], in_=st1[0:1, 2 * Q:3 * Q],
                    func=AF.Exp, scale=-0.5), ("y4",), ("y4",))
                E("tensor", lambda e: e.matmul(
                    ps[psC][:, 0:Q], bc1_t[:], st1[0:1, 0:Q],
                    start=True, stop=True, skip_group_check=True),
                  ("y4", "sel16", f"ps{psC}"), (f"ps{psC}",))
                E("tensor", lambda e: e.matmul(
                    ps[psC][:, Q:2 * Q], bc1_t[:], st1[0:1, 3 * Q:4 * Q],
                    start=True, stop=True, skip_group_check=True),
                  ("y4", "sel16", f"ps{psC}"), (f"ps{psC}",))
                E("scalar", lambda e: e.copy(
                    out=mb_t, in_=ps[psC][:, 0:Q]),
                  (f"ps{psC}", "y4"), ("y4",))
                E("scalar", lambda e: e.copy(
                    out=rb_t, in_=ps[psC][:, Q:2 * Q]),
                  (f"ps{psC}", "y4"), ("y4", f"ps{psC}"))
                for dc in range(NDC):
                    dp = dc % 2
                    E("vector", lambda e, dc=dc, dp=dp: e.tensor_tensor(
                        out=tA[dp][:], in0=src_buf[:, dc * Q:(dc + 1) * Q],
                        in1=mb_t, op=AL.subtract),
                      (src_res, "y4", f"tA{dp}"), (f"tA{dp}",))
                    E("vector", lambda e, dc=dc, dp=dp: e.tensor_tensor(
                        out=tB[dp][:], in0=tA[dp][:], in1=rb_t, op=AL.mult),
                      (f"tA{dp}", "y4", f"tB{dp}"), (f"tB{dp}",))
                    out_cb(dc)

            def ln1_out(dc):
                split_pair(tB[dc % 2][:],
                           x1h_buf[:, dc * Q:(dc + 1) * Q],
                           x1l_buf[:, dc * Q:(dc + 1) * Q],
                           (f"tB{dc % 2}",), ("x1pair",))
            layer_norm(res_buf, "res", ln1_out, 4, 5, 6)
            if c == tapc:
                tap("x1", x1h_buf[:], (128, NDC * Q), "x1pair")

            if c == 0:
                for h in range(2):
                    dma(laut[h][:], s_lau[h][:, :], reads=("s_lau",),
                        writes=("laut",))

            for dc in range(NDC):
                mm3(ps[2][0:16, 0:Q],
                    (laut[0][:, dc * 16:(dc + 1) * 16],
                     laut[1][:, dc * 16:(dc + 1) * 16]),
                    (x1h_buf[:, dc * Q:(dc + 1) * Q],
                     x1l_buf[:, dc * Q:(dc + 1) * Q]),
                    first=(dc == 0), last=(dc == NDC - 1),
                    res_r=("laut", "x1pair", "ps2"), res_w=("ps2",))
            E("scalar", lambda e: e.copy(out=tG[0:16, :], in_=ps[2][0:16, 0:Q]),
              ("ps2", "tG"), ("tG", "ps2"))
            split_pair(tG[0:16, :], t1h[:], t1l[:], ("tG",), ("dtipair",))

            for hbk in range(8):
                hpar = hbk % 2
                for jj in range(8):
                    hc = hbk * 8 + jj
                    par = jj % 2
                    (wtiles, wres) = wa_slot(par)
                    for h in range(2):
                        dma(wtiles[h], s_wup[h][hc], reads=("s_wup",),
                            writes=(wres,), q=("sync", "scalar")[par])
                    pb = ps[par][:, 0:Q]
                    for dc in range(NDC):
                        mm3(pb, (wtiles[0][:, dc * 128:(dc + 1) * 128],
                                 wtiles[1][:, dc * 128:(dc + 1) * 128]),
                            (x1h_buf[:, dc * Q:(dc + 1) * Q],
                             x1l_buf[:, dc * Q:(dc + 1) * Q]),
                            first=(dc == 0), last=False,
                            res_r=(wres, "x1pair", f"ps{par}"),
                            res_w=(f"ps{par}",))
                    for h in range(2):
                        dma(lbut[h][:, par * 128:(par + 1) * 128],
                            s_lbu[h][:, hc * 128:(hc + 1) * 128],
                            reads=("s_lbu",), writes=(f"lbu{par}",),
                            q=("sync", "scalar")[par])
                    mm3(pb, (lbut[0][:, par * 128:(par + 1) * 128],
                             lbut[1][:, par * 128:(par + 1) * 128]),
                        (t1h[:], t1l[:]), first=False, last=True,
                        res_r=(f"lbu{par}", "dtipair", f"ps{par}"),
                        res_w=(f"ps{par}",))
                    E("scalar", lambda e, pb=pb, hc=hc, par=par: e.activation(
                        out=tA[par][:], in_=pb, func=AF.Relu,
                        bias=bup_t[:, hc:hc + 1]),
                      (f"ps{par}", "bup", f"tA{par}"), (f"tA{par}", f"ps{par}"))
                    o = (hpar * 8 + jj) * Q
                    split_pair(tA[par][:], x2h[:, o:o + Q], x2l[:, o:o + Q],
                               (f"tA{par}",), (f"x2p{hpar}",))
                    for h in range(2):
                        dma(ladt[h][:, par * 16:(par + 1) * 16],
                            s_lad[h][:, hc * 16:(hc + 1) * 16],
                            reads=("s_lad",), writes=(f"lad{par}",),
                            q=("sync", "scalar")[par])
                    mm3(ps[3][0:16, 0:Q],
                        (ladt[0][:, par * 16:(par + 1) * 16],
                         ladt[1][:, par * 16:(par + 1) * 16]),
                        (x2h[:, o:o + Q], x2l[:, o:o + Q]),
                        first=(hc == 0), last=(hc == NHC - 1),
                        res_r=(f"lad{par}", f"x2p{hpar}", "ps3"), res_w=("ps3",))
                for dc in range(NDC):
                    par = dc % 2
                    (wtiles, wres) = wb_slot(par)
                    for h in range(2):
                        dma(wtiles[h],
                            s_wdn[h][dc][:, hbk * 8 * 128:(hbk + 1) * 8 * 128],
                            reads=("s_wdn",), writes=(wres,),
                            q=("sync", "scalar")[par])
                    pb = ps[4 + par][:, 0:Q]
                    for jj in range(8):
                        o = (hpar * 8 + jj) * Q
                        mm3(pb, (wtiles[0][:, jj * 128:(jj + 1) * 128],
                                 wtiles[1][:, jj * 128:(jj + 1) * 128]),
                            (x2h[:, o:o + Q], x2l[:, o:o + Q]),
                            first=(jj == 0), last=(jj == 7 and hbk != 7),
                            res_r=(wres, f"x2p{hpar}", f"ps{4 + par}"),
                            res_w=(f"ps{4 + par}",))
                    if hbk == 7:
                        if dc == 0:
                            E("scalar", lambda e: e.copy(
                                out=tG[0:16, :], in_=ps[3][0:16, 0:Q]),
                              ("ps3", "tG"), ("tG", "ps3"))
                            split_pair(tG[0:16, :], t2h[:], t2l[:], ("tG",),
                                       ("dtipair",))
                        for h in range(2):
                            dma(lbdt[h][:, par * 128:(par + 1) * 128],
                                s_lbd[h][:, dc * 128:(dc + 1) * 128],
                                reads=("s_lbd",), writes=(f"lbu{par}",),
                                q=("sync", "scalar")[par])
                        mm3(pb, (lbdt[0][:, par * 128:(par + 1) * 128],
                                 lbdt[1][:, par * 128:(par + 1) * 128]),
                            (t2h[:], t2l[:]), first=False, last=True,
                            res_r=(f"lbu{par}", "dtipair", f"ps{4 + par}"),
                            res_w=(f"ps{4 + par}",))
                    cur = hbk % 2
                    prv = 1 - cur
                    dst = u_buf[:, (cur * NDC + dc) * Q:(cur * NDC + dc + 1) * Q]
                    srcp = u_buf[:, (prv * NDC + dc) * Q:(prv * NDC + dc + 1) * Q]
                    if hbk == 0:
                        E("scalar", lambda e, pb=pb, d=dst: e.copy(
                            out=d, in_=pb),
                          (f"ps{4 + par}", "u"), ("u", f"ps{4 + par}"))
                    else:
                        E("vector", lambda e, pb=pb, d=dst, s=srcp:
                          e.tensor_tensor(out=d, in0=pb, in1=s, op=AL.add),
                          (f"ps{4 + par}", "u"), ("u", f"ps{4 + par}"))
            for dc in range(NDC):
                y2fin = u_buf[:, (NDC + dc) * Q:(NDC + dc + 1) * Q]
                E("vector", lambda e, s=y2fin, dc=dc: e.tensor_scalar_add(
                    out=tA[0][:], in0=s, scalar1=bdn_t[:, dc:dc + 1]),
                  ("u", "bdn", "tA0"), ("tA0",))
                E("vector", lambda e, dc=dc: e.tensor_tensor(
                    out=tB[1][:], in0=tA[0][:],
                    in1=x1h_buf[:, dc * Q:(dc + 1) * Q], op=AL.add),
                  ("tA0", "x1pair", "tB1"), ("tB1",))
                E("vector", lambda e, dc=dc: e.tensor_tensor(
                    out=res_buf[:, dc * Q:(dc + 1) * Q], in0=tB[1][:],
                    in1=x1l_buf[:, dc * Q:(dc + 1) * Q], op=AL.add),
                  ("tB1", "x1pair"), ("res",))
            if c == tapc:
                tap("ffnres", res_buf[:], (128, NDC * Q), "res")

            def ln2_out(dc, c=c):
                E("sync", lambda e, dc=dc, c=c: e.dma_start(
                    out=out_t[dc * 128:(dc + 1) * 128, c * Q:(c + 1) * Q],
                    in_=tB[dc % 2][:]), (f"tB{dc % 2}",), ("outdma",))
            layer_norm(res_buf, "res", ln2_out, 4, 5, 6)

        A.final_waits = ["sync", "gpsimd"]
        A.assemble(nc, block)

    return nc, tap_names


def prep_inputs(inputs):
    f32 = np.float32
    g = {k: np.asarray(v) for k, v in inputs.items()}
    x = g["L0_attn_input_states"]
    base = {}
    base["w_in"] = np.ascontiguousarray(g["L0_attn_in_proj_w"], f32)
    cw = np.asarray(g["L0_attn_conv_w"], f32).reshape(IN, 4)
    base["conv_w"] = np.ascontiguousarray(
        cw.reshape(NIC, 128, 4).transpose(1, 0, 2).reshape(128, NIC * 4))
    base["conv_b"] = np.ascontiguousarray(
        np.asarray(g["L0_attn_conv_b"], f32).reshape(NIC, 128).T)
    xp = np.asarray(g["L0_attn_x_proj_w"], f32)
    base["xp_dt"] = np.ascontiguousarray(xp[:, 0:R])
    base["xp_b"] = np.ascontiguousarray(xp[:, R:R + NS])
    base["xp_c"] = np.ascontiguousarray(xp[:, R + NS:R + 2 * NS])
    base["dtp_w"] = np.ascontiguousarray(g["L0_attn_dt_proj_w"], f32)
    base["dtp_b"] = np.ascontiguousarray(
        np.asarray(g["L0_attn_dt_proj_b"], f32).reshape(NIC, 128).T)
    base["a_log"] = np.ascontiguousarray(
        np.asarray(g["L0_attn_A_log"], f32).reshape(NIC, 128, NS)
        .transpose(1, 0, 2).reshape(128, NIC * NS))
    base["d_vec"] = np.ascontiguousarray(
        np.asarray(g["L0_attn_D"], f32).reshape(NIC, 128).T)
    base["op_w"] = np.ascontiguousarray(g["L0_attn_out_proj_w"], f32)
    base["wup_q"] = np.ascontiguousarray(np.asarray(g["L0_ffn_w_up_q"]).T)
    base["wup_s"] = np.ascontiguousarray(np.asarray(g["L0_ffn_w_up_scale"], f32).T)
    base["b_up"] = np.ascontiguousarray(
        np.asarray(g["L0_ffn_b_up"], f32).reshape(NHC, 128).T)
    base["lau"] = np.ascontiguousarray(g["L0_ffn_w_up_lora_a"], f32)
    base["lbu"] = np.ascontiguousarray(g["L0_ffn_w_up_lora_b"], f32)
    base["wdn_q"] = np.ascontiguousarray(np.asarray(g["L0_ffn_w_down_q"]).T)
    base["wdn_s"] = np.ascontiguousarray(
        np.asarray(g["L0_ffn_w_down_scale"], f32).T)
    base["b_dn"] = np.ascontiguousarray(
        np.asarray(g["L0_ffn_b_down"], f32).reshape(NDC, 128).T)
    base["lad"] = np.ascontiguousarray(g["L0_ffn_w_down_lora_a"], f32)
    base["lbd"] = np.ascontiguousarray(g["L0_ffn_w_down_lora_b"], f32)
    sel16v = np.zeros((NS, NS * 128), f32)
    for n in range(NS):
        sel16v[n, n * 128:(n + 1) * 128] = 1.0
    base["sel16"] = sel16v
    sel2v = np.zeros((2, 128), f32)
    sel2v[0, 0:64] = 1.0
    sel2v[1, 64:128] = 1.0
    base["sel2"] = sel2v
    base["ones_c"] = np.ones((128, 1), f32)

    in_maps = []
    for cid in range(NCORES):
        m = dict(base)
        m["x_t"] = np.ascontiguousarray(x[cid].T.astype(f32))
        in_maps.append(m)
    return in_maps


def kernel(**inputs):
    in_maps = prep_inputs(inputs)
    nc, _ = build_program()
    res = run_bass_kernel_spmd(nc, in_maps, core_ids=list(range(NCORES)))
    out = np.stack([np.asarray(r["out_t"]).T for r in res.results])
    return np.ascontiguousarray(out).astype(np.float32)


# version 06
# speedup vs baseline: 1.1724x; 1.1533x over previous
import numpy as np
from contextlib import ExitStack
import concourse.bass as bass
import concourse.mybir as mybir
from concourse.bass_utils import run_bass_kernel_spmd

F32 = mybir.dt.float32
BF16 = mybir.dt.bfloat16
I32 = mybir.dt.int32
AL = mybir.AluOpType
AF = mybir.ActivationFunctionType
AX = mybir.AxisListType

T, D, IN, NS, R, HF = 2048, 2048, 4096, 16, 128, 8192
Q = 256
NCH = T // Q
NDC = D // 128
NIC = IN // 128
NHC = HF // 128
NG = 4
GN = NS // NG
EPS = 1e-5
NCORES = 8

ENGS = ("sync", "tensor", "vector", "scalar", "gpsimd")
DMA_ENGS = {"sync"}


class Asm:

    def __init__(self):
        self.ops = []
        self.final_waits = []

    def emit(self, eng, build, reads=(), writes=(), sfence=False, is_dma=False):
        self.ops.append([eng, build, tuple(reads), tuple(writes), False, sfence,
                         is_dma or eng in DMA_ENGS])

    def assemble(self, nc, block):
        ops = self.ops
        n = len(ops)
        last_w = {}
        readers = {}
        deps = [None] * n
        for i, (eng, _b, rd, wr, _f, sfence, _d) in enumerate(ops):
            dd = set()
            for r in rd:
                j = last_w.get(r)
                if j is not None and (ops[j][0] != eng or sfence):
                    dd.add(j)
            for r in wr:
                j = last_w.get(r)
                if j is not None and (ops[j][0] != eng or sfence):
                    dd.add(j)
                for j in readers.get(r, ()):
                    if ops[j][0] != eng or sfence:
                        dd.add(j)
            for r in rd:
                readers.setdefault(r, []).append(i)
            for r in wr:
                last_w[r] = i
                readers[r] = []
            deps[i] = dd
        for i in range(n):
            for j in deps[i]:
                ops[j][4] = True
        def key_of(i):
            eng, _b, _r, wr, _m, _sf, isd = ops[i]
            if isd:
                assert wr, f"dma op {i} needs a write resource"
                return "d:" + wr[0]
            return eng
        for op in ops:
            if op[6]:
                op[4] = True
        last_of = {}
        for i, op in enumerate(ops):
            last_of[op[0]] = i
        for e in ENGS:
            if e in last_of:
                ops[last_of[e]][4] = True
        kcount = {}
        inc_val = [0] * n
        for i in range(n):
            k = key_of(i)
            if ops[i][4]:
                kcount[k] = kcount.get(k, 0) + (16 if ops[i][6] else 1)
            inc_val[i] = kcount.get(k, 0)
        need_keys = [None] * n
        for i in range(n):
            ks = set()
            for j in deps[i]:
                ks.add(key_of(j))
            need_keys[i] = ks
        run_cnt = {}
        wait_val = [None] * n
        for i in range(n):
            if need_keys[i]:
                wv = {}
                for j in deps[i]:
                    k = key_of(j)
                    if k.startswith("d:"):
                        wv[k] = run_cnt.get(k, 0)
                    else:
                        wv[k] = max(wv.get(k, 0), inc_val[j])
                wait_val[i] = wv
            k = key_of(i)
            if ops[i][4]:
                run_cnt[k] = run_cnt.get(k, 0) + (16 if ops[i][6] else 1)
        finals = dict(run_cnt)
        all_keys = sorted(finals)
        sems = {k: nc.alloc_semaphore(f"s_{k.replace(':', '_')}")
                for k in all_keys}
        watermark = {e: {} for e in ENGS}

        def run_engine(e_obj, eng):
            wm = watermark[eng]
            for i in range(n):
                oe, build, _r, _w, marked, _sf, isd = ops[i]
                if oe != eng:
                    continue
                if wait_val[i]:
                    for k in sorted(wait_val[i]):
                        v = wait_val[i][k]
                        if v > wm.get(k, 0):
                            e_obj.wait_ge(sems[k], v)
                            wm[k] = v
                ins = build(e_obj)
                if marked:
                    ins.then_inc(sems[key_of(i)], 16 if isd else 1)
            if eng in self.final_waits:
                for k in all_keys:
                    if k != eng and finals[k] > watermark[eng].get(k, 0):
                        e_obj.wait_ge(sems[k], finals[k])
                        watermark[eng][k] = finals[k]

        for eng, starter in (("sync", block.sync), ("tensor", block.tensor),
                             ("vector", block.vector), ("scalar", block.scalar),
                             ("gpsimd", block.gpsimd)):
            starter(lambda e, eng=eng: run_engine(e, eng))


def build_program(taps=(), nch=NCH, tapc=0):
    nc = bass.Bass("TRN2", target_bir_lowering=False,
                   detect_race_conditions=False)
    A = Asm()
    tap_names = []

    def din(name, shape, dt=F32):
        return nc.dram_tensor(name, list(shape), dt, kind="ExternalInput").ap()

    x_t = din("x_t", (D, T))
    w_in = din("w_in", (D, HF))
    conv_w = din("conv_w", (128, NIC * 4))
    conv_b = din("conv_b", (128, NIC))
    xp_dt = din("xp_dt", (IN, R))
    xp_b = din("xp_b", (IN, NS))
    xp_c = din("xp_c", (IN, NS))
    dtp_w = din("dtp_w", (R, IN))
    dtp_b = din("dtp_b", (128, NIC))
    a_log = din("a_log", (128, NIC * NS))
    d_vec = din("d_vec", (128, NIC))
    op_w = din("op_w", (IN, D))
    wup_q = din("wup_q", (D, HF), I32)
    wup_s = din("wup_s", (2 * NDC, HF))
    b_up = din("b_up", (128, NHC))
    lau = din("lau", (D, 16))
    lbu = din("lbu", (16, HF))
    wdn_q = din("wdn_q", (HF, D), I32)
    wdn_s = din("wdn_s", (2 * NHC, D))
    b_dn = din("b_dn", (128, NDC))
    lad = din("lad", (HF, 16))
    lbd = din("lbd", (16, D))
    sel16 = din("sel16", (NS, NS * 128))
    sel2 = din("sel2", (2, 128))
    ones_c = din("ones_c", (128, 1))

    out_t = nc.dram_tensor("out_t", [D, T], F32, kind="ExternalOutput").ap()

    def scr(name, shape, dt=BF16):
        return nc.dram_tensor(name, list(shape), dt, kind="Internal").ap()

    s_win = [scr("s_win_h", (2 * NIC, 128, NDC * 128)),
             scr("s_win_l", (2 * NIC, 128, NDC * 128))]
    s_opw = [scr("s_opw_h", (NIC, 128, NDC * 128)),
             scr("s_opw_l", (NIC, 128, NDC * 128))]
    s_dtp = [scr("s_dtp_h", (NIC, 128, 128)), scr("s_dtp_l", (NIC, 128, 128))]
    s_xp = [scr("s_xp_h", (NIC, 128, R + 2 * NS)),
            scr("s_xp_l", (NIC, 128, R + 2 * NS))]
    s_wup = [scr("s_wup_h", (NHC, 128, NDC * 128)),
             scr("s_wup_l", (NHC, 128, NDC * 128))]
    s_wdn = [scr("s_wdn_h", (NDC, 128, NHC * 128)),
             scr("s_wdn_l", (NDC, 128, NHC * 128))]
    s_lau = [scr("s_lau_h", (128, NDC * 16)), scr("s_lau_l", (128, NDC * 16))]
    s_lbu = [scr("s_lbu_h", (16, HF)), scr("s_lbu_l", (16, HF))]
    s_lad = [scr("s_lad_h", (128, NHC * 16)), scr("s_lad_l", (128, NHC * 16))]
    s_lbd = [scr("s_lbd_h", (16, D)), scr("s_lbd_l", (16, D))]

    es = ExitStack()

    def sbt(name, shape, dt=F32):
        return es.enter_context(nc.sbuf_tensor(name, list(shape), dt))

    with es:
        u_buf = sbt("u_buf", (128, NIC * Q))
        bb_buf = sbt("bb_buf", (128, NS * Q))
        cb_buf = sbt("cb_buf", (128, NS * Q))
        scanh = sbt("scanh", (128, GN * Q))
        tmpP = sbt("tmpP", (128, GN * Q))
        tmpQ = sbt("tmpQ", (128, GN * Q))
        res_buf = sbt("res_buf", (128, NDC * Q))
        sc0p = [res_buf[:, 0:GN * Q], res_buf[:, GN * Q:2 * GN * Q]]
        sc1p = [res_buf[:, 2 * GN * Q:3 * GN * Q],
                res_buf[:, 3 * GN * Q:4 * GN * Q]]
        dtt = [tmpP[:, 0:Q], tmpP[:, Q:2 * Q]]
        dtut = [tmpP[:, 2 * Q:3 * Q], tmpP[:, 3 * Q:4 * Q]]
        get_ = [tmpQ[:, 0:Q], tmpQ[:, Q:2 * Q]]
        ggt = [tmpQ[:, 2 * Q:3 * Q], tmpQ[:, 3 * Q:4 * Q]]
        y4 = sbt("y4", (128, NG * Q))
        st1 = y4[0:1, :]
        mb_t = y4[:, 0:Q]
        rb_t = y4[:, Q:2 * Q]
        hidt = [sbt(f"hid{i}", (128, Q + 4)) for i in range(2)]
        tA = [sbt(f"tA{i}", (128, Q)) for i in range(2)]
        tB = [sbt(f"tB{i}", (128, Q)) for i in range(2)]
        tC = [sbt(f"tC{i}", (128, Q)) for i in range(2)]
        tD = sbt("tD", (128, Q))
        tE = sbt("tE", (128, Q))
        tF = sbt("tF", (128, Q))
        tG = sbt("tG", (128, Q))
        tH = sbt("tH", (128, Q))
        tt1 = [tD, tE]
        tt2 = [tF, tH]
        a_buf = sbt("a_buf", (128, NIC * NS))
        state = sbt("state", (128, NIC * NS))
        tails = sbt("tails", (128, NIC * 4))
        sel16_t = sbt("sel16_t", (NS, NS * 128))
        sel2_t = sbt("sel2_t", (2, 128))
        bc1_t = sel16_t[0:1, 0:128]
        ones_t = sbt("ones_t", (128, 1))
        convw_t = sbt("convw_t", (128, NIC * 4))
        convb_t = sbt("convb_t", (128, NIC))
        dtpb_t = sbt("dtpb_t", (128, NIC))
        dvec_t = sbt("dvec_t", (128, NIC))
        bup_t = sbt("bup_t", (128, NHC))
        bdn_t = sbt("bdn_t", (128, NDC))
        bt_t = sbt("bt_t", (NS, Q))
        ct_t = sbt("ct_t", (NS, Q))
        xst = sbt("xst", (128, 2 * Q))
        xh_buf = sbt("xh_buf", (128, NDC * Q), BF16)
        xl_buf = sbt("xl_buf", (128, NDC * Q), BF16)
        x1h_buf = sbt("x1h_buf", (128, NDC * Q), BF16)
        x1l_buf = sbt("x1l_buf", (128, NDC * Q), BF16)
        x2h = sbt("x2h", (128, 16 * Q), BF16)
        x2l = sbt("x2l", (128, 16 * Q), BF16)
        wa_h = sbt("wa_h", (128, 2 * NDC * 128), BF16)
        wa_l = sbt("wa_l", (128, 2 * NDC * 128), BF16)
        wb_h = sbt("wb_h", (128, 2 * 8 * 128), BF16)
        wb_l = sbt("wb_l", (128, 2 * 8 * 128), BF16)
        dtih = sbt("dtih", (128, Q), BF16)
        dtil = sbt("dtil", (128, Q), BF16)
        uht = [sbt(f"uh{i}", (128, Q), BF16) for i in range(2)]
        ult = [sbt(f"ul{i}", (128, Q), BF16) for i in range(2)]
        yph = uht
        ypl = ult
        t1h = dtih[0:16, :]
        t1l = dtil[0:16, :]
        t2h = dtih[0:16, :]
        t2l = dtil[0:16, :]
        laut = [sbt("laut_h", (128, NDC * 16), BF16),
                sbt("laut_l", (128, NDC * 16), BF16)]
        lbut = [sbt("lbut_h", (16, 2 * 128), BF16),
                sbt("lbut_l", (16, 2 * 128), BF16)]
        ladt = [sbt("ladt_h", (128, 2 * 16), BF16),
                sbt("ladt_l", (128, 2 * 16), BF16)]
        lbdt = lbut
        pp_f = [res_buf[:, 0:2048], res_buf[:, 2048:4096]]
        pp_h = [x2h[:, 0:2048], x2h[:, 2048:4096]]
        pp_l = [x2l[:, 0:2048], x2l[:, 2048:4096]]
        pp_s = [u_buf[:, 0:1024], u_buf[:, 1024:2048]]
        pp_q0 = sbt("pp_q0", (128, 512), I32)
        pp_q = [pp_q0, pp_q0]
        sc_r = [bb_buf[0:2, 0:1024], cb_buf[0:2, 0:1024]]
        ppfr = ["res", "res1"]
        pphr = ["x2p0", "x2p1"]
        ppsr = ["u", "u1"]
        ppqr = ["ppq0", "ppq0"]
        scrr = ["bb", "cb"]

        ps = [es.enter_context(nc.psum_tensor(f"pb{i}", [128, 512], F32))
              for i in range(8)]

        block = es.enter_context(nc.Block())

        def E(eng, build, reads=(), writes=(), sfence=False, is_dma=False):
            A.emit(eng, build, reads, writes, sfence, is_dma)

        def dma(dst, src, reads=(), writes=(), q="sync"):
            E(q, lambda e, d=dst, s=src: e.dma_start(out=d, in_=s),
              reads, writes, is_dma=True)

        def tap(name, ap, shape, res):
            if name not in taps:
                return
            h = nc.dram_tensor("tap_" + name, list(shape), ap.dtype,
                               kind="ExternalOutput").ap()
            tap_names.append(name)
            dma(h[:], ap, reads=(res,), writes=("tapdma",))

        def mm3(psum_ap, lhs, rhs, first, last, res_r, res_w, skip=True):
            lh, ll = lhs
            rh, rl = rhs
            E("tensor", lambda e, a=psum_ap, x=lh, y=rh, st=first:
              e.matmul(a, x, y, start=st, stop=False, skip_group_check=skip),
              res_r, res_w if not last else ())
            E("tensor", lambda e, a=psum_ap, x=lh, y=rl:
              e.matmul(a, x, y, start=False, stop=False, skip_group_check=skip),
              res_r, ())
            E("tensor", lambda e, a=psum_ap, x=ll, y=rh, sp=last:
              e.matmul(a, x, y, start=False, stop=sp, skip_group_check=skip),
              res_r, res_w if last else ())

        def split_pair(src_ap, dst_h, dst_l, rd, wr):
            E("vector", lambda e, s=src_ap, d=dst_h: e.tensor_copy(out=d, in_=s),
              rd, wr)
            E("vector", lambda e, s=src_ap, h=dst_h, d=dst_l: e.tensor_tensor(
                out=d, in0=s, in1=h, op=AL.subtract), rd + wr, wr)

        for dst, src, tg in ((sel16_t, sel16, "sel16"), (sel2_t, sel2, "sel2"),
                             (ones_t, ones_c, "ones"),
                             (convw_t, conv_w, "convw"), (convb_t, conv_b, "convb"),
                             (dtpb_t, dtp_b, "dtpb"), (dvec_t, d_vec, "dvec"),
                             (bup_t, b_up, "bup"), (bdn_t, b_dn, "bdn")):
            dma(dst[:], src[:], writes=(tg,))

        pp_ctr = [0]

        def pp_next():
            p = pp_ctr[0] & 1
            pp_ctr[0] += 1
            return p, ("sync", "scalar")[p]

        def pp_split(p, cols, rows=128):
            split_pair(pp_f[p][0:rows, 0:cols], pp_h[p][0:rows, 0:cols],
                       pp_l[p][0:rows, 0:cols], (ppfr[p],), (pphr[p],))

        for dc in range(NDC):
            for piece in range(4):
                p, q = pp_next()
                dma(pp_f[p][:, 0:2048],
                    w_in[dc * 128:(dc + 1) * 128, piece * 2048:(piece + 1) * 2048],
                    writes=(ppfr[p],), q=q)
                pp_split(p, 2048)
                for jj in range(16):
                    jc = piece * 16 + jj
                    for h in range(2):
                        dma(s_win[h][jc][:, dc * 128:(dc + 1) * 128],
                            [pp_h, pp_l][h][p][:, jj * 128:(jj + 1) * 128],
                            reads=(pphr[p],), writes=("s_win",), q=q)
        for ic in range(NIC):
            p, q = pp_next()
            dma(pp_f[p][:, 0:2048], op_w[ic * 128:(ic + 1) * 128, :],
                writes=(ppfr[p],), q=q)
            pp_split(p, 2048)
            for h in range(2):
                dma(s_opw[h][ic], [pp_h, pp_l][h][p][:, 0:2048],
                    reads=(pphr[p],), writes=("s_opw",), q=q)
        for piece in range(2):
            p, q = pp_next()
            dma(pp_f[p][:, 0:2048], dtp_w[:, piece * 2048:(piece + 1) * 2048],
                writes=(ppfr[p],), q=q)
            pp_split(p, 2048)
            for jj in range(16):
                ic = piece * 16 + jj
                for h in range(2):
                    dma(s_dtp[h][ic],
                        [pp_h, pp_l][h][p][:, jj * 128:(jj + 1) * 128],
                        reads=(pphr[p],), writes=("s_dtp",), q=q)
        for ic in range(NIC):
            p, q = pp_next()
            dma(pp_f[p][:, 0:R], xp_dt[ic * 128:(ic + 1) * 128, :],
                writes=(ppfr[p],), q=q)
            dma(pp_f[p][:, R:R + NS], xp_b[ic * 128:(ic + 1) * 128, :],
                writes=(ppfr[p],), q=q)
            dma(pp_f[p][:, R + NS:R + 2 * NS], xp_c[ic * 128:(ic + 1) * 128, :],
                writes=(ppfr[p],), q=q)
            pp_split(p, R + 2 * NS)
            for h in range(2):
                dma(s_xp[h][ic], [pp_h, pp_l][h][p][:, 0:R + 2 * NS],
                    reads=(pphr[p],), writes=("s_xp",), q=q)
        p, q = pp_next()
        for dc in range(NDC):
            dma(pp_f[p][:, dc * 16:(dc + 1) * 16],
                lau[dc * 128:(dc + 1) * 128, :], writes=(ppfr[p],), q=q)
        pp_split(p, NDC * 16)
        for h in range(2):
            dma(s_lau[h][:, :], [pp_h, pp_l][h][p][:, 0:NDC * 16],
                reads=(pphr[p],), writes=("s_lau",), q=q)
        for half in range(2):
            p, q = pp_next()
            for j in range(NHC // 2):
                hc = half * (NHC // 2) + j
                dma(pp_f[p][:, j * 16:(j + 1) * 16],
                    lad[hc * 128:(hc + 1) * 128, :], writes=(ppfr[p],), q=q)
            pp_split(p, (NHC // 2) * 16)
            for h in range(2):
                dma(s_lad[h][:, half * (NHC // 2) * 16:
                             (half + 1) * (NHC // 2) * 16],
                    [pp_h, pp_l][h][p][:, 0:(NHC // 2) * 16],
                    reads=(pphr[p],), writes=("s_lad",), q=q)
        for piece in range(4):
            p, q = pp_next()
            dma(pp_f[p][0:16, 0:2048], lbu[:, piece * 2048:(piece + 1) * 2048],
                writes=(ppfr[p],), q=q)
            pp_split(p, 2048, rows=16)
            for h in range(2):
                dma(s_lbu[h][:, piece * 2048:(piece + 1) * 2048],
                    [pp_h, pp_l][h][p][0:16, 0:2048],
                    reads=(pphr[p],), writes=("s_lbu",), q=q)
        p, q = pp_next()
        dma(pp_f[p][0:16, 0:2048], lbd[:, :], writes=(ppfr[p],), q=q)
        pp_split(p, 2048, rows=16)
        for h in range(2):
            dma(s_lbd[h][:, :], [pp_h, pp_l][h][p][0:16, 0:2048],
                reads=(pphr[p],), writes=("s_lbd",), q=q)

        def dequant(qsrc, ssrc, dst, n_stripes, n_cols, tile_res):
            for si in range(n_stripes):
                for half in range(n_cols // 512):
                    c0 = half * 512
                    p, q = pp_next()
                    dma(sc_r[p][:, 0:512],
                        ssrc[2 * si:2 * si + 2, c0:c0 + 512],
                        writes=(scrr[p],), q=q)
                    dma(pp_q[p][:, 0:512],
                        qsrc[si * 128:(si + 1) * 128, c0:c0 + 512],
                        writes=(ppqr[p],), q=q)
                    E("tensor", lambda e, p=p: e.matmul(
                        ps[6 + p][:, 0:512], sel2_t[:], sc_r[p][:, 0:512],
                        start=True, stop=True, skip_group_check=True),
                      (scrr[p], "sel2", f"ps{6 + p}"), (f"ps{6 + p}",))
                    E("scalar", lambda e, p=p: e.copy(
                        out=pp_s[p][:, 0:512], in_=ps[6 + p][:, 0:512]),
                      (f"ps{6 + p}",), (ppsr[p], f"ps{6 + p}"))
                    E("vector", lambda e, p=p: e.scalar_tensor_tensor(
                        out=pp_f[p][:, 0:512], in0=pp_q[p][:, 0:512],
                        scalar=-7.5, in1=pp_s[p][:, 0:512],
                        op0=AL.add, op1=AL.mult),
                      (ppqr[p], ppsr[p]), (ppfr[p],))
                    pp_split(p, 512)
                    for jj in range(4):
                        j = half * 4 + jj
                        for h in range(2):
                            dma(dst[h][j][:, si * 128:(si + 1) * 128],
                                [pp_h, pp_l][h][p][:, jj * 128:(jj + 1) * 128],
                                reads=(pphr[p],), writes=(tile_res,), q=q)
        dequant(wup_q, wup_s, s_wup, NDC, HF, "s_wup")
        dequant(wdn_q, wdn_s, s_wdn, NHC, D, "s_wdn")

        p, q = pp_next()
        dma(pp_f[p][:, 0:NIC * NS], a_log[:], writes=(ppfr[p],), q=q)
        E("scalar", lambda e, p=p: e.activation(out=pp_s[p][:, 0:NIC * NS],
                                                in_=pp_f[p][:, 0:NIC * NS],
                                                func=AF.Exp),
          (ppfr[p],), (ppsr[p],))
        E("vector", lambda e, p=p: e.tensor_scalar_mul(
            out=a_buf[:], in0=pp_s[p][:, 0:NIC * NS], scalar1=-1.0),
          (ppsr[p],), ("abuf",))
        E("vector", lambda e: e.memset(state[:], 0.0), (), ("state",))
        E("vector", lambda e: e.memset(tails[:], 0.0), (), ("tails",))
        E("vector", lambda e: e.engine_nop(),
          ("res", "res1", "u", "u1", "x2p0", "x2p1", "bb", "cb"),
          ("res", "res1", "u", "u1", "x2p0", "x2p1", "bb", "cb"))

        def wa_slot(par):
            o = par * NDC * 128
            return ((wa_h[:, o:o + NDC * 128], wa_l[:, o:o + NDC * 128]),
                    f"wa{par}")

        def wb_slot(par, cols=8 * 128):
            o = par * 8 * 128
            return ((wb_h[:, o:o + cols], wb_l[:, o:o + cols]), f"wb{par}")

        for c in range(nch):
            tc0 = c * Q

            for dc in range(NDC):
                xp_ = dc % 2
                dma(xst[:, xp_ * Q:(xp_ + 1) * Q],
                    x_t[dc * 128:(dc + 1) * 128, tc0:tc0 + Q],
                    writes=(f"xst{xp_}",))
                split_pair(xst[:, xp_ * Q:(xp_ + 1) * Q],
                           xh_buf[:, dc * Q:(dc + 1) * Q],
                           xl_buf[:, dc * Q:(dc + 1) * Q],
                           (f"xst{xp_}",), ("xpair",))

            for ic in range(NIC):
                par = ic % 2
                (wtiles, wres) = wa_slot(par)
                for h in range(2):
                    dma(wtiles[h], s_win[h][ic], reads=("s_win",), writes=(wres,))
                pb = ps[par][:, 0:Q]
                for dc in range(NDC):
                    mm3(pb, (wtiles[0][:, dc * 128:(dc + 1) * 128],
                             wtiles[1][:, dc * 128:(dc + 1) * 128]),
                        (xh_buf[:, dc * Q:(dc + 1) * Q],
                         xl_buf[:, dc * Q:(dc + 1) * Q]),
                        first=(dc == 0), last=(dc == NDC - 1),
                        res_r=(wres, "xpair", f"ps{par}"), res_w=(f"ps{par}",))
                hb = hidt[par]
                E("scalar", lambda e, ic=ic, hb=hb: e.copy(
                    out=hb[:, 0:3], in_=tails[:, ic * 4:ic * 4 + 3]),
                  ("tails",), (f"hid{par}",))
                E("scalar", lambda e, pb=pb, hb=hb: e.copy(
                    out=hb[:, 3:3 + Q], in_=pb),
                  (f"ps{par}",), (f"hid{par}", f"ps{par}"))
                E("scalar", lambda e, ic=ic, hb=hb: e.copy(
                    out=tails[:, ic * 4:ic * 4 + 3], in_=hb[:, Q:Q + 3]),
                  (f"hid{par}",), ("tails",))
                E("vector", lambda e, ic=ic, hb=hb, par=par: e.tensor_scalar(
                    out=tB[par][:], in0=hb[:, 0:Q],
                    scalar1=convw_t[:, ic * 4:ic * 4 + 1],
                    scalar2=convb_t[:, ic:ic + 1], op0=AL.mult, op1=AL.add),
                  (f"hid{par}", "convw", "convb", f"tB{par}"), (f"tB{par}",))
                E("vector", lambda e, ic=ic, hb=hb, par=par:
                  e.scalar_tensor_tensor(
                      out=tC[par][:], in0=hb[:, 1:1 + Q],
                      scalar=convw_t[:, ic * 4 + 1:ic * 4 + 2], in1=tB[par][:],
                      op0=AL.mult, op1=AL.add),
                  (f"hid{par}", f"tB{par}", "convw", f"tC{par}"), (f"tC{par}",))
                E("vector", lambda e, ic=ic, hb=hb, par=par:
                  e.scalar_tensor_tensor(
                      out=tB[par][:], in0=hb[:, 2:2 + Q],
                      scalar=convw_t[:, ic * 4 + 2:ic * 4 + 3], in1=tC[par][:],
                      op0=AL.mult, op1=AL.add),
                  (f"hid{par}", f"tC{par}", "convw", f"tB{par}"), (f"tB{par}",))
                E("vector", lambda e, ic=ic, hb=hb, par=par:
                  e.scalar_tensor_tensor(
                      out=tC[par][:], in0=hb[:, 3:3 + Q],
                      scalar=convw_t[:, ic * 4 + 3:ic * 4 + 4], in1=tB[par][:],
                      op0=AL.mult, op1=AL.add),
                  (f"hid{par}", f"tB{par}", "convw", f"tC{par}"), (f"tC{par}",))
                E("scalar", lambda e, ic=ic, par=par: e.activation(
                    out=u_buf[:, ic * Q:(ic + 1) * Q], in_=tC[par][:],
                    func=AF.Silu), (f"tC{par}",), ("u",))
            if c == tapc:
                tap("u", u_buf[:], (128, NIC * Q), "u")

            for ic in range(NIC):
                par = ic % 2
                (wtiles, wres) = wb_slot(par, cols=R + 2 * NS)
                for h in range(2):
                    dma(wtiles[h], s_xp[h][ic], reads=("s_xp",), writes=(wres,))
                split_pair(u_buf[:, ic * Q:(ic + 1) * Q],
                           uht[par][:], ult[par][:], ("u",), (f"upair{par}",))
                upair = (uht[par][:], ult[par][:])
                mm3(ps[2][:, 0:Q], (wtiles[0][:, 0:R], wtiles[1][:, 0:R]),
                    upair, first=(ic == 0), last=(ic == NIC - 1),
                    res_r=(wres, f"upair{par}", "ps2"), res_w=("ps2",))
                mm3(ps[3][0:NS, 0:Q],
                    (wtiles[0][:, R:R + NS], wtiles[1][:, R:R + NS]),
                    upair, first=(ic == 0), last=(ic == NIC - 1),
                    res_r=(wres, f"upair{par}", "ps3"), res_w=("ps3",))
                mm3(ps[4][0:NS, 0:Q],
                    (wtiles[0][:, R + NS:R + 2 * NS],
                     wtiles[1][:, R + NS:R + 2 * NS]),
                    upair, first=(ic == 0), last=(ic == NIC - 1),
                    res_r=(wres, f"upair{par}", "ps4"), res_w=("ps4",))
            E("scalar", lambda e: e.copy(out=tG[:], in_=ps[2][:, 0:Q]),
              ("ps2",), ("tG", "ps2"))
            E("scalar", lambda e: e.copy(out=bt_t[:], in_=ps[3][0:NS, 0:Q]),
              ("ps3",), ("btct", "ps3"))
            E("scalar", lambda e: e.copy(out=ct_t[:], in_=ps[4][0:NS, 0:Q]),
              ("ps4",), ("btct", "ps4"))
            if c == tapc:
                tap("dtin", tG[:], (128, Q), "tG")
                tap("bt", bt_t[:], (NS, Q), "btct")
                tap("ct", ct_t[:], (NS, Q), "btct")
            split_pair(tG[:], dtih[:], dtil[:], ("tG",), ("dtipair",))

            for which, src_t, dst_buf, dres in ((0, bt_t, bb_buf, "bb"),
                                                (1, ct_t, cb_buf, "cb")):
                for n in range(NS):
                    p2 = n % 2
                    g, j = n // GN, n % GN
                    E("tensor", lambda e, n=n, p2=p2, s=src_t: e.matmul(
                        ps[5 + p2][:, 0:Q], sel16_t[:, n * 128:(n + 1) * 128],
                        s[:], start=True, stop=True, skip_group_check=True),
                      ("btct", "sel16", f"ps{5 + p2}"), (f"ps{5 + p2}",))
                    E("scalar", lambda e, g=g, j=j, p2=p2, d=dst_buf: e.copy(
                        out=d[:, g * GN * Q:(g + 1) * GN * Q].rearrange(
                            "p (q n) -> p n q", n=GN)[:, j, :],
                        in_=ps[5 + p2][:, 0:Q]),
                      (f"ps{5 + p2}",), (dres, f"ps{5 + p2}"))
            if c == tapc:
                tap("bb", bb_buf[:], (128, NS * Q), "bb")

            if c == tapc:
                tap("state_in", state[:], (128, NIC * NS), "state")
            E("vector", lambda e: e.engine_nop(),
              ("res", "z0", "z1", "e0", "e1"),
              ("res", "z0", "z1", "e0", "e1"))
            for ic in range(NIC):
                par = ic % 2
                (wtiles, wres) = wb_slot(par, cols=128)
                for h in range(2):
                    dma(wtiles[h], s_dtp[h][ic], reads=("s_dtp",), writes=(wres,))
                pdt = [ps[7], ps[4]][par]
                pdtr = ["ps7", "ps4"][par]
                mm3(pdt[:, 0:Q], (wtiles[0], wtiles[1]), (dtih[:], dtil[:]),
                    first=True, last=True,
                    res_r=(wres, "dtipair", pdtr), res_w=(pdtr,))
                E("scalar", lambda e, ic=ic, pdt=pdt, par=par: e.activation(
                    out=tt1[par][:], in_=pdt[:, 0:Q], func=AF.Exp,
                    bias=dtpb_t[:, ic:ic + 1]),
                  (pdtr, "dtpb", f"tt1{par}"), (f"tt1{par}", pdtr))
                E("scalar", lambda e, par=par: e.activation(
                    out=dtt[par], in_=tt1[par][:], func=AF.Ln, bias=1.0),
                  (f"tt1{par}", f"dtt{par}"), (f"dtt{par}",))
                if c == tapc and ic == 0:
                    tap("dt0", dtt[0], (128, Q), "dtt0")
                E("vector", lambda e, ic=ic, par=par: e.tensor_tensor(
                    out=dtut[par], in0=dtt[par],
                    in1=u_buf[:, ic * Q:(ic + 1) * Q],
                    op=AL.mult), (f"dtt{par}", "u", f"dtu{par}"), (f"dtu{par}",))
                (gtiles, gres) = wa_slot(par)
                for h in range(2):
                    dma(gtiles[h], s_win[h][NIC + ic], reads=("s_win",),
                        writes=(gres,))
                pbg = ps[5 + par][:, 0:Q]
                for dc in range(NDC):
                    mm3(pbg, (gtiles[0][:, dc * 128:(dc + 1) * 128],
                              gtiles[1][:, dc * 128:(dc + 1) * 128]),
                        (xh_buf[:, dc * Q:(dc + 1) * Q],
                         xl_buf[:, dc * Q:(dc + 1) * Q]),
                        first=(dc == 0), last=(dc == NDC - 1),
                        res_r=(gres, "xpair", f"ps{5 + par}"),
                        res_w=(f"ps{5 + par}",))
                E("scalar", lambda e, p=pbg, par=par: e.activation(
                    out=get_[par], in_=p, func=AF.Exp, scale=-1.0),
                  (f"ps{5 + par}", f"ge{par}"), (f"ge{par}",))
                E("scalar", lambda e, p=pbg, par=par: e.copy(
                    out=ggt[par], in_=p),
                  (f"ps{5 + par}", f"gg{par}"), (f"gg{par}", f"ps{5 + par}"))

                gq = GN * Q

                def emit_zarg(g, par=par, ic=ic):
                    pg = g % 2
                    dt_b = dtt[par].unsqueeze(2).broadcast_to([128, Q, GN])
                    a_sl = a_buf[:, ic * NS + g * GN: ic * NS + (g + 1) * GN]
                    a_b = a_sl.unsqueeze(1).broadcast_to([128, Q, GN])
                    E("vector", lambda e, db=dt_b, ab=a_b, pg=pg:
                      e.tensor_tensor(out=sc0p[pg].rearrange(
                          "p (q n) -> p q n", n=GN), in0=db, in1=ab,
                          op=AL.mult),
                      (f"dtt{par}", "abuf", f"z{pg}"), (f"z{pg}",))
                    E("scalar", lambda e, pg=pg: e.activation(
                        out=sc1p[pg], in_=sc0p[pg], func=AF.Exp),
                      (f"z{pg}", f"e{pg}"), (f"e{pg}",))

                emit_zarg(0)
                emit_zarg(1)
                for g in range(NG):
                    pg = g % 2
                    bb_g = bb_buf[:, g * gq:(g + 1) * gq]
                    cb_g = cb_buf[:, g * gq:(g + 1) * gq]
                    dtu_b = dtut[par].unsqueeze(2).broadcast_to([128, Q, GN])
                    E("vector", lambda e, db=dtu_b, b=bb_g, pg=pg:
                      e.tensor_tensor(out=sc0p[pg].rearrange(
                          "p (q n) -> p q n", n=GN), in0=db,
                          in1=b.rearrange("p (q n) -> p q n", n=GN),
                          op=AL.mult),
                      (f"dtu{par}", "bb", f"z{pg}", f"e{pg}"), (f"z{pg}",))
                    for j in range(GN):
                        n = g * GN + j
                        da_ap = sc1p[pg].rearrange("p (q n) -> p n q",
                                                   n=GN)[:, j, :]
                        w_ap = sc0p[pg].rearrange("p (q n) -> p n q",
                                                  n=GN)[:, j, :]
                        h_ap = scanh[:, 0:gq].rearrange("p (q n) -> p n q",
                                                        n=GN)[:, j, :]
                        E("vector", lambda e, a=da_ap, w=w_ap, o=h_ap,
                          ini=state[:, ic * NS + n:ic * NS + n + 1]:
                          e.tensor_tensor_scan(o, a, w, ini, AL.mult, AL.add),
                          (f"z{pg}", f"e{pg}", "state", "sc2"), ("sc2",))
                    E("vector", lambda e, cb=cb_g, gq=gq, pg=pg: e.tensor_tensor(
                        out=sc1p[pg], in0=scanh[:, 0:gq], in1=cb,
                        op=AL.mult), ("sc2", "cb", f"e{pg}"), (f"e{pg}",))
                    E("vector", lambda e, g=g, ic=ic, gq=gq: e.tensor_copy(
                        out=state[:, ic * NS + g * GN:ic * NS + (g + 1) * GN],
                        in_=scanh[:, gq - GN:gq]), ("sc2", "state"), ("state",))
                    E("vector", lambda e, g=g, gq=gq, pg=pg: e.tensor_reduce(
                        out=y4[:, g * Q:(g + 1) * Q],
                        in_=sc1p[pg].rearrange("p (q n) -> p q n", n=GN),
                        axis=AX.X, op=AL.add), (f"e{pg}", "y4"), ("y4",))
                    if g + 2 < NG:
                        emit_zarg(g + 2)
                E("vector", lambda e, par=par: e.tensor_tensor(
                    out=tt2[par][:], in0=y4[:, 0:Q], in1=y4[:, Q:2 * Q],
                    op=AL.add), ("y4", f"tt2{par}"), (f"tt2{par}",))
                E("vector", lambda e, par=par: e.tensor_tensor(
                    out=tt1[par][:], in0=y4[:, 2 * Q:3 * Q],
                    in1=y4[:, 3 * Q:4 * Q],
                    op=AL.add), ("y4", f"tt1{par}"), (f"tt1{par}",),
                  sfence=True)
                E("vector", lambda e, par=par: e.tensor_scalar_add(
                    out=dtt[par], in0=get_[par], scalar1=1.0),
                  (f"ge{par}", f"dtt{par}"), (f"dtt{par}",))
                E("vector", lambda e, par=par: e.reciprocal(
                    out=get_[par], in_=dtt[par]),
                  (f"dtt{par}", f"ge{par}"), (f"ge{par}",))
                E("vector", lambda e, par=par: e.tensor_tensor(
                    out=dtt[par], in0=ggt[par], in1=get_[par], op=AL.mult),
                  (f"gg{par}", f"ge{par}", f"dtt{par}"), (f"dtt{par}",))
                E("vector", lambda e, par=par: e.tensor_tensor(
                    out=dtut[par], in0=tt2[par][:], in1=tt1[par][:],
                    op=AL.add),
                  (f"tt2{par}", f"tt1{par}", f"dtu{par}"), (f"dtu{par}",))
                E("vector", lambda e, ic=ic, par=par: e.scalar_tensor_tensor(
                    out=tt1[par][:], in0=u_buf[:, ic * Q:(ic + 1) * Q],
                    scalar=dvec_t[:, ic:ic + 1], in1=dtut[par],
                    op0=AL.mult, op1=AL.add),
                  ("u", f"dtu{par}", "dvec", f"tt1{par}"), (f"tt1{par}",))
                E("vector", lambda e, ic=ic, par=par: e.tensor_tensor(
                    out=u_buf[:, ic * Q:(ic + 1) * Q], in0=tt1[par][:],
                    in1=dtt[par], op=AL.mult),
                  (f"tt1{par}", f"dtt{par}", "u"), ("u",))
                if c == tapc and ic == 0:
                    tap("yfin", u_buf[:, 0:Q], (128, Q), "u")
            if c == tapc:
                tap("ymall", u_buf[:], (128, NIC * Q), "u")
            E("vector", lambda e: e.engine_nop(),
              ("res", "z0", "z1", "e0", "e1"),
              ("res", "z0", "z1", "e0", "e1"))

            for grp in range(2):
                for ic in range(NIC):
                    par = ic % 2
                    split_pair(u_buf[:, ic * Q:(ic + 1) * Q],
                               yph[par][:], ypl[par][:], ("u",), (f"upair{par}",))
                    (wtiles, wres) = wb_slot(par)
                    for h in range(2):
                        dma(wtiles[h],
                            s_opw[h][ic][:, grp * 8 * 128:(grp + 1) * 8 * 128],
                            reads=("s_opw",), writes=(wres,))
                    for j in range(8):
                        pa = ps[j][:, 0:Q]
                        mm3(pa, (wtiles[0][:, j * 128:(j + 1) * 128],
                                 wtiles[1][:, j * 128:(j + 1) * 128]),
                            (yph[par][:], ypl[par][:]),
                            first=(ic == 0), last=(ic == NIC - 1),
                            res_r=(wres, f"upair{par}", f"ps{j}"),
                            res_w=(f"ps{j}",))
                for j in range(8):
                    dc = grp * 8 + j
                    pa = ps[j][:, 0:Q]
                    E("vector", lambda e, pa=pa, dc=dc: e.tensor_tensor(
                        out=tA[0][:], in0=pa, in1=xh_buf[:, dc * Q:(dc + 1) * Q],
                        op=AL.add),
                      (f"ps{j}", "xpair", "tA0"),
                      ("tA0", f"ps{j}"))
                    E("vector", lambda e, dc=dc: e.tensor_tensor(
                        out=res_buf[:, dc * Q:(dc + 1) * Q], in0=tA[0][:],
                        in1=xl_buf[:, dc * Q:(dc + 1) * Q], op=AL.add),
                      ("tA0", "xpair"), ("res",))
            if c == tapc:
                tap("attnres", res_buf[:], (128, NDC * Q), "res")

            def layer_norm(src_buf, src_res, out_cb, psA, psB, psC):
                for dc in range(NDC):
                    dp = dc % 2
                    E("scalar", lambda e, dc=dc, dp=dp: e.activation(
                        out=tA[dp][:], in_=src_buf[:, dc * Q:(dc + 1) * Q],
                        func=AF.Square), (src_res, f"tA{dp}"), (f"tA{dp}",))
                    E("tensor", lambda e, dc=dc: e.matmul(
                        ps[psA][0:1, 0:Q], ones_t[:],
                        src_buf[:, dc * Q:(dc + 1) * Q],
                        start=(dc == 0), stop=(dc == NDC - 1),
                        skip_group_check=True),
                      (src_res, "ones", f"ps{psA}"), (f"ps{psA}",))
                    E("tensor", lambda e, dc=dc, dp=dp: e.matmul(
                        ps[psB][0:1, 0:Q], ones_t[:], tA[dp][:],
                        start=(dc == 0), stop=(dc == NDC - 1),
                        skip_group_check=True),
                      (f"tA{dp}", "ones", f"ps{psB}"), (f"ps{psB}",))
                E("scalar", lambda e: e.activation(
                    out=st1[0:1, 0:Q], in_=ps[psA][0:1, 0:Q], func=AF.Copy,
                    scale=1.0 / D), (f"ps{psA}", "y4"), ("y4", f"ps{psA}"))
                E("scalar", lambda e: e.activation(
                    out=st1[0:1, Q:2 * Q], in_=st1[0:1, 0:Q], func=AF.Square),
                  ("y4",), ("y4",))
                E("scalar", lambda e: e.activation(
                    out=st1[0:1, 2 * Q:3 * Q], in_=ps[psB][0:1, 0:Q],
                    func=AF.Copy, scale=1.0 / D),
                  (f"ps{psB}", "y4"), ("y4", f"ps{psB}"))
                E("vector", lambda e: e.tensor_tensor(
                    out=st1[0:1, 3 * Q:4 * Q], in0=st1[0:1, 2 * Q:3 * Q],
                    in1=st1[0:1, Q:2 * Q], op=AL.subtract), ("y4",), ("y4",))
                E("vector", lambda e: e.tensor_scalar_add(
                    out=st1[0:1, Q:2 * Q], in0=st1[0:1, 3 * Q:4 * Q],
                    scalar1=EPS), ("y4",), ("y4",))
                E("scalar", lambda e: e.activation(
                    out=st1[0:1, 2 * Q:3 * Q], in_=st1[0:1, Q:2 * Q],
                    func=AF.Ln), ("y4",), ("y4",))
                E("scalar", lambda e: e.activation(
                    out=st1[0:1, 3 * Q:4 * Q], in_=st1[0:1, 2 * Q:3 * Q],
                    func=AF.Exp, scale=-0.5), ("y4",), ("y4",))
                E("tensor", lambda e: e.matmul(
                    ps[psC][:, 0:Q], bc1_t[:], st1[0:1, 0:Q],
                    start=True, stop=True, skip_group_check=True),
                  ("y4", "sel16", f"ps{psC}"), (f"ps{psC}",))
                E("tensor", lambda e: e.matmul(
                    ps[psC][:, Q:2 * Q], bc1_t[:], st1[0:1, 3 * Q:4 * Q],
                    start=True, stop=True, skip_group_check=True),
                  ("y4", "sel16", f"ps{psC}"), (f"ps{psC}",))
                E("scalar", lambda e: e.copy(
                    out=mb_t, in_=ps[psC][:, 0:Q]),
                  (f"ps{psC}", "y4"), ("y4",))
                E("scalar", lambda e: e.copy(
                    out=rb_t, in_=ps[psC][:, Q:2 * Q]),
                  (f"ps{psC}", "y4"), ("y4", f"ps{psC}"))
                for dc in range(NDC):
                    dp = dc % 2
                    E("vector", lambda e, dc=dc, dp=dp: e.tensor_tensor(
                        out=tA[dp][:], in0=src_buf[:, dc * Q:(dc + 1) * Q],
                        in1=mb_t, op=AL.subtract),
                      (src_res, "y4", f"tA{dp}"), (f"tA{dp}",))
                    E("vector", lambda e, dc=dc, dp=dp: e.tensor_tensor(
                        out=tB[dp][:], in0=tA[dp][:], in1=rb_t, op=AL.mult),
                      (f"tA{dp}", "y4", f"tB{dp}"), (f"tB{dp}",))
                    out_cb(dc)

            def ln1_out(dc):
                split_pair(tB[dc % 2][:],
                           x1h_buf[:, dc * Q:(dc + 1) * Q],
                           x1l_buf[:, dc * Q:(dc + 1) * Q],
                           (f"tB{dc % 2}",), ("x1pair",))
            layer_norm(res_buf, "res", ln1_out, 4, 5, 6)
            if c == tapc:
                tap("x1", x1h_buf[:], (128, NDC * Q), "x1pair")

            if c == 0:
                for h in range(2):
                    dma(laut[h][:], s_lau[h][:, :], reads=("s_lau",),
                        writes=("laut",))

            for dc in range(NDC):
                mm3(ps[2][0:16, 0:Q],
                    (laut[0][:, dc * 16:(dc + 1) * 16],
                     laut[1][:, dc * 16:(dc + 1) * 16]),
                    (x1h_buf[:, dc * Q:(dc + 1) * Q],
                     x1l_buf[:, dc * Q:(dc + 1) * Q]),
                    first=(dc == 0), last=(dc == NDC - 1),
                    res_r=("laut", "x1pair", "ps2"), res_w=("ps2",))
            E("scalar", lambda e: e.copy(out=tG[0:16, :], in_=ps[2][0:16, 0:Q]),
              ("ps2", "tG"), ("tG", "ps2"))
            split_pair(tG[0:16, :], t1h[:], t1l[:], ("tG",), ("dtipair",))

            for hbk in range(8):
                hpar = hbk % 2
                for jj in range(8):
                    hc = hbk * 8 + jj
                    par = jj % 2
                    (wtiles, wres) = wa_slot(par)
                    for h in range(2):
                        dma(wtiles[h], s_wup[h][hc], reads=("s_wup",),
                            writes=(wres,))
                    pb = ps[par][:, 0:Q]
                    for dc in range(NDC):
                        mm3(pb, (wtiles[0][:, dc * 128:(dc + 1) * 128],
                                 wtiles[1][:, dc * 128:(dc + 1) * 128]),
                            (x1h_buf[:, dc * Q:(dc + 1) * Q],
                             x1l_buf[:, dc * Q:(dc + 1) * Q]),
                            first=(dc == 0), last=False,
                            res_r=(wres, "x1pair", f"ps{par}"),
                            res_w=(f"ps{par}",))
                    for h in range(2):
                        dma(lbut[h][:, par * 128:(par + 1) * 128],
                            s_lbu[h][:, hc * 128:(hc + 1) * 128],
                            reads=("s_lbu",), writes=(f"lbu{par}",))
                    mm3(pb, (lbut[0][:, par * 128:(par + 1) * 128],
                             lbut[1][:, par * 128:(par + 1) * 128]),
                        (t1h[:], t1l[:]), first=False, last=True,
                        res_r=(f"lbu{par}", "dtipair", f"ps{par}"),
                        res_w=(f"ps{par}",))
                    E("scalar", lambda e, pb=pb, hc=hc, par=par: e.activation(
                        out=tA[par][:], in_=pb, func=AF.Relu,
                        bias=bup_t[:, hc:hc + 1]),
                      (f"ps{par}", "bup", f"tA{par}"), (f"tA{par}", f"ps{par}"))
                    o = (hpar * 8 + jj) * Q
                    split_pair(tA[par][:], x2h[:, o:o + Q], x2l[:, o:o + Q],
                               (f"tA{par}",), (f"x2p{hpar}",))
                    for h in range(2):
                        dma(ladt[h][:, par * 16:(par + 1) * 16],
                            s_lad[h][:, hc * 16:(hc + 1) * 16],
                            reads=("s_lad",), writes=(f"lad{par}",))
                    mm3(ps[3][0:16, 0:Q],
                        (ladt[0][:, par * 16:(par + 1) * 16],
                         ladt[1][:, par * 16:(par + 1) * 16]),
                        (x2h[:, o:o + Q], x2l[:, o:o + Q]),
                        first=(hc == 0), last=(hc == NHC - 1),
                        res_r=(f"lad{par}", f"x2p{hpar}", "ps3"), res_w=("ps3",))
                for dc in range(NDC):
                    par = dc % 2
                    (wtiles, wres) = wb_slot(par)
                    for h in range(2):
                        dma(wtiles[h],
                            s_wdn[h][dc][:, hbk * 8 * 128:(hbk + 1) * 8 * 128],
                            reads=("s_wdn",), writes=(wres,))
                    pb = ps[4 + par][:, 0:Q]
                    for jj in range(8):
                        o = (hpar * 8 + jj) * Q
                        mm3(pb, (wtiles[0][:, jj * 128:(jj + 1) * 128],
                                 wtiles[1][:, jj * 128:(jj + 1) * 128]),
                            (x2h[:, o:o + Q], x2l[:, o:o + Q]),
                            first=(jj == 0), last=(jj == 7 and hbk != 7),
                            res_r=(wres, f"x2p{hpar}", f"ps{4 + par}"),
                            res_w=(f"ps{4 + par}",))
                    if hbk == 7:
                        if dc == 0:
                            E("scalar", lambda e: e.copy(
                                out=tG[0:16, :], in_=ps[3][0:16, 0:Q]),
                              ("ps3", "tG"), ("tG", "ps3"))
                            split_pair(tG[0:16, :], t2h[:], t2l[:], ("tG",),
                                       ("dtipair",))
                        for h in range(2):
                            dma(lbdt[h][:, par * 128:(par + 1) * 128],
                                s_lbd[h][:, dc * 128:(dc + 1) * 128],
                                reads=("s_lbd",), writes=(f"lbu{par}",))
                        mm3(pb, (lbdt[0][:, par * 128:(par + 1) * 128],
                                 lbdt[1][:, par * 128:(par + 1) * 128]),
                            (t2h[:], t2l[:]), first=False, last=True,
                            res_r=(f"lbu{par}", "dtipair", f"ps{4 + par}"),
                            res_w=(f"ps{4 + par}",))
                    cur = hbk % 2
                    prv = 1 - cur
                    dst = u_buf[:, (cur * NDC + dc) * Q:(cur * NDC + dc + 1) * Q]
                    srcp = u_buf[:, (prv * NDC + dc) * Q:(prv * NDC + dc + 1) * Q]
                    if hbk == 0:
                        E("scalar", lambda e, pb=pb, d=dst: e.copy(
                            out=d, in_=pb),
                          (f"ps{4 + par}", "u"), ("u", f"ps{4 + par}"))
                    else:
                        E("vector", lambda e, pb=pb, d=dst, s=srcp:
                          e.tensor_tensor(out=d, in0=pb, in1=s, op=AL.add),
                          (f"ps{4 + par}", "u"), ("u", f"ps{4 + par}"))
            for dc in range(NDC):
                y2fin = u_buf[:, (NDC + dc) * Q:(NDC + dc + 1) * Q]
                E("vector", lambda e, s=y2fin, dc=dc: e.tensor_scalar_add(
                    out=tA[0][:], in0=s, scalar1=bdn_t[:, dc:dc + 1]),
                  ("u", "bdn", "tA0"), ("tA0",))
                E("vector", lambda e, dc=dc: e.tensor_tensor(
                    out=tB[1][:], in0=tA[0][:],
                    in1=x1h_buf[:, dc * Q:(dc + 1) * Q], op=AL.add),
                  ("tA0", "x1pair", "tB1"), ("tB1",))
                E("vector", lambda e, dc=dc: e.tensor_tensor(
                    out=res_buf[:, dc * Q:(dc + 1) * Q], in0=tB[1][:],
                    in1=x1l_buf[:, dc * Q:(dc + 1) * Q], op=AL.add),
                  ("tB1", "x1pair"), ("res",))
            if c == tapc:
                tap("ffnres", res_buf[:], (128, NDC * Q), "res")

            def ln2_out(dc, c=c):
                E("sync", lambda e, dc=dc, c=c: e.dma_start(
                    out=out_t[dc * 128:(dc + 1) * 128, c * Q:(c + 1) * Q],
                    in_=tB[dc % 2][:]), (f"tB{dc % 2}",), ("outdma",))
            layer_norm(res_buf, "res", ln2_out, 4, 5, 6)

        A.final_waits = ["sync", "gpsimd"]
        A.assemble(nc, block)

    return nc, tap_names


def prep_inputs(inputs):
    f32 = np.float32
    g = {k: np.asarray(v) for k, v in inputs.items()}
    x = g["L0_attn_input_states"]
    base = {}
    base["w_in"] = np.ascontiguousarray(g["L0_attn_in_proj_w"], f32)
    cw = np.asarray(g["L0_attn_conv_w"], f32).reshape(IN, 4)
    base["conv_w"] = np.ascontiguousarray(
        cw.reshape(NIC, 128, 4).transpose(1, 0, 2).reshape(128, NIC * 4))
    base["conv_b"] = np.ascontiguousarray(
        np.asarray(g["L0_attn_conv_b"], f32).reshape(NIC, 128).T)
    xp = np.asarray(g["L0_attn_x_proj_w"], f32)
    base["xp_dt"] = np.ascontiguousarray(xp[:, 0:R])
    base["xp_b"] = np.ascontiguousarray(xp[:, R:R + NS])
    base["xp_c"] = np.ascontiguousarray(xp[:, R + NS:R + 2 * NS])
    base["dtp_w"] = np.ascontiguousarray(g["L0_attn_dt_proj_w"], f32)
    base["dtp_b"] = np.ascontiguousarray(
        np.asarray(g["L0_attn_dt_proj_b"], f32).reshape(NIC, 128).T)
    base["a_log"] = np.ascontiguousarray(
        np.asarray(g["L0_attn_A_log"], f32).reshape(NIC, 128, NS)
        .transpose(1, 0, 2).reshape(128, NIC * NS))
    base["d_vec"] = np.ascontiguousarray(
        np.asarray(g["L0_attn_D"], f32).reshape(NIC, 128).T)
    base["op_w"] = np.ascontiguousarray(g["L0_attn_out_proj_w"], f32)
    base["wup_q"] = np.ascontiguousarray(np.asarray(g["L0_ffn_w_up_q"]).T)
    base["wup_s"] = np.ascontiguousarray(np.asarray(g["L0_ffn_w_up_scale"], f32).T)
    base["b_up"] = np.ascontiguousarray(
        np.asarray(g["L0_ffn_b_up"], f32).reshape(NHC, 128).T)
    base["lau"] = np.ascontiguousarray(g["L0_ffn_w_up_lora_a"], f32)
    base["lbu"] = np.ascontiguousarray(g["L0_ffn_w_up_lora_b"], f32)
    base["wdn_q"] = np.ascontiguousarray(np.asarray(g["L0_ffn_w_down_q"]).T)
    base["wdn_s"] = np.ascontiguousarray(
        np.asarray(g["L0_ffn_w_down_scale"], f32).T)
    base["b_dn"] = np.ascontiguousarray(
        np.asarray(g["L0_ffn_b_down"], f32).reshape(NDC, 128).T)
    base["lad"] = np.ascontiguousarray(g["L0_ffn_w_down_lora_a"], f32)
    base["lbd"] = np.ascontiguousarray(g["L0_ffn_w_down_lora_b"], f32)
    sel16v = np.zeros((NS, NS * 128), f32)
    for n in range(NS):
        sel16v[n, n * 128:(n + 1) * 128] = 1.0
    base["sel16"] = sel16v
    sel2v = np.zeros((2, 128), f32)
    sel2v[0, 0:64] = 1.0
    sel2v[1, 64:128] = 1.0
    base["sel2"] = sel2v
    base["ones_c"] = np.ones((128, 1), f32)

    in_maps = []
    for cid in range(NCORES):
        m = dict(base)
        m["x_t"] = np.ascontiguousarray(x[cid].T.astype(f32))
        in_maps.append(m)
    return in_maps


def kernel(**inputs):
    in_maps = prep_inputs(inputs)
    nc, _ = build_program()
    res = run_bass_kernel_spmd(nc, in_maps, core_ids=list(range(NCORES)))
    out = np.stack([np.asarray(r["out_t"]).T for r in res.results])
    return np.ascontiguousarray(out).astype(np.float32)


# version 07
# speedup vs baseline: 1.1746x; 1.0018x over previous
import numpy as np
from contextlib import ExitStack
import concourse.bass as bass
import concourse.mybir as mybir
from concourse.bass_utils import run_bass_kernel_spmd

F32 = mybir.dt.float32
BF16 = mybir.dt.bfloat16
I32 = mybir.dt.int32
AL = mybir.AluOpType
AF = mybir.ActivationFunctionType
AX = mybir.AxisListType

T, D, IN, NS, R, HF = 2048, 2048, 4096, 16, 128, 8192
Q = 256
NCH = T // Q
NDC = D // 128
NIC = IN // 128
NHC = HF // 128
NG = 4
GN = NS // NG
EPS = 1e-5
NCORES = 8

ENGS = ("sync", "tensor", "vector", "scalar", "gpsimd")
DMA_ENGS = {"sync"}


class Asm:

    def __init__(self):
        self.ops = []
        self.final_waits = []

    def emit(self, eng, build, reads=(), writes=(), sfence=False, is_dma=False):
        self.ops.append([eng, build, tuple(reads), tuple(writes), False, sfence,
                         is_dma or eng in DMA_ENGS])

    def assemble(self, nc, block):
        ops = self.ops
        n = len(ops)
        last_w = {}
        readers = {}
        deps = [None] * n
        for i, (eng, _b, rd, wr, _f, sfence, _d) in enumerate(ops):
            dd = set()
            for r in rd:
                j = last_w.get(r)
                if j is not None and (ops[j][0] != eng or sfence):
                    dd.add(j)
            for r in wr:
                j = last_w.get(r)
                if j is not None and (ops[j][0] != eng or sfence):
                    dd.add(j)
                for j in readers.get(r, ()):
                    if ops[j][0] != eng or sfence:
                        dd.add(j)
            for r in rd:
                readers.setdefault(r, []).append(i)
            for r in wr:
                last_w[r] = i
                readers[r] = []
            deps[i] = dd
        for i in range(n):
            for j in deps[i]:
                ops[j][4] = True
        def key_of(i):
            eng, _b, _r, wr, _m, _sf, isd = ops[i]
            if isd:
                assert wr, f"dma op {i} needs a write resource"
                return "d:" + wr[0]
            return eng
        for op in ops:
            if op[6]:
                op[4] = True
        last_of = {}
        for i, op in enumerate(ops):
            last_of[op[0]] = i
        for e in ENGS:
            if e in last_of:
                ops[last_of[e]][4] = True
        kcount = {}
        inc_val = [0] * n
        for i in range(n):
            k = key_of(i)
            if ops[i][4]:
                kcount[k] = kcount.get(k, 0) + (16 if ops[i][6] else 1)
            inc_val[i] = kcount.get(k, 0)
        need_keys = [None] * n
        for i in range(n):
            ks = set()
            for j in deps[i]:
                ks.add(key_of(j))
            need_keys[i] = ks
        run_cnt = {}
        wait_val = [None] * n
        for i in range(n):
            if need_keys[i]:
                wv = {}
                for j in deps[i]:
                    k = key_of(j)
                    if k.startswith("d:"):
                        wv[k] = run_cnt.get(k, 0)
                    else:
                        wv[k] = max(wv.get(k, 0), inc_val[j])
                wait_val[i] = wv
            k = key_of(i)
            if ops[i][4]:
                run_cnt[k] = run_cnt.get(k, 0) + (16 if ops[i][6] else 1)
        finals = dict(run_cnt)
        all_keys = sorted(finals)
        sems = {k: nc.alloc_semaphore(f"s_{k.replace(':', '_')}")
                for k in all_keys}
        watermark = {e: {} for e in ENGS}

        def run_engine(e_obj, eng):
            wm = watermark[eng]
            for i in range(n):
                oe, build, _r, _w, marked, _sf, isd = ops[i]
                if oe != eng:
                    continue
                if wait_val[i]:
                    for k in sorted(wait_val[i]):
                        v = wait_val[i][k]
                        if v > wm.get(k, 0):
                            e_obj.wait_ge(sems[k], v)
                            wm[k] = v
                ins = build(e_obj)
                if marked:
                    ins.then_inc(sems[key_of(i)], 16 if isd else 1)
            if eng in self.final_waits:
                for k in all_keys:
                    if k != eng and finals[k] > watermark[eng].get(k, 0):
                        e_obj.wait_ge(sems[k], finals[k])
                        watermark[eng][k] = finals[k]

        for eng, starter in (("sync", block.sync), ("tensor", block.tensor),
                             ("vector", block.vector), ("scalar", block.scalar),
                             ("gpsimd", block.gpsimd)):
            starter(lambda e, eng=eng: run_engine(e, eng))


def build_program(taps=(), nch=NCH, tapc=0):
    nc = bass.Bass("TRN2", target_bir_lowering=False,
                   detect_race_conditions=False)
    A = Asm()
    tap_names = []

    def din(name, shape, dt=F32):
        return nc.dram_tensor(name, list(shape), dt, kind="ExternalInput").ap()

    x_t = din("x_t", (D, T))
    w_in = din("w_in", (D, HF))
    conv_w = din("conv_w", (128, NIC * 4))
    conv_b = din("conv_b", (128, NIC))
    xp_dt = din("xp_dt", (IN, R))
    xp_b = din("xp_b", (IN, NS))
    xp_c = din("xp_c", (IN, NS))
    dtp_w = din("dtp_w", (R, IN))
    dtp_b = din("dtp_b", (128, NIC))
    a_log = din("a_log", (128, NIC * NS))
    d_vec = din("d_vec", (128, NIC))
    op_w = din("op_w", (IN, D))
    wup_q = din("wup_q", (D, HF), I32)
    wup_s = din("wup_s", (2 * NDC, HF))
    b_up = din("b_up", (128, NHC))
    lau = din("lau", (D, 16))
    lbu = din("lbu", (16, HF))
    wdn_q = din("wdn_q", (HF, D), I32)
    wdn_s = din("wdn_s", (2 * NHC, D))
    b_dn = din("b_dn", (128, NDC))
    lad = din("lad", (HF, 16))
    lbd = din("lbd", (16, D))
    sel16 = din("sel16", (NS, NS * 128))
    sel2 = din("sel2", (2, 128))
    ones_c = din("ones_c", (128, 1))

    out_t = nc.dram_tensor("out_t", [D, T], F32, kind="ExternalOutput").ap()

    def scr(name, shape, dt=BF16):
        return nc.dram_tensor(name, list(shape), dt, kind="Internal").ap()

    s_win = [scr("s_win_h", (2 * NIC, 128, NDC * 128)),
             scr("s_win_l", (2 * NIC, 128, NDC * 128))]
    s_opw = [scr("s_opw_h", (NIC, 128, NDC * 128)),
             scr("s_opw_l", (NIC, 128, NDC * 128))]
    s_dtp = [scr("s_dtp_h", (NIC, 128, 128)), scr("s_dtp_l", (NIC, 128, 128))]
    s_xp = [scr("s_xp_h", (NIC, 128, R + 2 * NS)),
            scr("s_xp_l", (NIC, 128, R + 2 * NS))]
    s_wup = [scr("s_wup_h", (NHC, 128, NDC * 128)),
             scr("s_wup_l", (NHC, 128, NDC * 128))]
    s_wdn = [scr("s_wdn_h", (NDC, 128, NHC * 128)),
             scr("s_wdn_l", (NDC, 128, NHC * 128))]
    s_lau = [scr("s_lau_h", (128, NDC * 16)), scr("s_lau_l", (128, NDC * 16))]
    s_lbu = [scr("s_lbu_h", (16, HF)), scr("s_lbu_l", (16, HF))]
    s_lad = [scr("s_lad_h", (128, NHC * 16)), scr("s_lad_l", (128, NHC * 16))]
    s_lbd = [scr("s_lbd_h", (16, D)), scr("s_lbd_l", (16, D))]

    es = ExitStack()

    def sbt(name, shape, dt=F32):
        return es.enter_context(nc.sbuf_tensor(name, list(shape), dt))

    with es:
        u_buf = sbt("u_buf", (128, NIC * Q))
        bb_buf = sbt("bb_buf", (128, NS * Q))
        cb_buf = sbt("cb_buf", (128, NS * Q))
        scanh = sbt("scanh", (128, GN * Q))
        tmpP = sbt("tmpP", (128, GN * Q))
        tmpQ = sbt("tmpQ", (128, GN * Q))
        res_buf = sbt("res_buf", (128, NDC * Q))
        sc0p = [res_buf[:, 0:GN * Q], res_buf[:, GN * Q:2 * GN * Q]]
        sc1p = [res_buf[:, 2 * GN * Q:3 * GN * Q],
                res_buf[:, 3 * GN * Q:4 * GN * Q]]
        dtt = [tmpP[:, 0:Q], tmpP[:, Q:2 * Q]]
        dtut = [tmpP[:, 2 * Q:3 * Q], tmpP[:, 3 * Q:4 * Q]]
        get_ = [tmpQ[:, 0:Q], tmpQ[:, Q:2 * Q]]
        ggt = [tmpQ[:, 2 * Q:3 * Q], tmpQ[:, 3 * Q:4 * Q]]
        y4 = sbt("y4", (128, NG * Q))
        st1 = y4[0:1, :]
        mb_t = y4[:, 0:Q]
        rb_t = y4[:, Q:2 * Q]
        hidt = [sbt(f"hid{i}", (128, Q + 4)) for i in range(2)]
        tA = [sbt(f"tA{i}", (128, Q)) for i in range(2)]
        tB = [sbt(f"tB{i}", (128, Q)) for i in range(2)]
        tC = [sbt(f"tC{i}", (128, Q)) for i in range(2)]
        tD = sbt("tD", (128, Q))
        tE = sbt("tE", (128, Q))
        tF = sbt("tF", (128, Q))
        tG = sbt("tG", (128, Q))
        tH = sbt("tH", (128, Q))
        tt1 = [tD, tE]
        tt2 = [tF, tH]
        a_buf = sbt("a_buf", (128, NIC * NS))
        state = sbt("state", (128, NIC * NS))
        tails = sbt("tails", (128, NIC * 4))
        sel16_t = sbt("sel16_t", (NS, NS * 128))
        sel2_t = sbt("sel2_t", (2, 128))
        bc1_t = sel16_t[0:1, 0:128]
        ones_t = sbt("ones_t", (128, 1))
        convw_t = sbt("convw_t", (128, NIC * 4))
        convb_t = sbt("convb_t", (128, NIC))
        dtpb_t = sbt("dtpb_t", (128, NIC))
        dvec_t = sbt("dvec_t", (128, NIC))
        bup_t = sbt("bup_t", (128, NHC))
        bdn_t = sbt("bdn_t", (128, NDC))
        bt_t = sbt("bt_t", (NS, Q))
        ct_t = sbt("ct_t", (NS, Q))
        xst = sbt("xst", (128, 2 * Q))
        xh_buf = sbt("xh_buf", (128, NDC * Q), BF16)
        xl_buf = sbt("xl_buf", (128, NDC * Q), BF16)
        x1h_buf = sbt("x1h_buf", (128, NDC * Q), BF16)
        x1l_buf = sbt("x1l_buf", (128, NDC * Q), BF16)
        x2h = sbt("x2h", (128, 16 * Q), BF16)
        x2l = sbt("x2l", (128, 16 * Q), BF16)
        wa_h = sbt("wa_h", (128, 2 * NDC * 128), BF16)
        wa_l = sbt("wa_l", (128, 2 * NDC * 128), BF16)
        wb_h = sbt("wb_h", (128, 2 * 8 * 128), BF16)
        wb_l = sbt("wb_l", (128, 2 * 8 * 128), BF16)
        dtih = sbt("dtih", (128, Q), BF16)
        dtil = sbt("dtil", (128, Q), BF16)
        uht = [sbt(f"uh{i}", (128, Q), BF16) for i in range(2)]
        ult = [sbt(f"ul{i}", (128, Q), BF16) for i in range(2)]
        yph = uht
        ypl = ult
        t1h = dtih[0:16, :]
        t1l = dtil[0:16, :]
        t2h = dtih[0:16, :]
        t2l = dtil[0:16, :]
        laut = [sbt("laut_h", (128, NDC * 16), BF16),
                sbt("laut_l", (128, NDC * 16), BF16)]
        lbut = [sbt("lbut_h", (16, 2 * 128), BF16),
                sbt("lbut_l", (16, 2 * 128), BF16)]
        ladt = [sbt("ladt_h", (128, 2 * 16), BF16),
                sbt("ladt_l", (128, 2 * 16), BF16)]
        lbdt = lbut
        pp_f = [res_buf[:, 0:2048], res_buf[:, 2048:4096]]
        pp_h = [x2h[:, 0:2048], x2h[:, 2048:4096]]
        pp_l = [x2l[:, 0:2048], x2l[:, 2048:4096]]
        pp_s = [xst[:, 0:512], xst[:, 0:512]]
        pp_q0 = sbt("pp_q0", (128, 512), I32)
        pp_q = [pp_q0, pp_q0]
        sc_r = [bb_buf[0:2, 0:1024], cb_buf[0:2, 0:1024]]
        ppfr = ["res", "res1"]
        pphr = ["x2p0", "x2p1"]
        ppsr = ["xst0", "xst0"]
        ppqr = ["ppq0", "ppq0"]
        scrr = ["bb", "cb"]

        ps = [es.enter_context(nc.psum_tensor(f"pb{i}", [128, 512], F32))
              for i in range(8)]

        block = es.enter_context(nc.Block())

        def E(eng, build, reads=(), writes=(), sfence=False, is_dma=False):
            A.emit(eng, build, reads, writes, sfence, is_dma)

        def dma(dst, src, reads=(), writes=(), q="sync"):
            E(q, lambda e, d=dst, s=src: e.dma_start(out=d, in_=s),
              reads, writes, is_dma=True)

        def tap(name, ap, shape, res):
            if name not in taps:
                return
            h = nc.dram_tensor("tap_" + name, list(shape), ap.dtype,
                               kind="ExternalOutput").ap()
            tap_names.append(name)
            dma(h[:], ap, reads=(res,), writes=("tapdma",))

        def mm3(psum_ap, lhs, rhs, first, last, res_r, res_w, skip=True):
            lh, ll = lhs
            rh, rl = rhs
            E("tensor", lambda e, a=psum_ap, x=lh, y=rh, st=first:
              e.matmul(a, x, y, start=st, stop=False, skip_group_check=skip),
              res_r, res_w if not last else ())
            E("tensor", lambda e, a=psum_ap, x=lh, y=rl:
              e.matmul(a, x, y, start=False, stop=False, skip_group_check=skip),
              res_r, ())
            E("tensor", lambda e, a=psum_ap, x=ll, y=rh, sp=last:
              e.matmul(a, x, y, start=False, stop=sp, skip_group_check=skip),
              res_r, res_w if last else ())

        def split_pair(src_ap, dst_h, dst_l, rd, wr):
            E("vector", lambda e, s=src_ap, d=dst_h: e.tensor_copy(out=d, in_=s),
              rd, wr)
            E("vector", lambda e, s=src_ap, h=dst_h, d=dst_l: e.tensor_tensor(
                out=d, in0=s, in1=h, op=AL.subtract), rd + wr, wr)

        for dst, src, tg in ((sel16_t, sel16, "sel16"), (sel2_t, sel2, "sel2"),
                             (ones_t, ones_c, "ones"),
                             (convw_t, conv_w, "convw"), (convb_t, conv_b, "convb"),
                             (dtpb_t, dtp_b, "dtpb"), (dvec_t, d_vec, "dvec"),
                             (bup_t, b_up, "bup"), (bdn_t, b_dn, "bdn")):
            dma(dst[:], src[:], writes=(tg,))

        pp_ctr = [0]

        def pp_next():
            p = pp_ctr[0] & 1
            pp_ctr[0] += 1
            return p, ("sync", "scalar")[p]

        def pp_split(p, cols, rows=128):
            split_pair(pp_f[p][0:rows, 0:cols], pp_h[p][0:rows, 0:cols],
                       pp_l[p][0:rows, 0:cols], (ppfr[p],), (pphr[p],))

        for dc in range(NDC):
            for piece in range(4):
                p, q = pp_next()
                dma(pp_f[p][:, 0:2048],
                    w_in[dc * 128:(dc + 1) * 128, piece * 2048:(piece + 1) * 2048],
                    writes=(ppfr[p],), q=q)
                pp_split(p, 2048)
                for jj in range(16):
                    jc = piece * 16 + jj
                    for h in range(2):
                        dma(s_win[h][jc][:, dc * 128:(dc + 1) * 128],
                            [pp_h, pp_l][h][p][:, jj * 128:(jj + 1) * 128],
                            reads=(pphr[p],), writes=("s_win",), q=q)
        for ic in range(NIC):
            p, q = pp_next()
            dma(pp_f[p][:, 0:2048], op_w[ic * 128:(ic + 1) * 128, :],
                writes=(ppfr[p],), q=q)
            pp_split(p, 2048)
            for h in range(2):
                dma(s_opw[h][ic], [pp_h, pp_l][h][p][:, 0:2048],
                    reads=(pphr[p],), writes=("s_opw",), q=q)
        for piece in range(2):
            p, q = pp_next()
            dma(pp_f[p][:, 0:2048], dtp_w[:, piece * 2048:(piece + 1) * 2048],
                writes=(ppfr[p],), q=q)
            pp_split(p, 2048)
            for jj in range(16):
                ic = piece * 16 + jj
                for h in range(2):
                    dma(s_dtp[h][ic],
                        [pp_h, pp_l][h][p][:, jj * 128:(jj + 1) * 128],
                        reads=(pphr[p],), writes=("s_dtp",), q=q)
        for ic in range(NIC):
            p, q = pp_next()
            dma(pp_f[p][:, 0:R], xp_dt[ic * 128:(ic + 1) * 128, :],
                writes=(ppfr[p],), q=q)
            dma(pp_f[p][:, R:R + NS], xp_b[ic * 128:(ic + 1) * 128, :],
                writes=(ppfr[p],), q=q)
            dma(pp_f[p][:, R + NS:R + 2 * NS], xp_c[ic * 128:(ic + 1) * 128, :],
                writes=(ppfr[p],), q=q)
            pp_split(p, R + 2 * NS)
            for h in range(2):
                dma(s_xp[h][ic], [pp_h, pp_l][h][p][:, 0:R + 2 * NS],
                    reads=(pphr[p],), writes=("s_xp",), q=q)
        p, q = pp_next()
        for dc in range(NDC):
            dma(pp_f[p][:, dc * 16:(dc + 1) * 16],
                lau[dc * 128:(dc + 1) * 128, :], writes=(ppfr[p],), q=q)
        pp_split(p, NDC * 16)
        for h in range(2):
            dma(s_lau[h][:, :], [pp_h, pp_l][h][p][:, 0:NDC * 16],
                reads=(pphr[p],), writes=("s_lau",), q=q)
        for half in range(2):
            p, q = pp_next()
            for j in range(NHC // 2):
                hc = half * (NHC // 2) + j
                dma(pp_f[p][:, j * 16:(j + 1) * 16],
                    lad[hc * 128:(hc + 1) * 128, :], writes=(ppfr[p],), q=q)
            pp_split(p, (NHC // 2) * 16)
            for h in range(2):
                dma(s_lad[h][:, half * (NHC // 2) * 16:
                             (half + 1) * (NHC // 2) * 16],
                    [pp_h, pp_l][h][p][:, 0:(NHC // 2) * 16],
                    reads=(pphr[p],), writes=("s_lad",), q=q)
        for piece in range(4):
            p, q = pp_next()
            dma(pp_f[p][0:16, 0:2048], lbu[:, piece * 2048:(piece + 1) * 2048],
                writes=(ppfr[p],), q=q)
            pp_split(p, 2048, rows=16)
            for h in range(2):
                dma(s_lbu[h][:, piece * 2048:(piece + 1) * 2048],
                    [pp_h, pp_l][h][p][0:16, 0:2048],
                    reads=(pphr[p],), writes=("s_lbu",), q=q)
        p, q = pp_next()
        dma(pp_f[p][0:16, 0:2048], lbd[:, :], writes=(ppfr[p],), q=q)
        pp_split(p, 2048, rows=16)
        for h in range(2):
            dma(s_lbd[h][:, :], [pp_h, pp_l][h][p][0:16, 0:2048],
                reads=(pphr[p],), writes=("s_lbd",), q=q)

        def dequant(qsrc, ssrc, dst, n_stripes, n_cols, tile_res):
            for si in range(n_stripes):
                for half in range(n_cols // 512):
                    c0 = half * 512
                    p, q = pp_next()
                    dma(sc_r[p][:, 0:512],
                        ssrc[2 * si:2 * si + 2, c0:c0 + 512],
                        writes=(scrr[p],), q=q)
                    dma(pp_q[p][:, 0:512],
                        qsrc[si * 128:(si + 1) * 128, c0:c0 + 512],
                        writes=(ppqr[p],), q=q)
                    E("tensor", lambda e, p=p: e.matmul(
                        ps[6 + p][:, 0:512], sel2_t[:], sc_r[p][:, 0:512],
                        start=True, stop=True, skip_group_check=True),
                      (scrr[p], "sel2", f"ps{6 + p}"), (f"ps{6 + p}",))
                    E("vector", lambda e, p=p: e.scalar_tensor_tensor(
                        out=pp_f[p][:, 0:512], in0=pp_q[p][:, 0:512],
                        scalar=-7.5, in1=ps[6 + p][:, 0:512],
                        op0=AL.add, op1=AL.mult),
                      (ppqr[p], f"ps{6 + p}"), (ppfr[p], f"ps{6 + p}"))
                    pp_split(p, 512)
                    for jj in range(4):
                        j = half * 4 + jj
                        for h in range(2):
                            dma(dst[h][j][:, si * 128:(si + 1) * 128],
                                [pp_h, pp_l][h][p][:, jj * 128:(jj + 1) * 128],
                                reads=(pphr[p],), writes=(tile_res,), q=q)
        dequant(wup_q, wup_s, s_wup, NDC, HF, "s_wup")
        dequant(wdn_q, wdn_s, s_wdn, NHC, D, "s_wdn")

        p, q = pp_next()
        dma(pp_f[p][:, 0:NIC * NS], a_log[:], writes=(ppfr[p],), q=q)
        E("scalar", lambda e, p=p: e.activation(out=pp_s[p][:, 0:NIC * NS],
                                                in_=pp_f[p][:, 0:NIC * NS],
                                                func=AF.Exp),
          (ppfr[p],), (ppsr[p],))
        E("vector", lambda e, p=p: e.tensor_scalar_mul(
            out=a_buf[:], in0=pp_s[p][:, 0:NIC * NS], scalar1=-1.0),
          (ppsr[p],), ("abuf",))
        E("vector", lambda e: e.memset(state[:], 0.0), (), ("state",))
        E("vector", lambda e: e.memset(tails[:], 0.0), (), ("tails",))
        E("vector", lambda e: e.engine_nop(),
          ("res", "res1", "x2p0", "x2p1", "bb", "cb", "xst0", "xst1"),
          ("res", "res1", "x2p0", "x2p1", "bb", "cb", "xst0", "xst1"))

        def wa_slot(par):
            o = par * NDC * 128
            return ((wa_h[:, o:o + NDC * 128], wa_l[:, o:o + NDC * 128]),
                    f"wa{par}")

        def wb_slot(par, cols=8 * 128):
            o = par * 8 * 128
            return ((wb_h[:, o:o + cols], wb_l[:, o:o + cols]), f"wb{par}")

        for c in range(nch):
            tc0 = c * Q

            for dc in range(NDC):
                xp_ = dc % 2
                dma(xst[:, xp_ * Q:(xp_ + 1) * Q],
                    x_t[dc * 128:(dc + 1) * 128, tc0:tc0 + Q],
                    writes=(f"xst{xp_}",))
                split_pair(xst[:, xp_ * Q:(xp_ + 1) * Q],
                           xh_buf[:, dc * Q:(dc + 1) * Q],
                           xl_buf[:, dc * Q:(dc + 1) * Q],
                           (f"xst{xp_}",), ("xpair",))

            for ic in range(NIC):
                par = ic % 2
                (wtiles, wres) = wa_slot(par)
                for h in range(2):
                    dma(wtiles[h], s_win[h][ic], reads=("s_win",), writes=(wres,))
                pb = ps[par][:, 0:Q]
                for dc in range(NDC):
                    mm3(pb, (wtiles[0][:, dc * 128:(dc + 1) * 128],
                             wtiles[1][:, dc * 128:(dc + 1) * 128]),
                        (xh_buf[:, dc * Q:(dc + 1) * Q],
                         xl_buf[:, dc * Q:(dc + 1) * Q]),
                        first=(dc == 0), last=(dc == NDC - 1),
                        res_r=(wres, "xpair", f"ps{par}"), res_w=(f"ps{par}",))
                hb = hidt[par]
                E("scalar", lambda e, ic=ic, hb=hb: e.copy(
                    out=hb[:, 0:3], in_=tails[:, ic * 4:ic * 4 + 3]),
                  ("tails",), (f"hid{par}",))
                E("scalar", lambda e, pb=pb, hb=hb: e.copy(
                    out=hb[:, 3:3 + Q], in_=pb),
                  (f"ps{par}",), (f"hid{par}", f"ps{par}"))
                E("scalar", lambda e, ic=ic, hb=hb: e.copy(
                    out=tails[:, ic * 4:ic * 4 + 3], in_=hb[:, Q:Q + 3]),
                  (f"hid{par}",), ("tails",))
                E("vector", lambda e, ic=ic, hb=hb, par=par: e.tensor_scalar(
                    out=tB[par][:], in0=hb[:, 0:Q],
                    scalar1=convw_t[:, ic * 4:ic * 4 + 1],
                    scalar2=convb_t[:, ic:ic + 1], op0=AL.mult, op1=AL.add),
                  (f"hid{par}", "convw", "convb", f"tB{par}"), (f"tB{par}",))
                E("vector", lambda e, ic=ic, hb=hb, par=par:
                  e.scalar_tensor_tensor(
                      out=tC[par][:], in0=hb[:, 1:1 + Q],
                      scalar=convw_t[:, ic * 4 + 1:ic * 4 + 2], in1=tB[par][:],
                      op0=AL.mult, op1=AL.add),
                  (f"hid{par}", f"tB{par}", "convw", f"tC{par}"), (f"tC{par}",))
                E("vector", lambda e, ic=ic, hb=hb, par=par:
                  e.scalar_tensor_tensor(
                      out=tB[par][:], in0=hb[:, 2:2 + Q],
                      scalar=convw_t[:, ic * 4 + 2:ic * 4 + 3], in1=tC[par][:],
                      op0=AL.mult, op1=AL.add),
                  (f"hid{par}", f"tC{par}", "convw", f"tB{par}"), (f"tB{par}",))
                E("vector", lambda e, ic=ic, hb=hb, par=par:
                  e.scalar_tensor_tensor(
                      out=tC[par][:], in0=hb[:, 3:3 + Q],
                      scalar=convw_t[:, ic * 4 + 3:ic * 4 + 4], in1=tB[par][:],
                      op0=AL.mult, op1=AL.add),
                  (f"hid{par}", f"tB{par}", "convw", f"tC{par}"), (f"tC{par}",))
                E("scalar", lambda e, ic=ic, par=par: e.activation(
                    out=u_buf[:, ic * Q:(ic + 1) * Q], in_=tC[par][:],
                    func=AF.Silu), (f"tC{par}",), ("u",))
            if c == tapc:
                tap("u", u_buf[:], (128, NIC * Q), "u")

            for ic in range(NIC):
                par = ic % 2
                (wtiles, wres) = wb_slot(par, cols=R + 2 * NS)
                for h in range(2):
                    dma(wtiles[h], s_xp[h][ic], reads=("s_xp",), writes=(wres,))
                split_pair(u_buf[:, ic * Q:(ic + 1) * Q],
                           uht[par][:], ult[par][:], ("u",), (f"upair{par}",))
                upair = (uht[par][:], ult[par][:])
                mm3(ps[2][:, 0:Q], (wtiles[0][:, 0:R], wtiles[1][:, 0:R]),
                    upair, first=(ic == 0), last=(ic == NIC - 1),
                    res_r=(wres, f"upair{par}", "ps2"), res_w=("ps2",))
                mm3(ps[3][0:NS, 0:Q],
                    (wtiles[0][:, R:R + NS], wtiles[1][:, R:R + NS]),
                    upair, first=(ic == 0), last=(ic == NIC - 1),
                    res_r=(wres, f"upair{par}", "ps3"), res_w=("ps3",))
                mm3(ps[4][0:NS, 0:Q],
                    (wtiles[0][:, R + NS:R + 2 * NS],
                     wtiles[1][:, R + NS:R + 2 * NS]),
                    upair, first=(ic == 0), last=(ic == NIC - 1),
                    res_r=(wres, f"upair{par}", "ps4"), res_w=("ps4",))
            E("scalar", lambda e: e.copy(out=tG[:], in_=ps[2][:, 0:Q]),
              ("ps2",), ("tG", "ps2"))
            E("scalar", lambda e: e.copy(out=bt_t[:], in_=ps[3][0:NS, 0:Q]),
              ("ps3",), ("btct", "ps3"))
            E("scalar", lambda e: e.copy(out=ct_t[:], in_=ps[4][0:NS, 0:Q]),
              ("ps4",), ("btct", "ps4"))
            if c == tapc:
                tap("dtin", tG[:], (128, Q), "tG")
                tap("bt", bt_t[:], (NS, Q), "btct")
                tap("ct", ct_t[:], (NS, Q), "btct")
            split_pair(tG[:], dtih[:], dtil[:], ("tG",), ("dtipair",))

            for which, src_t, dst_buf, dres in ((0, bt_t, bb_buf, "bb"),
                                                (1, ct_t, cb_buf, "cb")):
                for n in range(NS):
                    p2 = n % 2
                    g, j = n // GN, n % GN
                    E("tensor", lambda e, n=n, p2=p2, s=src_t: e.matmul(
                        ps[5 + p2][:, 0:Q], sel16_t[:, n * 128:(n + 1) * 128],
                        s[:], start=True, stop=True, skip_group_check=True),
                      ("btct", "sel16", f"ps{5 + p2}"), (f"ps{5 + p2}",))
                    E("scalar", lambda e, g=g, j=j, p2=p2, d=dst_buf: e.copy(
                        out=d[:, g * GN * Q:(g + 1) * GN * Q].rearrange(
                            "p (q n) -> p n q", n=GN)[:, j, :],
                        in_=ps[5 + p2][:, 0:Q]),
                      (f"ps{5 + p2}",), (dres, f"ps{5 + p2}"))
            if c == tapc:
                tap("bb", bb_buf[:], (128, NS * Q), "bb")

            if c == tapc:
                tap("state_in", state[:], (128, NIC * NS), "state")
            E("vector", lambda e: e.engine_nop(),
              ("res", "z0", "z1", "e0", "e1"),
              ("res", "z0", "z1", "e0", "e1"))
            for ic in range(NIC):
                par = ic % 2
                (wtiles, wres) = wb_slot(par, cols=128)
                for h in range(2):
                    dma(wtiles[h], s_dtp[h][ic], reads=("s_dtp",), writes=(wres,))
                pdt = [ps[7], ps[4]][par]
                pdtr = ["ps7", "ps4"][par]
                mm3(pdt[:, 0:Q], (wtiles[0], wtiles[1]), (dtih[:], dtil[:]),
                    first=True, last=True,
                    res_r=(wres, "dtipair", pdtr), res_w=(pdtr,))
                E("scalar", lambda e, ic=ic, pdt=pdt, par=par: e.activation(
                    out=tt1[par][:], in_=pdt[:, 0:Q], func=AF.Exp,
                    bias=dtpb_t[:, ic:ic + 1]),
                  (pdtr, "dtpb", f"tt1{par}"), (f"tt1{par}", pdtr))
                E("scalar", lambda e, par=par: e.activation(
                    out=dtt[par], in_=tt1[par][:], func=AF.Ln, bias=1.0),
                  (f"tt1{par}", f"dtt{par}"), (f"dtt{par}",))
                if c == tapc and ic == 0:
                    tap("dt0", dtt[0], (128, Q), "dtt0")
                E("vector", lambda e, ic=ic, par=par: e.tensor_tensor(
                    out=dtut[par], in0=dtt[par],
                    in1=u_buf[:, ic * Q:(ic + 1) * Q],
                    op=AL.mult), (f"dtt{par}", "u", f"dtu{par}"), (f"dtu{par}",))
                (gtiles, gres) = wa_slot(par)
                for h in range(2):
                    dma(gtiles[h], s_win[h][NIC + ic], reads=("s_win",),
                        writes=(gres,))
                pbg = ps[5 + par][:, 0:Q]
                for dc in range(NDC):
                    mm3(pbg, (gtiles[0][:, dc * 128:(dc + 1) * 128],
                              gtiles[1][:, dc * 128:(dc + 1) * 128]),
                        (xh_buf[:, dc * Q:(dc + 1) * Q],
                         xl_buf[:, dc * Q:(dc + 1) * Q]),
                        first=(dc == 0), last=(dc == NDC - 1),
                        res_r=(gres, "xpair", f"ps{5 + par}"),
                        res_w=(f"ps{5 + par}",))
                E("scalar", lambda e, p=pbg, par=par: e.activation(
                    out=get_[par], in_=p, func=AF.Exp, scale=-1.0),
                  (f"ps{5 + par}", f"ge{par}"), (f"ge{par}",))
                E("scalar", lambda e, p=pbg, par=par: e.copy(
                    out=ggt[par], in_=p),
                  (f"ps{5 + par}", f"gg{par}"), (f"gg{par}", f"ps{5 + par}"))

                gq = GN * Q

                def emit_zarg(g, par=par, ic=ic):
                    pg = g % 2
                    dt_b = dtt[par].unsqueeze(2).broadcast_to([128, Q, GN])
                    a_sl = a_buf[:, ic * NS + g * GN: ic * NS + (g + 1) * GN]
                    a_b = a_sl.unsqueeze(1).broadcast_to([128, Q, GN])
                    E("vector", lambda e, db=dt_b, ab=a_b, pg=pg:
                      e.tensor_tensor(out=sc0p[pg].rearrange(
                          "p (q n) -> p q n", n=GN), in0=db, in1=ab,
                          op=AL.mult),
                      (f"dtt{par}", "abuf", f"z{pg}"), (f"z{pg}",))
                    E("scalar", lambda e, pg=pg: e.activation(
                        out=sc1p[pg], in_=sc0p[pg], func=AF.Exp),
                      (f"z{pg}", f"e{pg}"), (f"e{pg}",))

                emit_zarg(0)
                emit_zarg(1)
                for g in range(NG):
                    pg = g % 2
                    bb_g = bb_buf[:, g * gq:(g + 1) * gq]
                    cb_g = cb_buf[:, g * gq:(g + 1) * gq]
                    dtu_b = dtut[par].unsqueeze(2).broadcast_to([128, Q, GN])
                    E("vector", lambda e, db=dtu_b, b=bb_g, pg=pg:
                      e.tensor_tensor(out=sc0p[pg].rearrange(
                          "p (q n) -> p q n", n=GN), in0=db,
                          in1=b.rearrange("p (q n) -> p q n", n=GN),
                          op=AL.mult),
                      (f"dtu{par}", "bb", f"z{pg}", f"e{pg}"), (f"z{pg}",))
                    for j in range(GN):
                        n = g * GN + j
                        da_ap = sc1p[pg].rearrange("p (q n) -> p n q",
                                                   n=GN)[:, j, :]
                        w_ap = sc0p[pg].rearrange("p (q n) -> p n q",
                                                  n=GN)[:, j, :]
                        h_ap = scanh[:, 0:gq].rearrange("p (q n) -> p n q",
                                                        n=GN)[:, j, :]
                        E("vector", lambda e, a=da_ap, w=w_ap, o=h_ap,
                          ini=state[:, ic * NS + n:ic * NS + n + 1]:
                          e.tensor_tensor_scan(o, a, w, ini, AL.mult, AL.add),
                          (f"z{pg}", f"e{pg}", "state", "sc2"), ("sc2",))
                    E("vector", lambda e, cb=cb_g, gq=gq, pg=pg: e.tensor_tensor(
                        out=sc1p[pg], in0=scanh[:, 0:gq], in1=cb,
                        op=AL.mult), ("sc2", "cb", f"e{pg}"), (f"e{pg}",))
                    E("vector", lambda e, g=g, ic=ic, gq=gq: e.tensor_copy(
                        out=state[:, ic * NS + g * GN:ic * NS + (g + 1) * GN],
                        in_=scanh[:, gq - GN:gq]), ("sc2", "state"), ("state",))
                    E("vector", lambda e, g=g, gq=gq, pg=pg: e.tensor_reduce(
                        out=y4[:, g * Q:(g + 1) * Q],
                        in_=sc1p[pg].rearrange("p (q n) -> p q n", n=GN),
                        axis=AX.X, op=AL.add), (f"e{pg}", "y4"), ("y4",))
                    if g + 2 < NG:
                        emit_zarg(g + 2)
                E("vector", lambda e, par=par: e.tensor_tensor(
                    out=tt2[par][:], in0=y4[:, 0:Q], in1=y4[:, Q:2 * Q],
                    op=AL.add), ("y4", f"tt2{par}"), (f"tt2{par}",))
                E("vector", lambda e, par=par: e.tensor_tensor(
                    out=tt1[par][:], in0=y4[:, 2 * Q:3 * Q],
                    in1=y4[:, 3 * Q:4 * Q],
                    op=AL.add), ("y4", f"tt1{par}"), (f"tt1{par}",),
                  sfence=True)
                E("vector", lambda e, par=par: e.tensor_scalar_add(
                    out=dtt[par], in0=get_[par], scalar1=1.0),
                  (f"ge{par}", f"dtt{par}"), (f"dtt{par}",))
                E("vector", lambda e, par=par: e.reciprocal(
                    out=get_[par], in_=dtt[par]),
                  (f"dtt{par}", f"ge{par}"), (f"ge{par}",))
                E("vector", lambda e, par=par: e.tensor_tensor(
                    out=dtt[par], in0=ggt[par], in1=get_[par], op=AL.mult),
                  (f"gg{par}", f"ge{par}", f"dtt{par}"), (f"dtt{par}",))
                E("vector", lambda e, par=par: e.tensor_tensor(
                    out=dtut[par], in0=tt2[par][:], in1=tt1[par][:],
                    op=AL.add),
                  (f"tt2{par}", f"tt1{par}", f"dtu{par}"), (f"dtu{par}",))
                E("vector", lambda e, ic=ic, par=par: e.scalar_tensor_tensor(
                    out=tt1[par][:], in0=u_buf[:, ic * Q:(ic + 1) * Q],
                    scalar=dvec_t[:, ic:ic + 1], in1=dtut[par],
                    op0=AL.mult, op1=AL.add),
                  ("u", f"dtu{par}", "dvec", f"tt1{par}"), (f"tt1{par}",))
                E("vector", lambda e, ic=ic, par=par: e.tensor_tensor(
                    out=u_buf[:, ic * Q:(ic + 1) * Q], in0=tt1[par][:],
                    in1=dtt[par], op=AL.mult),
                  (f"tt1{par}", f"dtt{par}", "u"), ("u",))
                if c == tapc and ic == 0:
                    tap("yfin", u_buf[:, 0:Q], (128, Q), "u")
            if c == tapc:
                tap("ymall", u_buf[:], (128, NIC * Q), "u")
            E("vector", lambda e: e.engine_nop(),
              ("res", "z0", "z1", "e0", "e1"),
              ("res", "z0", "z1", "e0", "e1"))

            for grp in range(2):
                for ic in range(NIC):
                    par = ic % 2
                    split_pair(u_buf[:, ic * Q:(ic + 1) * Q],
                               yph[par][:], ypl[par][:], ("u",), (f"upair{par}",))
                    (wtiles, wres) = wb_slot(par)
                    for h in range(2):
                        dma(wtiles[h],
                            s_opw[h][ic][:, grp * 8 * 128:(grp + 1) * 8 * 128],
                            reads=("s_opw",), writes=(wres,))
                    for j in range(8):
                        pa = ps[j][:, 0:Q]
                        mm3(pa, (wtiles[0][:, j * 128:(j + 1) * 128],
                                 wtiles[1][:, j * 128:(j + 1) * 128]),
                            (yph[par][:], ypl[par][:]),
                            first=(ic == 0), last=(ic == NIC - 1),
                            res_r=(wres, f"upair{par}", f"ps{j}"),
                            res_w=(f"ps{j}",))
                for j in range(8):
                    dc = grp * 8 + j
                    pa = ps[j][:, 0:Q]
                    E("vector", lambda e, pa=pa, dc=dc: e.tensor_tensor(
                        out=tA[0][:], in0=pa, in1=xh_buf[:, dc * Q:(dc + 1) * Q],
                        op=AL.add),
                      (f"ps{j}", "xpair", "tA0"),
                      ("tA0", f"ps{j}"))
                    E("vector", lambda e, dc=dc: e.tensor_tensor(
                        out=res_buf[:, dc * Q:(dc + 1) * Q], in0=tA[0][:],
                        in1=xl_buf[:, dc * Q:(dc + 1) * Q], op=AL.add),
                      ("tA0", "xpair"), ("res",))
            if c == tapc:
                tap("attnres", res_buf[:], (128, NDC * Q), "res")

            def layer_norm(src_buf, src_res, out_cb, psA, psB, psC):
                for dc in range(NDC):
                    dp = dc % 2
                    E("scalar", lambda e, dc=dc, dp=dp: e.activation(
                        out=tA[dp][:], in_=src_buf[:, dc * Q:(dc + 1) * Q],
                        func=AF.Square), (src_res, f"tA{dp}"), (f"tA{dp}",))
                    E("tensor", lambda e, dc=dc: e.matmul(
                        ps[psA][0:1, 0:Q], ones_t[:],
                        src_buf[:, dc * Q:(dc + 1) * Q],
                        start=(dc == 0), stop=(dc == NDC - 1),
                        skip_group_check=True),
                      (src_res, "ones", f"ps{psA}"), (f"ps{psA}",))
                    E("tensor", lambda e, dc=dc, dp=dp: e.matmul(
                        ps[psB][0:1, 0:Q], ones_t[:], tA[dp][:],
                        start=(dc == 0), stop=(dc == NDC - 1),
                        skip_group_check=True),
                      (f"tA{dp}", "ones", f"ps{psB}"), (f"ps{psB}",))
                E("scalar", lambda e: e.activation(
                    out=st1[0:1, 0:Q], in_=ps[psA][0:1, 0:Q], func=AF.Copy,
                    scale=1.0 / D), (f"ps{psA}", "y4"), ("y4", f"ps{psA}"))
                E("scalar", lambda e: e.activation(
                    out=st1[0:1, Q:2 * Q], in_=st1[0:1, 0:Q], func=AF.Square),
                  ("y4",), ("y4",))
                E("scalar", lambda e: e.activation(
                    out=st1[0:1, 2 * Q:3 * Q], in_=ps[psB][0:1, 0:Q],
                    func=AF.Copy, scale=1.0 / D),
                  (f"ps{psB}", "y4"), ("y4", f"ps{psB}"))
                E("vector", lambda e: e.tensor_tensor(
                    out=st1[0:1, 3 * Q:4 * Q], in0=st1[0:1, 2 * Q:3 * Q],
                    in1=st1[0:1, Q:2 * Q], op=AL.subtract), ("y4",), ("y4",))
                E("vector", lambda e: e.tensor_scalar_add(
                    out=st1[0:1, Q:2 * Q], in0=st1[0:1, 3 * Q:4 * Q],
                    scalar1=EPS), ("y4",), ("y4",))
                E("scalar", lambda e: e.activation(
                    out=st1[0:1, 2 * Q:3 * Q], in_=st1[0:1, Q:2 * Q],
                    func=AF.Ln), ("y4",), ("y4",))
                E("scalar", lambda e: e.activation(
                    out=st1[0:1, 3 * Q:4 * Q], in_=st1[0:1, 2 * Q:3 * Q],
                    func=AF.Exp, scale=-0.5), ("y4",), ("y4",))
                E("tensor", lambda e: e.matmul(
                    ps[psC][:, 0:Q], bc1_t[:], st1[0:1, 0:Q],
                    start=True, stop=True, skip_group_check=True),
                  ("y4", "sel16", f"ps{psC}"), (f"ps{psC}",))
                E("tensor", lambda e: e.matmul(
                    ps[psC][:, Q:2 * Q], bc1_t[:], st1[0:1, 3 * Q:4 * Q],
                    start=True, stop=True, skip_group_check=True),
                  ("y4", "sel16", f"ps{psC}"), (f"ps{psC}",))
                E("scalar", lambda e: e.copy(
                    out=mb_t, in_=ps[psC][:, 0:Q]),
                  (f"ps{psC}", "y4"), ("y4",))
                E("scalar", lambda e: e.copy(
                    out=rb_t, in_=ps[psC][:, Q:2 * Q]),
                  (f"ps{psC}", "y4"), ("y4", f"ps{psC}"))
                for dc in range(NDC):
                    dp = dc % 2
                    E("vector", lambda e, dc=dc, dp=dp: e.tensor_tensor(
                        out=tA[dp][:], in0=src_buf[:, dc * Q:(dc + 1) * Q],
                        in1=mb_t, op=AL.subtract),
                      (src_res, "y4", f"tA{dp}"), (f"tA{dp}",))
                    E("vector", lambda e, dc=dc, dp=dp: e.tensor_tensor(
                        out=tB[dp][:], in0=tA[dp][:], in1=rb_t, op=AL.mult),
                      (f"tA{dp}", "y4", f"tB{dp}"), (f"tB{dp}",))
                    out_cb(dc)

            def ln1_out(dc):
                split_pair(tB[dc % 2][:],
                           x1h_buf[:, dc * Q:(dc + 1) * Q],
                           x1l_buf[:, dc * Q:(dc + 1) * Q],
                           (f"tB{dc % 2}",), ("x1pair",))
            layer_norm(res_buf, "res", ln1_out, 4, 5, 6)
            if c == tapc:
                tap("x1", x1h_buf[:], (128, NDC * Q), "x1pair")

            if c == 0:
                for h in range(2):
                    dma(laut[h][:], s_lau[h][:, :], reads=("s_lau",),
                        writes=("laut",))

            for dc in range(NDC):
                mm3(ps[2][0:16, 0:Q],
                    (laut[0][:, dc * 16:(dc + 1) * 16],
                     laut[1][:, dc * 16:(dc + 1) * 16]),
                    (x1h_buf[:, dc * Q:(dc + 1) * Q],
                     x1l_buf[:, dc * Q:(dc + 1) * Q]),
                    first=(dc == 0), last=(dc == NDC - 1),
                    res_r=("laut", "x1pair", "ps2"), res_w=("ps2",))
            E("scalar", lambda e: e.copy(out=tG[0:16, :], in_=ps[2][0:16, 0:Q]),
              ("ps2", "tG"), ("tG", "ps2"))
            split_pair(tG[0:16, :], t1h[:], t1l[:], ("tG",), ("dtipair",))

            for hbk in range(8):
                hpar = hbk % 2
                for jj in range(8):
                    hc = hbk * 8 + jj
                    par = jj % 2
                    (wtiles, wres) = wa_slot(par)
                    for h in range(2):
                        dma(wtiles[h], s_wup[h][hc], reads=("s_wup",),
                            writes=(wres,))
                    pb = ps[par][:, 0:Q]
                    for dc in range(NDC):
                        mm3(pb, (wtiles[0][:, dc * 128:(dc + 1) * 128],
                                 wtiles[1][:, dc * 128:(dc + 1) * 128]),
                            (x1h_buf[:, dc * Q:(dc + 1) * Q],
                             x1l_buf[:, dc * Q:(dc + 1) * Q]),
                            first=(dc == 0), last=False,
                            res_r=(wres, "x1pair", f"ps{par}"),
                            res_w=(f"ps{par}",))
                    for h in range(2):
                        dma(lbut[h][:, par * 128:(par + 1) * 128],
                            s_lbu[h][:, hc * 128:(hc + 1) * 128],
                            reads=("s_lbu",), writes=(f"lbu{par}",))
                    mm3(pb, (lbut[0][:, par * 128:(par + 1) * 128],
                             lbut[1][:, par * 128:(par + 1) * 128]),
                        (t1h[:], t1l[:]), first=False, last=True,
                        res_r=(f"lbu{par}", "dtipair", f"ps{par}"),
                        res_w=(f"ps{par}",))
                    E("scalar", lambda e, pb=pb, hc=hc, par=par: e.activation(
                        out=tA[par][:], in_=pb, func=AF.Relu,
                        bias=bup_t[:, hc:hc + 1]),
                      (f"ps{par}", "bup", f"tA{par}"), (f"tA{par}", f"ps{par}"))
                    o = (hpar * 8 + jj) * Q
                    split_pair(tA[par][:], x2h[:, o:o + Q], x2l[:, o:o + Q],
                               (f"tA{par}",), (f"x2p{hpar}",))
                    for h in range(2):
                        dma(ladt[h][:, par * 16:(par + 1) * 16],
                            s_lad[h][:, hc * 16:(hc + 1) * 16],
                            reads=("s_lad",), writes=(f"lad{par}",))
                    mm3(ps[3][0:16, 0:Q],
                        (ladt[0][:, par * 16:(par + 1) * 16],
                         ladt[1][:, par * 16:(par + 1) * 16]),
                        (x2h[:, o:o + Q], x2l[:, o:o + Q]),
                        first=(hc == 0), last=(hc == NHC - 1),
                        res_r=(f"lad{par}", f"x2p{hpar}", "ps3"), res_w=("ps3",))
                for dc in range(NDC):
                    par = dc % 2
                    (wtiles, wres) = wb_slot(par)
                    for h in range(2):
                        dma(wtiles[h],
                            s_wdn[h][dc][:, hbk * 8 * 128:(hbk + 1) * 8 * 128],
                            reads=("s_wdn",), writes=(wres,))
                    pb = ps[4 + par][:, 0:Q]
                    for jj in range(8):
                        o = (hpar * 8 + jj) * Q
                        mm3(pb, (wtiles[0][:, jj * 128:(jj + 1) * 128],
                                 wtiles[1][:, jj * 128:(jj + 1) * 128]),
                            (x2h[:, o:o + Q], x2l[:, o:o + Q]),
                            first=(jj == 0), last=(jj == 7 and hbk != 7),
                            res_r=(wres, f"x2p{hpar}", f"ps{4 + par}"),
                            res_w=(f"ps{4 + par}",))
                    if hbk == 7:
                        if dc == 0:
                            E("scalar", lambda e: e.copy(
                                out=tG[0:16, :], in_=ps[3][0:16, 0:Q]),
                              ("ps3", "tG"), ("tG", "ps3"))
                            split_pair(tG[0:16, :], t2h[:], t2l[:], ("tG",),
                                       ("dtipair",))
                        for h in range(2):
                            dma(lbdt[h][:, par * 128:(par + 1) * 128],
                                s_lbd[h][:, dc * 128:(dc + 1) * 128],
                                reads=("s_lbd",), writes=(f"lbu{par}",))
                        mm3(pb, (lbdt[0][:, par * 128:(par + 1) * 128],
                                 lbdt[1][:, par * 128:(par + 1) * 128]),
                            (t2h[:], t2l[:]), first=False, last=True,
                            res_r=(f"lbu{par}", "dtipair", f"ps{4 + par}"),
                            res_w=(f"ps{4 + par}",))
                    cur = hbk % 2
                    prv = 1 - cur
                    dst = u_buf[:, (cur * NDC + dc) * Q:(cur * NDC + dc + 1) * Q]
                    srcp = u_buf[:, (prv * NDC + dc) * Q:(prv * NDC + dc + 1) * Q]
                    if hbk == 0:
                        E("scalar", lambda e, pb=pb, d=dst: e.copy(
                            out=d, in_=pb),
                          (f"ps{4 + par}", "u"), ("u", f"ps{4 + par}"))
                    else:
                        E("vector", lambda e, pb=pb, d=dst, s=srcp:
                          e.tensor_tensor(out=d, in0=pb, in1=s, op=AL.add),
                          (f"ps{4 + par}", "u"), ("u", f"ps{4 + par}"))
            for dc in range(NDC):
                y2fin = u_buf[:, (NDC + dc) * Q:(NDC + dc + 1) * Q]
                E("vector", lambda e, s=y2fin, dc=dc: e.tensor_scalar_add(
                    out=tA[0][:], in0=s, scalar1=bdn_t[:, dc:dc + 1]),
                  ("u", "bdn", "tA0"), ("tA0",))
                E("vector", lambda e, dc=dc: e.tensor_tensor(
                    out=tB[1][:], in0=tA[0][:],
                    in1=x1h_buf[:, dc * Q:(dc + 1) * Q], op=AL.add),
                  ("tA0", "x1pair", "tB1"), ("tB1",))
                E("vector", lambda e, dc=dc: e.tensor_tensor(
                    out=res_buf[:, dc * Q:(dc + 1) * Q], in0=tB[1][:],
                    in1=x1l_buf[:, dc * Q:(dc + 1) * Q], op=AL.add),
                  ("tB1", "x1pair"), ("res",))
            if c == tapc:
                tap("ffnres", res_buf[:], (128, NDC * Q), "res")

            def ln2_out(dc, c=c):
                E("sync", lambda e, dc=dc, c=c: e.dma_start(
                    out=out_t[dc * 128:(dc + 1) * 128, c * Q:(c + 1) * Q],
                    in_=tB[dc % 2][:]), (f"tB{dc % 2}",), ("outdma",))
            layer_norm(res_buf, "res", ln2_out, 4, 5, 6)

        A.final_waits = ["sync", "gpsimd"]
        A.assemble(nc, block)

    return nc, tap_names


def prep_inputs(inputs):
    f32 = np.float32
    g = {k: np.asarray(v) for k, v in inputs.items()}
    x = g["L0_attn_input_states"]
    base = {}
    base["w_in"] = np.ascontiguousarray(g["L0_attn_in_proj_w"], f32)
    cw = np.asarray(g["L0_attn_conv_w"], f32).reshape(IN, 4)
    base["conv_w"] = np.ascontiguousarray(
        cw.reshape(NIC, 128, 4).transpose(1, 0, 2).reshape(128, NIC * 4))
    base["conv_b"] = np.ascontiguousarray(
        np.asarray(g["L0_attn_conv_b"], f32).reshape(NIC, 128).T)
    xp = np.asarray(g["L0_attn_x_proj_w"], f32)
    base["xp_dt"] = np.ascontiguousarray(xp[:, 0:R])
    base["xp_b"] = np.ascontiguousarray(xp[:, R:R + NS])
    base["xp_c"] = np.ascontiguousarray(xp[:, R + NS:R + 2 * NS])
    base["dtp_w"] = np.ascontiguousarray(g["L0_attn_dt_proj_w"], f32)
    base["dtp_b"] = np.ascontiguousarray(
        np.asarray(g["L0_attn_dt_proj_b"], f32).reshape(NIC, 128).T)
    base["a_log"] = np.ascontiguousarray(
        np.asarray(g["L0_attn_A_log"], f32).reshape(NIC, 128, NS)
        .transpose(1, 0, 2).reshape(128, NIC * NS))
    base["d_vec"] = np.ascontiguousarray(
        np.asarray(g["L0_attn_D"], f32).reshape(NIC, 128).T)
    base["op_w"] = np.ascontiguousarray(g["L0_attn_out_proj_w"], f32)
    base["wup_q"] = np.ascontiguousarray(np.asarray(g["L0_ffn_w_up_q"]).T)
    base["wup_s"] = np.ascontiguousarray(np.asarray(g["L0_ffn_w_up_scale"], f32).T)
    base["b_up"] = np.ascontiguousarray(
        np.asarray(g["L0_ffn_b_up"], f32).reshape(NHC, 128).T)
    base["lau"] = np.ascontiguousarray(g["L0_ffn_w_up_lora_a"], f32)
    base["lbu"] = np.ascontiguousarray(g["L0_ffn_w_up_lora_b"], f32)
    base["wdn_q"] = np.ascontiguousarray(np.asarray(g["L0_ffn_w_down_q"]).T)
    base["wdn_s"] = np.ascontiguousarray(
        np.asarray(g["L0_ffn_w_down_scale"], f32).T)
    base["b_dn"] = np.ascontiguousarray(
        np.asarray(g["L0_ffn_b_down"], f32).reshape(NDC, 128).T)
    base["lad"] = np.ascontiguousarray(g["L0_ffn_w_down_lora_a"], f32)
    base["lbd"] = np.ascontiguousarray(g["L0_ffn_w_down_lora_b"], f32)
    sel16v = np.zeros((NS, NS * 128), f32)
    for n in range(NS):
        sel16v[n, n * 128:(n + 1) * 128] = 1.0
    base["sel16"] = sel16v
    sel2v = np.zeros((2, 128), f32)
    sel2v[0, 0:64] = 1.0
    sel2v[1, 64:128] = 1.0
    base["sel2"] = sel2v
    base["ones_c"] = np.ones((128, 1), f32)

    in_maps = []
    for cid in range(NCORES):
        m = dict(base)
        m["x_t"] = np.ascontiguousarray(x[cid].T.astype(f32))
        in_maps.append(m)
    return in_maps


def kernel(**inputs):
    in_maps = prep_inputs(inputs)
    nc, _ = build_program()
    res = run_bass_kernel_spmd(nc, in_maps, core_ids=list(range(NCORES)))
    out = np.stack([np.asarray(r["out_t"]).T for r in res.results])
    return np.ascontiguousarray(out).astype(np.float32)


# version 08 (final)
# speedup vs baseline: 1.2216x; 1.0400x over previous
import numpy as np
from contextlib import ExitStack
import concourse.bass as bass
import concourse.mybir as mybir
from concourse.bass_utils import run_bass_kernel_spmd

F32 = mybir.dt.float32
BF16 = mybir.dt.bfloat16
I32 = mybir.dt.int32
AL = mybir.AluOpType
AF = mybir.ActivationFunctionType
AX = mybir.AxisListType

T, D, IN, NS, R, HF = 2048, 2048, 4096, 16, 128, 8192
Q = 256
NCH = T // Q
NDC = D // 128
NIC = IN // 128
NHC = HF // 128
NG = 4
GN = NS // NG
EPS = 1e-5
NCORES = 8

ENGS = ("sync", "tensor", "vector", "scalar", "gpsimd")
DMA_ENGS = {"sync"}


class Asm:

    def __init__(self):
        self.ops = []
        self.final_waits = []

    def emit(self, eng, build, reads=(), writes=(), sfence=False, is_dma=False):
        self.ops.append([eng, build, tuple(reads), tuple(writes), False, sfence,
                         is_dma or eng in DMA_ENGS])

    def assemble(self, nc, block):
        ops = self.ops
        n = len(ops)
        last_w = {}
        readers = {}
        deps = [None] * n
        for i, (eng, _b, rd, wr, _f, sfence, _d) in enumerate(ops):
            dd = set()
            for r in rd:
                j = last_w.get(r)
                if j is not None and (ops[j][0] != eng or sfence):
                    dd.add(j)
            for r in wr:
                j = last_w.get(r)
                if j is not None and (ops[j][0] != eng or sfence):
                    dd.add(j)
                for j in readers.get(r, ()):
                    if ops[j][0] != eng or sfence:
                        dd.add(j)
            for r in rd:
                readers.setdefault(r, []).append(i)
            for r in wr:
                last_w[r] = i
                readers[r] = []
            deps[i] = dd
        for i in range(n):
            for j in deps[i]:
                ops[j][4] = True
        def key_of(i):
            eng, _b, _r, wr, _m, _sf, isd = ops[i]
            if isd:
                assert wr, f"dma op {i} needs a write resource"
                return "d:" + wr[0]
            return eng
        for op in ops:
            if op[6]:
                op[4] = True
        last_of = {}
        for i, op in enumerate(ops):
            last_of[op[0]] = i
        for e in ENGS:
            if e in last_of:
                ops[last_of[e]][4] = True
        kcount = {}
        inc_val = [0] * n
        for i in range(n):
            k = key_of(i)
            if ops[i][4]:
                kcount[k] = kcount.get(k, 0) + (16 if ops[i][6] else 1)
            inc_val[i] = kcount.get(k, 0)
        need_keys = [None] * n
        for i in range(n):
            ks = set()
            for j in deps[i]:
                ks.add(key_of(j))
            need_keys[i] = ks
        run_cnt = {}
        wait_val = [None] * n
        for i in range(n):
            if need_keys[i]:
                wv = {}
                for j in deps[i]:
                    k = key_of(j)
                    if k.startswith("d:"):
                        wv[k] = run_cnt.get(k, 0)
                    else:
                        wv[k] = max(wv.get(k, 0), inc_val[j])
                wait_val[i] = wv
            k = key_of(i)
            if ops[i][4]:
                run_cnt[k] = run_cnt.get(k, 0) + (16 if ops[i][6] else 1)
        finals = dict(run_cnt)
        all_keys = sorted(finals)
        sems = {k: nc.alloc_semaphore(f"s_{k.replace(':', '_')}")
                for k in all_keys}
        watermark = {e: {} for e in ENGS}

        def run_engine(e_obj, eng):
            wm = watermark[eng]
            for i in range(n):
                oe, build, _r, _w, marked, _sf, isd = ops[i]
                if oe != eng:
                    continue
                if wait_val[i]:
                    for k in sorted(wait_val[i]):
                        v = wait_val[i][k]
                        if v > wm.get(k, 0):
                            e_obj.wait_ge(sems[k], v)
                            wm[k] = v
                ins = build(e_obj)
                if marked:
                    ins.then_inc(sems[key_of(i)], 16 if isd else 1)
            if eng in self.final_waits:
                for k in all_keys:
                    if k != eng and finals[k] > watermark[eng].get(k, 0):
                        e_obj.wait_ge(sems[k], finals[k])
                        watermark[eng][k] = finals[k]

        for eng, starter in (("sync", block.sync), ("tensor", block.tensor),
                             ("vector", block.vector), ("scalar", block.scalar),
                             ("gpsimd", block.gpsimd)):
            starter(lambda e, eng=eng: run_engine(e, eng))


def build_program(taps=(), nch=NCH, tapc=0):
    nc = bass.Bass("TRN2", target_bir_lowering=False,
                   detect_race_conditions=False)
    A = Asm()
    tap_names = []

    def din(name, shape, dt=F32):
        return nc.dram_tensor(name, list(shape), dt, kind="ExternalInput").ap()

    x_t = din("x_t", (D, T))
    w_in = din("w_in", (D, HF))
    conv_w = din("conv_w", (128, NIC * 4))
    conv_b = din("conv_b", (128, NIC))
    xp_dt = din("xp_dt", (IN, R))
    xp_b = din("xp_b", (IN, NS))
    xp_c = din("xp_c", (IN, NS))
    dtp_w = din("dtp_w", (R, IN))
    dtp_b = din("dtp_b", (128, NIC))
    a_log = din("a_log", (128, NIC * NS))
    d_vec = din("d_vec", (128, NIC))
    op_w = din("op_w", (IN, D))
    wup_q = din("wup_q", (D, HF), I32)
    wup_s = din("wup_s", (2 * NDC, HF))
    b_up = din("b_up", (128, NHC))
    lau = din("lau", (D, 16))
    lbu = din("lbu", (16, HF))
    wdn_q = din("wdn_q", (HF, D), I32)
    wdn_s = din("wdn_s", (2 * NHC, D))
    b_dn = din("b_dn", (128, NDC))
    lad = din("lad", (HF, 16))
    lbd = din("lbd", (16, D))
    sel16 = din("sel16", (NS, NS * 128))
    sel2 = din("sel2", (2, 128))
    ones_c = din("ones_c", (128, 1))

    out_t = nc.dram_tensor("out_t", [D, T], F32, kind="ExternalOutput").ap()

    def scr(name, shape, dt=BF16):
        return nc.dram_tensor(name, list(shape), dt, kind="Internal").ap()

    s_win = [scr("s_win_h", (2 * NIC, 128, NDC * 128)),
             scr("s_win_l", (2 * NIC, 128, NDC * 128))]
    s_opw = [scr("s_opw_h", (NIC, 128, NDC * 128)),
             scr("s_opw_l", (NIC, 128, NDC * 128))]
    s_dtp = [scr("s_dtp_h", (NIC, 128, 128)), scr("s_dtp_l", (NIC, 128, 128))]
    s_xp = [scr("s_xp_h", (NIC, 128, R + 2 * NS)),
            scr("s_xp_l", (NIC, 128, R + 2 * NS))]
    s_wup = [scr("s_wup_h", (NHC, 128, NDC * 128)),
             scr("s_wup_l", (NHC, 128, NDC * 128))]
    s_wdn = [scr("s_wdn_h", (NDC, 128, NHC * 128)),
             scr("s_wdn_l", (NDC, 128, NHC * 128))]
    s_lau = [scr("s_lau_h", (128, NDC * 16)), scr("s_lau_l", (128, NDC * 16))]
    s_lbu = [scr("s_lbu_h", (16, HF)), scr("s_lbu_l", (16, HF))]
    s_lad = [scr("s_lad_h", (128, NHC * 16)), scr("s_lad_l", (128, NHC * 16))]
    s_lbd = [scr("s_lbd_h", (16, D)), scr("s_lbd_l", (16, D))]

    es = ExitStack()

    def sbt(name, shape, dt=F32):
        return es.enter_context(nc.sbuf_tensor(name, list(shape), dt))

    with es:
        u_buf = sbt("u_buf", (128, NIC * Q))
        bb_buf = sbt("bb_buf", (128, NS * Q))
        cb_buf = sbt("cb_buf", (128, NS * Q))
        scanh = sbt("scanh", (128, GN * Q))
        tmpP = sbt("tmpP", (128, GN * Q))
        tmpQ = sbt("tmpQ", (128, GN * Q))
        res_buf = sbt("res_buf", (128, NDC * Q))
        sc0p = [res_buf[:, 0:GN * Q], res_buf[:, GN * Q:2 * GN * Q]]
        sc1p = [res_buf[:, 2 * GN * Q:3 * GN * Q],
                res_buf[:, 3 * GN * Q:4 * GN * Q]]
        dtt = [tmpP[:, 0:Q], tmpP[:, Q:2 * Q]]
        dtut = [tmpP[:, 2 * Q:3 * Q], tmpP[:, 3 * Q:4 * Q]]
        get_ = [tmpQ[:, 0:Q], tmpQ[:, Q:2 * Q]]
        ggt = [tmpQ[:, 2 * Q:3 * Q], tmpQ[:, 3 * Q:4 * Q]]
        y4 = sbt("y4", (128, NG * Q))
        st1 = y4[0:1, :]
        mb_t = y4[:, 0:Q]
        rb_t = y4[:, Q:2 * Q]
        hidt = [sbt(f"hid{i}", (128, Q + 4)) for i in range(2)]
        tA = [sbt(f"tA{i}", (128, Q)) for i in range(2)]
        tB = [sbt(f"tB{i}", (128, Q)) for i in range(2)]
        tC = [sbt(f"tC{i}", (128, Q)) for i in range(2)]
        tD = sbt("tD", (128, Q))
        tE = sbt("tE", (128, Q))
        tF = sbt("tF", (128, Q))
        tG = sbt("tG", (128, Q))
        tH = sbt("tH", (128, Q))
        tt1 = [tD, tE]
        tt2 = [tF, tH]
        a_buf = sbt("a_buf", (128, NIC * NS))
        state = sbt("state", (128, NIC * NS))
        tails = sbt("tails", (128, NIC * 4))
        sel16_t = sbt("sel16_t", (NS, NS * 128))
        sel2_t = sbt("sel2_t", (2, 128))
        bc1_t = sel16_t[0:1, 0:128]
        ones_t = sbt("ones_t", (128, 1))
        convw_t = sbt("convw_t", (128, NIC * 4))
        convb_t = sbt("convb_t", (128, NIC))
        dtpb_t = sbt("dtpb_t", (128, NIC))
        dvec_t = sbt("dvec_t", (128, NIC))
        bup_t = sbt("bup_t", (128, NHC))
        bdn_t = sbt("bdn_t", (128, NDC))
        bt_t = sbt("bt_t", (NS, Q))
        ct_t = sbt("ct_t", (NS, Q))
        xst = sbt("xst", (128, 2 * Q))
        xh_buf = sbt("xh_buf", (128, NDC * Q), BF16)
        xl_buf = sbt("xl_buf", (128, NDC * Q), BF16)
        x1h_buf = sbt("x1h_buf", (128, NDC * Q), BF16)
        x1l_buf = sbt("x1l_buf", (128, NDC * Q), BF16)
        x2h = sbt("x2h", (128, 16 * Q), BF16)
        x2l = sbt("x2l", (128, 16 * Q), BF16)
        wa_h = sbt("wa_h", (128, 2 * NDC * 128), BF16)
        wa_l = sbt("wa_l", (128, 2 * NDC * 128), BF16)
        wb_h = sbt("wb_h", (128, 2 * 8 * 128), BF16)
        wb_l = sbt("wb_l", (128, 2 * 8 * 128), BF16)
        dtih = sbt("dtih", (128, Q), BF16)
        dtil = sbt("dtil", (128, Q), BF16)
        uht = [sbt(f"uh{i}", (128, Q), BF16) for i in range(2)]
        ult = [sbt(f"ul{i}", (128, Q), BF16) for i in range(2)]
        yph = uht
        ypl = ult
        t1h = dtih[0:16, :]
        t1l = dtil[0:16, :]
        t2h = dtih[0:16, :]
        t2l = dtil[0:16, :]
        laut = [sbt("laut_h", (128, NDC * 16), BF16),
                sbt("laut_l", (128, NDC * 16), BF16)]
        lbut = [sbt("lbut_h", (16, 2 * 128), BF16),
                sbt("lbut_l", (16, 2 * 128), BF16)]
        ladt = [sbt("ladt_h", (128, 2 * 16), BF16),
                sbt("ladt_l", (128, 2 * 16), BF16)]
        lbdt = lbut
        pp_f = [res_buf[:, 0:2048], res_buf[:, 2048:4096]]
        pp_h = [x2h[:, 0:2048], x2h[:, 2048:4096]]
        pp_l = [x2l[:, 0:2048], x2l[:, 2048:4096]]
        pp_s = [xst[:, 0:512], xst[:, 0:512]]
        pp_q0 = sbt("pp_q0", (128, 512), I32)
        pp_q = [pp_q0, pp_q0]
        sc_r = [bb_buf[0:2, 0:1024], cb_buf[0:2, 0:1024]]
        ppfr = ["res", "res1"]
        pphr = ["x2p0", "x2p1"]
        ppsr = ["xst0", "xst0"]
        ppqr = ["ppq0", "ppq0"]
        scrr = ["bb", "cb"]

        ps = [es.enter_context(nc.psum_tensor(f"pb{i}", [128, 512], F32))
              for i in range(8)]

        block = es.enter_context(nc.Block())

        def E(eng, build, reads=(), writes=(), sfence=False, is_dma=False):
            A.emit(eng, build, reads, writes, sfence, is_dma)

        def dma(dst, src, reads=(), writes=(), q="sync"):
            E(q, lambda e, d=dst, s=src: e.dma_start(out=d, in_=s),
              reads, writes, is_dma=True)

        def tap(name, ap, shape, res):
            if name not in taps:
                return
            h = nc.dram_tensor("tap_" + name, list(shape), ap.dtype,
                               kind="ExternalOutput").ap()
            tap_names.append(name)
            dma(h[:], ap, reads=(res,), writes=("tapdma",))

        def mm3(psum_ap, lhs, rhs, first, last, res_r, res_w, skip=True):
            lh, ll = lhs
            rh, rl = rhs
            E("tensor", lambda e, a=psum_ap, x=lh, y=rh, st=first:
              e.matmul(a, x, y, start=st, stop=False, skip_group_check=skip),
              res_r, res_w if not last else ())
            E("tensor", lambda e, a=psum_ap, x=lh, y=rl:
              e.matmul(a, x, y, start=False, stop=False, skip_group_check=skip),
              res_r, ())
            E("tensor", lambda e, a=psum_ap, x=ll, y=rh, sp=last:
              e.matmul(a, x, y, start=False, stop=sp, skip_group_check=skip),
              res_r, res_w if last else ())

        def split_pair(src_ap, dst_h, dst_l, rd, wr, hi_eng="vector"):
            E(hi_eng, lambda e, s=src_ap, d=dst_h: e.tensor_copy(out=d, in_=s)
              if hi_eng == "vector" else e.copy(out=d, in_=s), rd, wr)
            E("vector", lambda e, s=src_ap, h=dst_h, d=dst_l: e.tensor_tensor(
                out=d, in0=s, in1=h, op=AL.subtract), rd + wr, wr)

        for dst, src, tg in ((sel16_t, sel16, "sel16"), (sel2_t, sel2, "sel2"),
                             (ones_t, ones_c, "ones"),
                             (convw_t, conv_w, "convw"), (convb_t, conv_b, "convb"),
                             (dtpb_t, dtp_b, "dtpb"), (dvec_t, d_vec, "dvec"),
                             (bup_t, b_up, "bup"), (bdn_t, b_dn, "bdn")):
            dma(dst[:], src[:], writes=(tg,))

        pp_ctr = [0]

        def pp_next():
            p = pp_ctr[0] & 1
            pp_ctr[0] += 1
            return p, ("sync", "scalar")[p]

        def pp_split(p, cols, rows=128):
            split_pair(pp_f[p][0:rows, 0:cols], pp_h[p][0:rows, 0:cols],
                       pp_l[p][0:rows, 0:cols], (ppfr[p],), (pphr[p],))

        for dc in range(NDC):
            for piece in range(4):
                p, q = pp_next()
                dma(pp_f[p][:, 0:2048],
                    w_in[dc * 128:(dc + 1) * 128, piece * 2048:(piece + 1) * 2048],
                    writes=(ppfr[p],), q=q)
                pp_split(p, 2048)
                for jj in range(16):
                    jc = piece * 16 + jj
                    for h in range(2):
                        dma(s_win[h][jc][:, dc * 128:(dc + 1) * 128],
                            [pp_h, pp_l][h][p][:, jj * 128:(jj + 1) * 128],
                            reads=(pphr[p],), writes=("s_win",), q=q)
        for ic in range(NIC):
            p, q = pp_next()
            dma(pp_f[p][:, 0:2048], op_w[ic * 128:(ic + 1) * 128, :],
                writes=(ppfr[p],), q=q)
            pp_split(p, 2048)
            for h in range(2):
                dma(s_opw[h][ic], [pp_h, pp_l][h][p][:, 0:2048],
                    reads=(pphr[p],), writes=("s_opw",), q=q)
        for piece in range(2):
            p, q = pp_next()
            dma(pp_f[p][:, 0:2048], dtp_w[:, piece * 2048:(piece + 1) * 2048],
                writes=(ppfr[p],), q=q)
            pp_split(p, 2048)
            for jj in range(16):
                ic = piece * 16 + jj
                for h in range(2):
                    dma(s_dtp[h][ic],
                        [pp_h, pp_l][h][p][:, jj * 128:(jj + 1) * 128],
                        reads=(pphr[p],), writes=("s_dtp",), q=q)
        for ic in range(NIC):
            p, q = pp_next()
            dma(pp_f[p][:, 0:R], xp_dt[ic * 128:(ic + 1) * 128, :],
                writes=(ppfr[p],), q=q)
            dma(pp_f[p][:, R:R + NS], xp_b[ic * 128:(ic + 1) * 128, :],
                writes=(ppfr[p],), q=q)
            dma(pp_f[p][:, R + NS:R + 2 * NS], xp_c[ic * 128:(ic + 1) * 128, :],
                writes=(ppfr[p],), q=q)
            pp_split(p, R + 2 * NS)
            for h in range(2):
                dma(s_xp[h][ic], [pp_h, pp_l][h][p][:, 0:R + 2 * NS],
                    reads=(pphr[p],), writes=("s_xp",), q=q)
        p, q = pp_next()
        for dc in range(NDC):
            dma(pp_f[p][:, dc * 16:(dc + 1) * 16],
                lau[dc * 128:(dc + 1) * 128, :], writes=(ppfr[p],), q=q)
        pp_split(p, NDC * 16)
        for h in range(2):
            dma(s_lau[h][:, :], [pp_h, pp_l][h][p][:, 0:NDC * 16],
                reads=(pphr[p],), writes=("s_lau",), q=q)
        for half in range(2):
            p, q = pp_next()
            for j in range(NHC // 2):
                hc = half * (NHC // 2) + j
                dma(pp_f[p][:, j * 16:(j + 1) * 16],
                    lad[hc * 128:(hc + 1) * 128, :], writes=(ppfr[p],), q=q)
            pp_split(p, (NHC // 2) * 16)
            for h in range(2):
                dma(s_lad[h][:, half * (NHC // 2) * 16:
                             (half + 1) * (NHC // 2) * 16],
                    [pp_h, pp_l][h][p][:, 0:(NHC // 2) * 16],
                    reads=(pphr[p],), writes=("s_lad",), q=q)
        for piece in range(4):
            p, q = pp_next()
            dma(pp_f[p][0:16, 0:2048], lbu[:, piece * 2048:(piece + 1) * 2048],
                writes=(ppfr[p],), q=q)
            pp_split(p, 2048, rows=16)
            for h in range(2):
                dma(s_lbu[h][:, piece * 2048:(piece + 1) * 2048],
                    [pp_h, pp_l][h][p][0:16, 0:2048],
                    reads=(pphr[p],), writes=("s_lbu",), q=q)
        p, q = pp_next()
        dma(pp_f[p][0:16, 0:2048], lbd[:, :], writes=(ppfr[p],), q=q)
        pp_split(p, 2048, rows=16)
        for h in range(2):
            dma(s_lbd[h][:, :], [pp_h, pp_l][h][p][0:16, 0:2048],
                reads=(pphr[p],), writes=("s_lbd",), q=q)

        def dequant(qsrc, ssrc, dst, n_stripes, n_cols, tile_res):
            for si in range(n_stripes):
                for half in range(n_cols // 512):
                    c0 = half * 512
                    p, q = pp_next()
                    dma(sc_r[p][:, 0:512],
                        ssrc[2 * si:2 * si + 2, c0:c0 + 512],
                        writes=(scrr[p],), q=q)
                    dma(pp_q[p][:, 0:512],
                        qsrc[si * 128:(si + 1) * 128, c0:c0 + 512],
                        writes=(ppqr[p],), q=q)
                    E("tensor", lambda e, p=p: e.matmul(
                        ps[6 + p][:, 0:512], sel2_t[:], sc_r[p][:, 0:512],
                        start=True, stop=True, skip_group_check=True),
                      (scrr[p], "sel2", f"ps{6 + p}"), (f"ps{6 + p}",))
                    E("vector", lambda e, p=p: e.scalar_tensor_tensor(
                        out=pp_f[p][:, 0:512], in0=pp_q[p][:, 0:512],
                        scalar=-7.5, in1=ps[6 + p][:, 0:512],
                        op0=AL.add, op1=AL.mult),
                      (ppqr[p], f"ps{6 + p}"), (ppfr[p], f"ps{6 + p}"))
                    pp_split(p, 512)
                    for jj in range(4):
                        j = half * 4 + jj
                        for h in range(2):
                            dma(dst[h][j][:, si * 128:(si + 1) * 128],
                                [pp_h, pp_l][h][p][:, jj * 128:(jj + 1) * 128],
                                reads=(pphr[p],), writes=(tile_res,), q=q)
        dequant(wup_q, wup_s, s_wup, NDC, HF, "s_wup")
        dequant(wdn_q, wdn_s, s_wdn, NHC, D, "s_wdn")

        p, q = pp_next()
        dma(pp_f[p][:, 0:NIC * NS], a_log[:], writes=(ppfr[p],), q=q)
        E("scalar", lambda e, p=p: e.activation(out=pp_s[p][:, 0:NIC * NS],
                                                in_=pp_f[p][:, 0:NIC * NS],
                                                func=AF.Exp),
          (ppfr[p],), (ppsr[p],))
        E("vector", lambda e, p=p: e.tensor_scalar_mul(
            out=a_buf[:], in0=pp_s[p][:, 0:NIC * NS], scalar1=-1.0),
          (ppsr[p],), ("abuf",))
        E("vector", lambda e: e.memset(state[:], 0.0), (), ("state",))
        E("vector", lambda e: e.memset(tails[:], 0.0), (), ("tails",))
        E("vector", lambda e: e.engine_nop(),
          ("res", "res1", "x2p0", "x2p1", "bb", "cb", "xst0", "xst1"),
          ("res", "res1", "x2p0", "x2p1", "bb", "cb", "xst0", "xst1"))

        def wa_slot(par):
            o = par * NDC * 128
            return ((wa_h[:, o:o + NDC * 128], wa_l[:, o:o + NDC * 128]),
                    f"wa{par}")

        def wb_slot(par, cols=8 * 128):
            o = par * 8 * 128
            return ((wb_h[:, o:o + cols], wb_l[:, o:o + cols]), f"wb{par}")

        for c in range(nch):
            tc0 = c * Q

            for dc in range(NDC):
                xp_ = dc % 2
                dma(xst[:, xp_ * Q:(xp_ + 1) * Q],
                    x_t[dc * 128:(dc + 1) * 128, tc0:tc0 + Q],
                    writes=(f"xst{xp_}",))
                split_pair(xst[:, xp_ * Q:(xp_ + 1) * Q],
                           xh_buf[:, dc * Q:(dc + 1) * Q],
                           xl_buf[:, dc * Q:(dc + 1) * Q],
                           (f"xst{xp_}",), ("xpair",))

            for ic in range(NIC):
                par = ic % 2
                (wtiles, wres) = wa_slot(par)
                for h in range(2):
                    dma(wtiles[h], s_win[h][ic], reads=("s_win",), writes=(wres,))
                pb = ps[par][:, 0:Q]
                for dc in range(NDC):
                    mm3(pb, (wtiles[0][:, dc * 128:(dc + 1) * 128],
                             wtiles[1][:, dc * 128:(dc + 1) * 128]),
                        (xh_buf[:, dc * Q:(dc + 1) * Q],
                         xl_buf[:, dc * Q:(dc + 1) * Q]),
                        first=(dc == 0), last=(dc == NDC - 1),
                        res_r=(wres, "xpair", f"ps{par}"), res_w=(f"ps{par}",))
                hb = hidt[par]
                E("scalar", lambda e, ic=ic, hb=hb: e.copy(
                    out=hb[:, 0:3], in_=tails[:, ic * 4:ic * 4 + 3]),
                  ("tails",), (f"hid{par}",))
                E("scalar", lambda e, pb=pb, hb=hb: e.copy(
                    out=hb[:, 3:3 + Q], in_=pb),
                  (f"ps{par}",), (f"hid{par}", f"ps{par}"))
                E("scalar", lambda e, ic=ic, hb=hb: e.copy(
                    out=tails[:, ic * 4:ic * 4 + 3], in_=hb[:, Q:Q + 3]),
                  (f"hid{par}",), ("tails",))
                E("vector", lambda e, ic=ic, hb=hb, par=par: e.tensor_scalar(
                    out=tB[par][:], in0=hb[:, 0:Q],
                    scalar1=convw_t[:, ic * 4:ic * 4 + 1],
                    scalar2=convb_t[:, ic:ic + 1], op0=AL.mult, op1=AL.add),
                  (f"hid{par}", "convw", "convb", f"tB{par}"), (f"tB{par}",))
                E("vector", lambda e, ic=ic, hb=hb, par=par:
                  e.scalar_tensor_tensor(
                      out=tC[par][:], in0=hb[:, 1:1 + Q],
                      scalar=convw_t[:, ic * 4 + 1:ic * 4 + 2], in1=tB[par][:],
                      op0=AL.mult, op1=AL.add),
                  (f"hid{par}", f"tB{par}", "convw", f"tC{par}"), (f"tC{par}",))
                E("vector", lambda e, ic=ic, hb=hb, par=par:
                  e.scalar_tensor_tensor(
                      out=tB[par][:], in0=hb[:, 2:2 + Q],
                      scalar=convw_t[:, ic * 4 + 2:ic * 4 + 3], in1=tC[par][:],
                      op0=AL.mult, op1=AL.add),
                  (f"hid{par}", f"tC{par}", "convw", f"tB{par}"), (f"tB{par}",))
                E("vector", lambda e, ic=ic, hb=hb, par=par:
                  e.scalar_tensor_tensor(
                      out=tC[par][:], in0=hb[:, 3:3 + Q],
                      scalar=convw_t[:, ic * 4 + 3:ic * 4 + 4], in1=tB[par][:],
                      op0=AL.mult, op1=AL.add),
                  (f"hid{par}", f"tB{par}", "convw", f"tC{par}"), (f"tC{par}",))
                E("scalar", lambda e, ic=ic, par=par: e.activation(
                    out=u_buf[:, ic * Q:(ic + 1) * Q], in_=tC[par][:],
                    func=AF.Silu), (f"tC{par}",), ("u",))
            if c == tapc:
                tap("u", u_buf[:], (128, NIC * Q), "u")

            for ic in range(NIC):
                par = ic % 2
                (wtiles, wres) = wb_slot(par, cols=R + 2 * NS)
                for h in range(2):
                    dma(wtiles[h], s_xp[h][ic], reads=("s_xp",), writes=(wres,))
                split_pair(u_buf[:, ic * Q:(ic + 1) * Q],
                           uht[par][:], ult[par][:], ("u",), (f"upair{par}",),
                           hi_eng="scalar")
                upair = (uht[par][:], ult[par][:])
                mm3(ps[2][:, 0:Q], (wtiles[0][:, 0:R], wtiles[1][:, 0:R]),
                    upair, first=(ic == 0), last=(ic == NIC - 1),
                    res_r=(wres, f"upair{par}", "ps2"), res_w=("ps2",))
                mm3(ps[3][0:NS, 0:Q],
                    (wtiles[0][:, R:R + NS], wtiles[1][:, R:R + NS]),
                    upair, first=(ic == 0), last=(ic == NIC - 1),
                    res_r=(wres, f"upair{par}", "ps3"), res_w=("ps3",))
                mm3(ps[4][0:NS, 0:Q],
                    (wtiles[0][:, R + NS:R + 2 * NS],
                     wtiles[1][:, R + NS:R + 2 * NS]),
                    upair, first=(ic == 0), last=(ic == NIC - 1),
                    res_r=(wres, f"upair{par}", "ps4"), res_w=("ps4",))
            E("scalar", lambda e: e.copy(out=tG[:], in_=ps[2][:, 0:Q]),
              ("ps2",), ("tG", "ps2"))
            E("scalar", lambda e: e.copy(out=bt_t[:], in_=ps[3][0:NS, 0:Q]),
              ("ps3",), ("btct", "ps3"))
            E("scalar", lambda e: e.copy(out=ct_t[:], in_=ps[4][0:NS, 0:Q]),
              ("ps4",), ("btct", "ps4"))
            if c == tapc:
                tap("dtin", tG[:], (128, Q), "tG")
                tap("bt", bt_t[:], (NS, Q), "btct")
                tap("ct", ct_t[:], (NS, Q), "btct")
            split_pair(tG[:], dtih[:], dtil[:], ("tG",), ("dtipair",))

            for which, src_t, dst_buf, dres in ((0, bt_t, bb_buf, "bb"),
                                                (1, ct_t, cb_buf, "cb")):
                for n in range(NS):
                    p2 = n % 2
                    g, j = n // GN, n % GN
                    E("tensor", lambda e, n=n, p2=p2, s=src_t: e.matmul(
                        ps[5 + p2][:, 0:Q], sel16_t[:, n * 128:(n + 1) * 128],
                        s[:], start=True, stop=True, skip_group_check=True),
                      ("btct", "sel16", f"ps{5 + p2}"), (f"ps{5 + p2}",))
                    E("scalar", lambda e, g=g, j=j, p2=p2, d=dst_buf: e.copy(
                        out=d[:, g * GN * Q:(g + 1) * GN * Q].rearrange(
                            "p (q n) -> p n q", n=GN)[:, j, :],
                        in_=ps[5 + p2][:, 0:Q]),
                      (f"ps{5 + p2}",), (dres, f"ps{5 + p2}"))
            if c == tapc:
                tap("bb", bb_buf[:], (128, NS * Q), "bb")

            if c == tapc:
                tap("state_in", state[:], (128, NIC * NS), "state")
            E("vector", lambda e: e.engine_nop(),
              ("res", "z0", "z1", "e0", "e1"),
              ("res", "z0", "z1", "e0", "e1"))
            for ic in range(NIC):
                par = ic % 2
                (wtiles, wres) = wb_slot(par, cols=128)
                for h in range(2):
                    dma(wtiles[h], s_dtp[h][ic], reads=("s_dtp",), writes=(wres,))
                pdt = [ps[7], ps[4]][par]
                pdtr = ["ps7", "ps4"][par]
                mm3(pdt[:, 0:Q], (wtiles[0], wtiles[1]), (dtih[:], dtil[:]),
                    first=True, last=True,
                    res_r=(wres, "dtipair", pdtr), res_w=(pdtr,))
                E("scalar", lambda e, ic=ic, pdt=pdt, par=par: e.activation(
                    out=tt1[par][:], in_=pdt[:, 0:Q], func=AF.Exp,
                    bias=dtpb_t[:, ic:ic + 1]),
                  (pdtr, "dtpb", f"tt1{par}"), (f"tt1{par}", pdtr))
                E("scalar", lambda e, par=par: e.activation(
                    out=dtt[par], in_=tt1[par][:], func=AF.Ln, bias=1.0),
                  (f"tt1{par}", f"dtt{par}"), (f"dtt{par}",))
                if c == tapc and ic == 0:
                    tap("dt0", dtt[0], (128, Q), "dtt0")
                E("vector", lambda e, ic=ic, par=par: e.tensor_tensor(
                    out=dtut[par], in0=dtt[par],
                    in1=u_buf[:, ic * Q:(ic + 1) * Q],
                    op=AL.mult), (f"dtt{par}", "u", f"dtu{par}"), (f"dtu{par}",))
                (gtiles, gres) = wa_slot(par)
                for h in range(2):
                    dma(gtiles[h], s_win[h][NIC + ic], reads=("s_win",),
                        writes=(gres,))
                pbg = ps[5 + par][:, 0:Q]
                for dc in range(NDC):
                    mm3(pbg, (gtiles[0][:, dc * 128:(dc + 1) * 128],
                              gtiles[1][:, dc * 128:(dc + 1) * 128]),
                        (xh_buf[:, dc * Q:(dc + 1) * Q],
                         xl_buf[:, dc * Q:(dc + 1) * Q]),
                        first=(dc == 0), last=(dc == NDC - 1),
                        res_r=(gres, "xpair", f"ps{5 + par}"),
                        res_w=(f"ps{5 + par}",))
                E("scalar", lambda e, p=pbg, par=par: e.activation(
                    out=get_[par], in_=p, func=AF.Exp, scale=-1.0),
                  (f"ps{5 + par}", f"ge{par}"), (f"ge{par}",))
                E("scalar", lambda e, p=pbg, par=par: e.copy(
                    out=ggt[par], in_=p),
                  (f"ps{5 + par}", f"gg{par}"), (f"gg{par}", f"ps{5 + par}"))

                gq = GN * Q

                def emit_da(g, par=par, ic=ic):
                    pg = g % 2
                    for j in range(GN):
                        n = g * GN + j
                        dst = sc1p[pg].rearrange("p (q n) -> p n q",
                                                 n=GN)[:, j, :]
                        E("scalar", lambda e, d=dst, n=n: e.activation(
                            out=d, in_=dtt[par], func=AF.Exp,
                            scale=a_buf[:, ic * NS + n:ic * NS + n + 1]),
                          (f"dtt{par}", "abuf", f"e{pg}"), (f"e{pg}",))

                emit_da(0)
                emit_da(1)
                for g in range(NG):
                    pg = g % 2
                    bb_g = bb_buf[:, g * gq:(g + 1) * gq]
                    cb_g = cb_buf[:, g * gq:(g + 1) * gq]
                    dtu_b = dtut[par].unsqueeze(2).broadcast_to([128, Q, GN])
                    E("vector", lambda e, db=dtu_b, b=bb_g, pg=pg:
                      e.tensor_tensor(out=sc0p[pg].rearrange(
                          "p (q n) -> p q n", n=GN), in0=db,
                          in1=b.rearrange("p (q n) -> p q n", n=GN),
                          op=AL.mult),
                      (f"dtu{par}", "bb", f"z{pg}", f"e{pg}"), (f"z{pg}",))
                    for j in range(GN):
                        n = g * GN + j
                        da_ap = sc1p[pg].rearrange("p (q n) -> p n q",
                                                   n=GN)[:, j, :]
                        w_ap = sc0p[pg].rearrange("p (q n) -> p n q",
                                                  n=GN)[:, j, :]
                        h_ap = scanh[:, 0:gq].rearrange("p (q n) -> p n q",
                                                        n=GN)[:, j, :]
                        E("vector", lambda e, a=da_ap, w=w_ap, o=h_ap,
                          ini=state[:, ic * NS + n:ic * NS + n + 1]:
                          e.tensor_tensor_scan(o, a, w, ini, AL.mult, AL.add),
                          (f"z{pg}", f"e{pg}", "state", "sc2"), ("sc2",))
                    E("vector", lambda e, cb=cb_g, gq=gq, pg=pg: e.tensor_tensor(
                        out=sc1p[pg], in0=scanh[:, 0:gq], in1=cb,
                        op=AL.mult), ("sc2", "cb", f"e{pg}"), (f"e{pg}",))
                    E("vector", lambda e, g=g, ic=ic, gq=gq: e.tensor_copy(
                        out=state[:, ic * NS + g * GN:ic * NS + (g + 1) * GN],
                        in_=scanh[:, gq - GN:gq]), ("sc2", "state"), ("state",))
                    E("vector", lambda e, g=g, gq=gq, pg=pg: e.tensor_reduce(
                        out=y4[:, g * Q:(g + 1) * Q],
                        in_=sc1p[pg].rearrange("p (q n) -> p q n", n=GN),
                        axis=AX.X, op=AL.add), (f"e{pg}", "y4"), ("y4",))
                    if g + 2 < NG:
                        emit_da(g + 2)
                E("vector", lambda e, par=par: e.tensor_tensor(
                    out=tt2[par][:], in0=y4[:, 0:Q], in1=y4[:, Q:2 * Q],
                    op=AL.add), ("y4", f"tt2{par}"), (f"tt2{par}",))
                E("vector", lambda e, par=par: e.tensor_tensor(
                    out=tt1[par][:], in0=y4[:, 2 * Q:3 * Q],
                    in1=y4[:, 3 * Q:4 * Q],
                    op=AL.add), ("y4", f"tt1{par}"), (f"tt1{par}",),
                  sfence=True)
                E("vector", lambda e, par=par: e.tensor_scalar_add(
                    out=dtt[par], in0=get_[par], scalar1=1.0),
                  (f"ge{par}", f"dtt{par}"), (f"dtt{par}",))
                E("vector", lambda e, par=par: e.reciprocal(
                    out=get_[par], in_=dtt[par]),
                  (f"dtt{par}", f"ge{par}"), (f"ge{par}",))
                E("vector", lambda e, par=par: e.tensor_tensor(
                    out=dtt[par], in0=ggt[par], in1=get_[par], op=AL.mult),
                  (f"gg{par}", f"ge{par}", f"dtt{par}"), (f"dtt{par}",))
                E("vector", lambda e, par=par: e.tensor_tensor(
                    out=dtut[par], in0=tt2[par][:], in1=tt1[par][:],
                    op=AL.add),
                  (f"tt2{par}", f"tt1{par}", f"dtu{par}"), (f"dtu{par}",))
                E("vector", lambda e, ic=ic, par=par: e.scalar_tensor_tensor(
                    out=tt1[par][:], in0=u_buf[:, ic * Q:(ic + 1) * Q],
                    scalar=dvec_t[:, ic:ic + 1], in1=dtut[par],
                    op0=AL.mult, op1=AL.add),
                  ("u", f"dtu{par}", "dvec", f"tt1{par}"), (f"tt1{par}",))
                E("vector", lambda e, ic=ic, par=par: e.tensor_tensor(
                    out=u_buf[:, ic * Q:(ic + 1) * Q], in0=tt1[par][:],
                    in1=dtt[par], op=AL.mult),
                  (f"tt1{par}", f"dtt{par}", "u"), ("u",))
                if c == tapc and ic == 0:
                    tap("yfin", u_buf[:, 0:Q], (128, Q), "u")
            if c == tapc:
                tap("ymall", u_buf[:], (128, NIC * Q), "u")
            E("vector", lambda e: e.engine_nop(),
              ("res", "z0", "z1", "e0", "e1"),
              ("res", "z0", "z1", "e0", "e1"))

            for grp in range(2):
                for ic in range(NIC):
                    par = ic % 2
                    split_pair(u_buf[:, ic * Q:(ic + 1) * Q],
                               yph[par][:], ypl[par][:], ("u",),
                               (f"upair{par}",), hi_eng="scalar")
                    (wtiles, wres) = wb_slot(par)
                    for h in range(2):
                        dma(wtiles[h],
                            s_opw[h][ic][:, grp * 8 * 128:(grp + 1) * 8 * 128],
                            reads=("s_opw",), writes=(wres,))
                    for j in range(8):
                        pa = ps[j][:, 0:Q]
                        mm3(pa, (wtiles[0][:, j * 128:(j + 1) * 128],
                                 wtiles[1][:, j * 128:(j + 1) * 128]),
                            (yph[par][:], ypl[par][:]),
                            first=(ic == 0), last=(ic == NIC - 1),
                            res_r=(wres, f"upair{par}", f"ps{j}"),
                            res_w=(f"ps{j}",))
                for j in range(8):
                    dc = grp * 8 + j
                    pa = ps[j][:, 0:Q]
                    E("vector", lambda e, pa=pa, dc=dc: e.tensor_tensor(
                        out=tA[0][:], in0=pa, in1=xh_buf[:, dc * Q:(dc + 1) * Q],
                        op=AL.add),
                      (f"ps{j}", "xpair", "tA0"),
                      ("tA0", f"ps{j}"))
                    E("vector", lambda e, dc=dc: e.tensor_tensor(
                        out=res_buf[:, dc * Q:(dc + 1) * Q], in0=tA[0][:],
                        in1=xl_buf[:, dc * Q:(dc + 1) * Q], op=AL.add),
                      ("tA0", "xpair"), ("res",))
            if c == tapc:
                tap("attnres", res_buf[:], (128, NDC * Q), "res")

            def layer_norm(src_buf, src_res, out_cb, psA, psB, psC):
                for dc in range(NDC):
                    dp = dc % 2
                    E("scalar", lambda e, dc=dc, dp=dp: e.activation(
                        out=tA[dp][:], in_=src_buf[:, dc * Q:(dc + 1) * Q],
                        func=AF.Square), (src_res, f"tA{dp}"), (f"tA{dp}",))
                    E("tensor", lambda e, dc=dc: e.matmul(
                        ps[psA][0:1, 0:Q], ones_t[:],
                        src_buf[:, dc * Q:(dc + 1) * Q],
                        start=(dc == 0), stop=(dc == NDC - 1),
                        skip_group_check=True),
                      (src_res, "ones", f"ps{psA}"), (f"ps{psA}",))
                    E("tensor", lambda e, dc=dc, dp=dp: e.matmul(
                        ps[psB][0:1, 0:Q], ones_t[:], tA[dp][:],
                        start=(dc == 0), stop=(dc == NDC - 1),
                        skip_group_check=True),
                      (f"tA{dp}", "ones", f"ps{psB}"), (f"ps{psB}",))
                E("scalar", lambda e: e.activation(
                    out=st1[0:1, 0:Q], in_=ps[psA][0:1, 0:Q], func=AF.Copy,
                    scale=1.0 / D), (f"ps{psA}", "y4"), ("y4", f"ps{psA}"))
                E("scalar", lambda e: e.activation(
                    out=st1[0:1, Q:2 * Q], in_=st1[0:1, 0:Q], func=AF.Square),
                  ("y4",), ("y4",))
                E("scalar", lambda e: e.activation(
                    out=st1[0:1, 2 * Q:3 * Q], in_=ps[psB][0:1, 0:Q],
                    func=AF.Copy, scale=1.0 / D),
                  (f"ps{psB}", "y4"), ("y4", f"ps{psB}"))
                E("vector", lambda e: e.tensor_tensor(
                    out=st1[0:1, 3 * Q:4 * Q], in0=st1[0:1, 2 * Q:3 * Q],
                    in1=st1[0:1, Q:2 * Q], op=AL.subtract), ("y4",), ("y4",))
                E("vector", lambda e: e.tensor_scalar_add(
                    out=st1[0:1, Q:2 * Q], in0=st1[0:1, 3 * Q:4 * Q],
                    scalar1=EPS), ("y4",), ("y4",))
                E("scalar", lambda e: e.activation(
                    out=st1[0:1, 2 * Q:3 * Q], in_=st1[0:1, Q:2 * Q],
                    func=AF.Ln), ("y4",), ("y4",))
                E("scalar", lambda e: e.activation(
                    out=st1[0:1, 3 * Q:4 * Q], in_=st1[0:1, 2 * Q:3 * Q],
                    func=AF.Exp, scale=-0.5), ("y4",), ("y4",))
                E("tensor", lambda e: e.matmul(
                    ps[psC][:, 0:Q], bc1_t[:], st1[0:1, 0:Q],
                    start=True, stop=True, skip_group_check=True),
                  ("y4", "sel16", f"ps{psC}"), (f"ps{psC}",))
                E("tensor", lambda e: e.matmul(
                    ps[psC][:, Q:2 * Q], bc1_t[:], st1[0:1, 3 * Q:4 * Q],
                    start=True, stop=True, skip_group_check=True),
                  ("y4", "sel16", f"ps{psC}"), (f"ps{psC}",))
                E("scalar", lambda e: e.copy(
                    out=mb_t, in_=ps[psC][:, 0:Q]),
                  (f"ps{psC}", "y4"), ("y4",))
                E("scalar", lambda e: e.copy(
                    out=rb_t, in_=ps[psC][:, Q:2 * Q]),
                  (f"ps{psC}", "y4"), ("y4", f"ps{psC}"))
                for dc in range(NDC):
                    dp = dc % 2
                    E("vector", lambda e, dc=dc, dp=dp: e.tensor_tensor(
                        out=tA[dp][:], in0=src_buf[:, dc * Q:(dc + 1) * Q],
                        in1=mb_t, op=AL.subtract),
                      (src_res, "y4", f"tA{dp}"), (f"tA{dp}",))
                    E("vector", lambda e, dc=dc, dp=dp: e.tensor_tensor(
                        out=tB[dp][:], in0=tA[dp][:], in1=rb_t, op=AL.mult),
                      (f"tA{dp}", "y4", f"tB{dp}"), (f"tB{dp}",))
                    out_cb(dc)

            def ln1_out(dc):
                split_pair(tB[dc % 2][:],
                           x1h_buf[:, dc * Q:(dc + 1) * Q],
                           x1l_buf[:, dc * Q:(dc + 1) * Q],
                           (f"tB{dc % 2}",), ("x1pair",))
            layer_norm(res_buf, "res", ln1_out, 4, 5, 6)
            if c == tapc:
                tap("x1", x1h_buf[:], (128, NDC * Q), "x1pair")

            if c == 0:
                for h in range(2):
                    dma(laut[h][:], s_lau[h][:, :], reads=("s_lau",),
                        writes=("laut",))

            for dc in range(NDC):
                mm3(ps[2][0:16, 0:Q],
                    (laut[0][:, dc * 16:(dc + 1) * 16],
                     laut[1][:, dc * 16:(dc + 1) * 16]),
                    (x1h_buf[:, dc * Q:(dc + 1) * Q],
                     x1l_buf[:, dc * Q:(dc + 1) * Q]),
                    first=(dc == 0), last=(dc == NDC - 1),
                    res_r=("laut", "x1pair", "ps2"), res_w=("ps2",))
            E("scalar", lambda e: e.copy(out=tG[0:16, :], in_=ps[2][0:16, 0:Q]),
              ("ps2", "tG"), ("tG", "ps2"))
            split_pair(tG[0:16, :], t1h[:], t1l[:], ("tG",), ("dtipair",))

            for hbk in range(8):
                hpar = hbk % 2
                for jj in range(8):
                    hc = hbk * 8 + jj
                    par = jj % 2
                    (wtiles, wres) = wa_slot(par)
                    for h in range(2):
                        dma(wtiles[h], s_wup[h][hc], reads=("s_wup",),
                            writes=(wres,))
                    pb = ps[par][:, 0:Q]
                    for dc in range(NDC):
                        mm3(pb, (wtiles[0][:, dc * 128:(dc + 1) * 128],
                                 wtiles[1][:, dc * 128:(dc + 1) * 128]),
                            (x1h_buf[:, dc * Q:(dc + 1) * Q],
                             x1l_buf[:, dc * Q:(dc + 1) * Q]),
                            first=(dc == 0), last=False,
                            res_r=(wres, "x1pair", f"ps{par}"),
                            res_w=(f"ps{par}",))
                    for h in range(2):
                        dma(lbut[h][:, par * 128:(par + 1) * 128],
                            s_lbu[h][:, hc * 128:(hc + 1) * 128],
                            reads=("s_lbu",), writes=(f"lbu{par}",))
                    mm3(pb, (lbut[0][:, par * 128:(par + 1) * 128],
                             lbut[1][:, par * 128:(par + 1) * 128]),
                        (t1h[:], t1l[:]), first=False, last=True,
                        res_r=(f"lbu{par}", "dtipair", f"ps{par}"),
                        res_w=(f"ps{par}",))
                    E("scalar", lambda e, pb=pb, hc=hc, par=par: e.activation(
                        out=tA[par][:], in_=pb, func=AF.Relu,
                        bias=bup_t[:, hc:hc + 1]),
                      (f"ps{par}", "bup", f"tA{par}"), (f"tA{par}", f"ps{par}"))
                    o = (hpar * 8 + jj) * Q
                    split_pair(tA[par][:], x2h[:, o:o + Q], x2l[:, o:o + Q],
                               (f"tA{par}",), (f"x2p{hpar}",), hi_eng="scalar")
                    for h in range(2):
                        dma(ladt[h][:, par * 16:(par + 1) * 16],
                            s_lad[h][:, hc * 16:(hc + 1) * 16],
                            reads=("s_lad",), writes=(f"lad{par}",))
                    mm3(ps[3][0:16, 0:Q],
                        (ladt[0][:, par * 16:(par + 1) * 16],
                         ladt[1][:, par * 16:(par + 1) * 16]),
                        (x2h[:, o:o + Q], x2l[:, o:o + Q]),
                        first=(hc == 0), last=(hc == NHC - 1),
                        res_r=(f"lad{par}", f"x2p{hpar}", "ps3"), res_w=("ps3",))
                for dc in range(NDC):
                    par = dc % 2
                    (wtiles, wres) = wb_slot(par)
                    for h in range(2):
                        dma(wtiles[h],
                            s_wdn[h][dc][:, hbk * 8 * 128:(hbk + 1) * 8 * 128],
                            reads=("s_wdn",), writes=(wres,))
                    pb = ps[4 + par][:, 0:Q]
                    for jj in range(8):
                        o = (hpar * 8 + jj) * Q
                        mm3(pb, (wtiles[0][:, jj * 128:(jj + 1) * 128],
                                 wtiles[1][:, jj * 128:(jj + 1) * 128]),
                            (x2h[:, o:o + Q], x2l[:, o:o + Q]),
                            first=(jj == 0), last=(jj == 7 and hbk != 7),
                            res_r=(wres, f"x2p{hpar}", f"ps{4 + par}"),
                            res_w=(f"ps{4 + par}",))
                    if hbk == 7:
                        if dc == 0:
                            E("scalar", lambda e: e.copy(
                                out=tG[0:16, :], in_=ps[3][0:16, 0:Q]),
                              ("ps3", "tG"), ("tG", "ps3"))
                            split_pair(tG[0:16, :], t2h[:], t2l[:], ("tG",),
                                       ("dtipair",))
                        for h in range(2):
                            dma(lbdt[h][:, par * 128:(par + 1) * 128],
                                s_lbd[h][:, dc * 128:(dc + 1) * 128],
                                reads=("s_lbd",), writes=(f"lbu{par}",))
                        mm3(pb, (lbdt[0][:, par * 128:(par + 1) * 128],
                                 lbdt[1][:, par * 128:(par + 1) * 128]),
                            (t2h[:], t2l[:]), first=False, last=True,
                            res_r=(f"lbu{par}", "dtipair", f"ps{4 + par}"),
                            res_w=(f"ps{4 + par}",))
                    cur = hbk % 2
                    prv = 1 - cur
                    dst = u_buf[:, (cur * NDC + dc) * Q:(cur * NDC + dc + 1) * Q]
                    srcp = u_buf[:, (prv * NDC + dc) * Q:(prv * NDC + dc + 1) * Q]
                    if hbk == 0:
                        E("scalar", lambda e, pb=pb, d=dst: e.copy(
                            out=d, in_=pb),
                          (f"ps{4 + par}", "u"), ("u", f"ps{4 + par}"))
                    else:
                        E("vector", lambda e, pb=pb, d=dst, s=srcp:
                          e.tensor_tensor(out=d, in0=pb, in1=s, op=AL.add),
                          (f"ps{4 + par}", "u"), ("u", f"ps{4 + par}"))
            for dc in range(NDC):
                y2fin = u_buf[:, (NDC + dc) * Q:(NDC + dc + 1) * Q]
                E("vector", lambda e, s=y2fin, dc=dc: e.tensor_scalar_add(
                    out=tA[0][:], in0=s, scalar1=bdn_t[:, dc:dc + 1]),
                  ("u", "bdn", "tA0"), ("tA0",))
                E("vector", lambda e, dc=dc: e.tensor_tensor(
                    out=tB[1][:], in0=tA[0][:],
                    in1=x1h_buf[:, dc * Q:(dc + 1) * Q], op=AL.add),
                  ("tA0", "x1pair", "tB1"), ("tB1",))
                E("vector", lambda e, dc=dc: e.tensor_tensor(
                    out=res_buf[:, dc * Q:(dc + 1) * Q], in0=tB[1][:],
                    in1=x1l_buf[:, dc * Q:(dc + 1) * Q], op=AL.add),
                  ("tB1", "x1pair"), ("res",))
            if c == tapc:
                tap("ffnres", res_buf[:], (128, NDC * Q), "res")

            def ln2_out(dc, c=c):
                E("sync", lambda e, dc=dc, c=c: e.dma_start(
                    out=out_t[dc * 128:(dc + 1) * 128, c * Q:(c + 1) * Q],
                    in_=tB[dc % 2][:]), (f"tB{dc % 2}",), ("outdma",))
            layer_norm(res_buf, "res", ln2_out, 4, 5, 6)

        A.final_waits = ["sync", "gpsimd"]
        A.assemble(nc, block)

    return nc, tap_names


def prep_inputs(inputs):
    f32 = np.float32
    g = {k: np.asarray(v) for k, v in inputs.items()}
    x = g["L0_attn_input_states"]
    base = {}
    base["w_in"] = np.ascontiguousarray(g["L0_attn_in_proj_w"], f32)
    cw = np.asarray(g["L0_attn_conv_w"], f32).reshape(IN, 4)
    base["conv_w"] = np.ascontiguousarray(
        cw.reshape(NIC, 128, 4).transpose(1, 0, 2).reshape(128, NIC * 4))
    base["conv_b"] = np.ascontiguousarray(
        np.asarray(g["L0_attn_conv_b"], f32).reshape(NIC, 128).T)
    xp = np.asarray(g["L0_attn_x_proj_w"], f32)
    base["xp_dt"] = np.ascontiguousarray(xp[:, 0:R])
    base["xp_b"] = np.ascontiguousarray(xp[:, R:R + NS])
    base["xp_c"] = np.ascontiguousarray(xp[:, R + NS:R + 2 * NS])
    base["dtp_w"] = np.ascontiguousarray(g["L0_attn_dt_proj_w"], f32)
    base["dtp_b"] = np.ascontiguousarray(
        np.asarray(g["L0_attn_dt_proj_b"], f32).reshape(NIC, 128).T)
    base["a_log"] = np.ascontiguousarray(
        np.asarray(g["L0_attn_A_log"], f32).reshape(NIC, 128, NS)
        .transpose(1, 0, 2).reshape(128, NIC * NS))
    base["d_vec"] = np.ascontiguousarray(
        np.asarray(g["L0_attn_D"], f32).reshape(NIC, 128).T)
    base["op_w"] = np.ascontiguousarray(g["L0_attn_out_proj_w"], f32)
    base["wup_q"] = np.ascontiguousarray(np.asarray(g["L0_ffn_w_up_q"]).T)
    base["wup_s"] = np.ascontiguousarray(np.asarray(g["L0_ffn_w_up_scale"], f32).T)
    base["b_up"] = np.ascontiguousarray(
        np.asarray(g["L0_ffn_b_up"], f32).reshape(NHC, 128).T)
    base["lau"] = np.ascontiguousarray(g["L0_ffn_w_up_lora_a"], f32)
    base["lbu"] = np.ascontiguousarray(g["L0_ffn_w_up_lora_b"], f32)
    base["wdn_q"] = np.ascontiguousarray(np.asarray(g["L0_ffn_w_down_q"]).T)
    base["wdn_s"] = np.ascontiguousarray(
        np.asarray(g["L0_ffn_w_down_scale"], f32).T)
    base["b_dn"] = np.ascontiguousarray(
        np.asarray(g["L0_ffn_b_down"], f32).reshape(NDC, 128).T)
    base["lad"] = np.ascontiguousarray(g["L0_ffn_w_down_lora_a"], f32)
    base["lbd"] = np.ascontiguousarray(g["L0_ffn_w_down_lora_b"], f32)
    sel16v = np.zeros((NS, NS * 128), f32)
    for n in range(NS):
        sel16v[n, n * 128:(n + 1) * 128] = 1.0
    base["sel16"] = sel16v
    sel2v = np.zeros((2, 128), f32)
    sel2v[0, 0:64] = 1.0
    sel2v[1, 64:128] = 1.0
    base["sel2"] = sel2v
    base["ones_c"] = np.ones((128, 1), f32)

    in_maps = []
    for cid in range(NCORES):
        m = dict(base)
        m["x_t"] = np.ascontiguousarray(x[cid].T.astype(f32))
        in_maps.append(m)
    return in_maps


def kernel(**inputs):
    in_maps = prep_inputs(inputs)
    nc, _ = build_program()
    res = run_bass_kernel_spmd(nc, in_maps, core_ids=list(range(NCORES)))
    out = np.stack([np.asarray(r["out_t"]).T for r in res.results])
    return np.ascontiguousarray(out).astype(np.float32)


# version 09
# speedup vs baseline: 1.2239x; 1.0019x over previous
import numpy as np
from contextlib import ExitStack
import concourse.bass as bass
import concourse.mybir as mybir
from concourse.bass_utils import run_bass_kernel_spmd

F32 = mybir.dt.float32
BF16 = mybir.dt.bfloat16
I32 = mybir.dt.int32
AL = mybir.AluOpType
AF = mybir.ActivationFunctionType
AX = mybir.AxisListType

T, D, IN, NS, R, HF = 2048, 2048, 4096, 16, 128, 8192
Q = 256
NCH = T // Q
NDC = D // 128
NIC = IN // 128
NHC = HF // 128
NG = 4
GN = NS // NG
EPS = 1e-5
NCORES = 8

ENGS = ("sync", "tensor", "vector", "scalar", "gpsimd")
DMA_ENGS = {"sync"}


class Asm:

    def __init__(self):
        self.ops = []
        self.final_waits = []

    def emit(self, eng, build, reads=(), writes=(), sfence=False, is_dma=False):
        self.ops.append([eng, build, tuple(reads), tuple(writes), False, sfence,
                         is_dma or eng in DMA_ENGS])

    def assemble(self, nc, block):
        ops = self.ops
        n = len(ops)
        last_w = {}
        readers = {}
        deps = [None] * n
        for i, (eng, _b, rd, wr, _f, sfence, _d) in enumerate(ops):
            dd = set()
            for r in rd:
                j = last_w.get(r)
                if j is not None and (ops[j][0] != eng or sfence):
                    dd.add(j)
            for r in wr:
                j = last_w.get(r)
                if j is not None and (ops[j][0] != eng or sfence):
                    dd.add(j)
                for j in readers.get(r, ()):
                    if ops[j][0] != eng or sfence:
                        dd.add(j)
            for r in rd:
                readers.setdefault(r, []).append(i)
            for r in wr:
                last_w[r] = i
                readers[r] = []
            deps[i] = dd
        for i in range(n):
            for j in deps[i]:
                ops[j][4] = True
        def key_of(i):
            eng, _b, _r, wr, _m, _sf, isd = ops[i]
            if isd:
                assert wr, f"dma op {i} needs a write resource"
                return "d:" + wr[0]
            return eng
        for op in ops:
            if op[6]:
                op[4] = True
        last_of = {}
        for i, op in enumerate(ops):
            last_of[op[0]] = i
        for e in ENGS:
            if e in last_of:
                ops[last_of[e]][4] = True
        kcount = {}
        inc_val = [0] * n
        for i in range(n):
            k = key_of(i)
            if ops[i][4]:
                kcount[k] = kcount.get(k, 0) + (16 if ops[i][6] else 1)
            inc_val[i] = kcount.get(k, 0)
        need_keys = [None] * n
        for i in range(n):
            ks = set()
            for j in deps[i]:
                ks.add(key_of(j))
            need_keys[i] = ks
        run_cnt = {}
        wait_val = [None] * n
        for i in range(n):
            if need_keys[i]:
                wv = {}
                for j in deps[i]:
                    k = key_of(j)
                    if k.startswith("d:"):
                        wv[k] = run_cnt.get(k, 0)
                    else:
                        wv[k] = max(wv.get(k, 0), inc_val[j])
                wait_val[i] = wv
            k = key_of(i)
            if ops[i][4]:
                run_cnt[k] = run_cnt.get(k, 0) + (16 if ops[i][6] else 1)
        finals = dict(run_cnt)
        all_keys = sorted(finals)
        sems = {k: nc.alloc_semaphore(f"s_{k.replace(':', '_')}")
                for k in all_keys}
        watermark = {e: {} for e in ENGS}

        def run_engine(e_obj, eng):
            wm = watermark[eng]
            for i in range(n):
                oe, build, _r, _w, marked, _sf, isd = ops[i]
                if oe != eng:
                    continue
                if wait_val[i]:
                    for k in sorted(wait_val[i]):
                        v = wait_val[i][k]
                        if v > wm.get(k, 0):
                            e_obj.wait_ge(sems[k], v)
                            wm[k] = v
                ins = build(e_obj)
                if marked:
                    ins.then_inc(sems[key_of(i)], 16 if isd else 1)
            if eng in self.final_waits:
                for k in all_keys:
                    if k != eng and finals[k] > watermark[eng].get(k, 0):
                        e_obj.wait_ge(sems[k], finals[k])
                        watermark[eng][k] = finals[k]

        for eng, starter in (("sync", block.sync), ("tensor", block.tensor),
                             ("vector", block.vector), ("scalar", block.scalar),
                             ("gpsimd", block.gpsimd)):
            starter(lambda e, eng=eng: run_engine(e, eng))


def build_program(taps=(), nch=NCH, tapc=0):
    nc = bass.Bass("TRN2", target_bir_lowering=False,
                   detect_race_conditions=False)
    A = Asm()
    tap_names = []

    def din(name, shape, dt=F32):
        return nc.dram_tensor(name, list(shape), dt, kind="ExternalInput").ap()

    x_t = din("x_t", (D, T))
    w_in = din("w_in", (D, HF))
    conv_w = din("conv_w", (128, NIC * 4))
    conv_b = din("conv_b", (128, NIC))
    xp_dt = din("xp_dt", (IN, R))
    xp_b = din("xp_b", (IN, NS))
    xp_c = din("xp_c", (IN, NS))
    dtp_w = din("dtp_w", (R, IN))
    dtp_b = din("dtp_b", (128, NIC))
    a_log = din("a_log", (128, NIC * NS))
    d_vec = din("d_vec", (128, NIC))
    op_w = din("op_w", (IN, D))
    wup_q = din("wup_q", (D, HF), I32)
    wup_s = din("wup_s", (2 * NDC, HF))
    b_up = din("b_up", (128, NHC))
    lau = din("lau", (D, 16))
    lbu = din("lbu", (16, HF))
    wdn_q = din("wdn_q", (HF, D), I32)
    wdn_s = din("wdn_s", (2 * NHC, D))
    b_dn = din("b_dn", (128, NDC))
    lad = din("lad", (HF, 16))
    lbd = din("lbd", (16, D))
    sel16 = din("sel16", (NS, NS * 128))
    sel2 = din("sel2", (2, 128))
    ones_c = din("ones_c", (128, 1))

    out_t = nc.dram_tensor("out_t", [D, T], F32, kind="ExternalOutput").ap()

    def scr(name, shape, dt=BF16):
        return nc.dram_tensor(name, list(shape), dt, kind="Internal").ap()

    s_win = [scr("s_win_h", (2 * NIC, 128, NDC * 128)),
             scr("s_win_l", (2 * NIC, 128, NDC * 128))]
    s_opw = [scr("s_opw_h", (NIC, 128, NDC * 128)),
             scr("s_opw_l", (NIC, 128, NDC * 128))]
    s_dtp = [scr("s_dtp_h", (NIC, 128, 128)), scr("s_dtp_l", (NIC, 128, 128))]
    s_xp = [scr("s_xp_h", (NIC, 128, R + 2 * NS)),
            scr("s_xp_l", (NIC, 128, R + 2 * NS))]
    s_wup = [scr("s_wup_h", (NHC, 128, NDC * 128)),
             scr("s_wup_l", (NHC, 128, NDC * 128))]
    s_wdn = [scr("s_wdn_h", (NDC, 128, NHC * 128)),
             scr("s_wdn_l", (NDC, 128, NHC * 128))]
    s_lau = [scr("s_lau_h", (128, NDC * 16)), scr("s_lau_l", (128, NDC * 16))]
    s_lbu = [scr("s_lbu_h", (16, HF)), scr("s_lbu_l", (16, HF))]
    s_lad = [scr("s_lad_h", (128, NHC * 16)), scr("s_lad_l", (128, NHC * 16))]
    s_lbd = [scr("s_lbd_h", (16, D)), scr("s_lbd_l", (16, D))]

    es = ExitStack()

    def sbt(name, shape, dt=F32):
        return es.enter_context(nc.sbuf_tensor(name, list(shape), dt))

    with es:
        u_buf = sbt("u_buf", (128, NIC * Q))
        bb_buf = sbt("bb_buf", (128, NS * Q))
        cb_buf = sbt("cb_buf", (128, NS * Q))
        scanh = sbt("scanh", (128, GN * Q))
        tmpP = sbt("tmpP", (128, GN * Q))
        tmpQ = sbt("tmpQ", (128, GN * Q))
        res_buf = sbt("res_buf", (128, NDC * Q))
        sc0p = [res_buf[:, 0:GN * Q], res_buf[:, GN * Q:2 * GN * Q]]
        sc1p = [res_buf[:, 2 * GN * Q:3 * GN * Q],
                res_buf[:, 3 * GN * Q:4 * GN * Q]]
        dtt = [tmpP[:, 0:Q], tmpP[:, Q:2 * Q]]
        dtut = [tmpP[:, 2 * Q:3 * Q], tmpP[:, 3 * Q:4 * Q]]
        get_ = [tmpQ[:, 0:Q], tmpQ[:, Q:2 * Q]]
        ggt = [tmpQ[:, 2 * Q:3 * Q], tmpQ[:, 3 * Q:4 * Q]]
        y4 = sbt("y4", (128, NG * Q))
        st1 = y4[0:1, :]
        mb_t = y4[:, 0:Q]
        rb_t = y4[:, Q:2 * Q]
        hidt = [sbt(f"hid{i}", (128, Q + 4)) for i in range(2)]
        tA = [sbt(f"tA{i}", (128, Q)) for i in range(2)]
        tB = [sbt(f"tB{i}", (128, Q)) for i in range(2)]
        tC = [sbt(f"tC{i}", (128, Q)) for i in range(2)]
        tD = sbt("tD", (128, Q))
        tE = sbt("tE", (128, Q))
        tF = sbt("tF", (128, Q))
        tG = sbt("tG", (128, Q))
        tH = sbt("tH", (128, Q))
        tt1 = [tD, tE]
        tt2 = [tF, tH]
        a_buf = sbt("a_buf", (128, NIC * NS))
        state = sbt("state", (128, NIC * NS))
        tails = sbt("tails", (128, NIC * 4))
        sel16_t = sbt("sel16_t", (NS, NS * 128))
        sel2_t = sbt("sel2_t", (2, 128))
        bc1_t = sel16_t[0:1, 0:128]
        ones_t = sbt("ones_t", (128, 1))
        convw_t = sbt("convw_t", (128, NIC * 4))
        convb_t = sbt("convb_t", (128, NIC))
        dtpb_t = sbt("dtpb_t", (128, NIC))
        dvec_t = sbt("dvec_t", (128, NIC))
        bup_t = sbt("bup_t", (128, NHC))
        bdn_t = sbt("bdn_t", (128, NDC))
        bt_t = sbt("bt_t", (NS, Q))
        ct_t = sbt("ct_t", (NS, Q))
        xst = sbt("xst", (128, 2 * Q))
        xh_buf = sbt("xh_buf", (128, NDC * Q), BF16)
        xl_buf = sbt("xl_buf", (128, NDC * Q), BF16)
        x1h_buf = sbt("x1h_buf", (128, NDC * Q), BF16)
        x1l_buf = sbt("x1l_buf", (128, NDC * Q), BF16)
        x2h = sbt("x2h", (128, 16 * Q), BF16)
        x2l = sbt("x2l", (128, 16 * Q), BF16)
        wa_h = sbt("wa_h", (128, 2 * NDC * 128), BF16)
        wa_l = sbt("wa_l", (128, 2 * NDC * 128), BF16)
        wb_h = sbt("wb_h", (128, 2 * 8 * 128), BF16)
        wb_l = sbt("wb_l", (128, 2 * 8 * 128), BF16)
        dtih = sbt("dtih", (128, Q), BF16)
        dtil = sbt("dtil", (128, Q), BF16)
        uht = [sbt(f"uh{i}", (128, Q), BF16) for i in range(2)]
        ult = [sbt(f"ul{i}", (128, Q), BF16) for i in range(2)]
        yph = uht
        ypl = ult
        t1h = dtih[0:16, :]
        t1l = dtil[0:16, :]
        t2h = dtih[0:16, :]
        t2l = dtil[0:16, :]
        laut = [sbt("laut_h", (128, NDC * 16), BF16),
                sbt("laut_l", (128, NDC * 16), BF16)]
        lbut = [sbt("lbut_h", (16, 2 * 128), BF16),
                sbt("lbut_l", (16, 2 * 128), BF16)]
        ladt = [sbt("ladt_h", (128, 2 * 16), BF16),
                sbt("ladt_l", (128, 2 * 16), BF16)]
        lbdt = lbut
        pp_f = [res_buf[:, 0:2048], res_buf[:, 2048:4096]]
        pp_h = [x2h[:, 0:2048], x2h[:, 2048:4096]]
        pp_l = [x2l[:, 0:2048], x2l[:, 2048:4096]]
        pp_s = [xst[:, 0:512], xst[:, 0:512]]
        pp_q0 = sbt("pp_q0", (128, 512), I32)
        pp_q = [pp_q0, pp_q0]
        sc_r = [bb_buf[0:2, 0:1024], cb_buf[0:2, 0:1024]]
        ppfr = ["res", "res1"]
        pphr = ["x2p0", "x2p1"]
        ppsr = ["xst0", "xst0"]
        ppqr = ["ppq0", "ppq0"]
        scrr = ["bb", "cb"]

        ps = [es.enter_context(nc.psum_tensor(f"pb{i}", [128, 512], F32))
              for i in range(8)]

        block = es.enter_context(nc.Block())

        def E(eng, build, reads=(), writes=(), sfence=False, is_dma=False):
            A.emit(eng, build, reads, writes, sfence, is_dma)

        def dma(dst, src, reads=(), writes=(), q="sync"):
            E(q, lambda e, d=dst, s=src: e.dma_start(out=d, in_=s),
              reads, writes, is_dma=True)

        def tap(name, ap, shape, res):
            if name not in taps:
                return
            h = nc.dram_tensor("tap_" + name, list(shape), ap.dtype,
                               kind="ExternalOutput").ap()
            tap_names.append(name)
            dma(h[:], ap, reads=(res,), writes=("tapdma",))

        def mm3(psum_ap, lhs, rhs, first, last, res_r, res_w, skip=True):
            lh, ll = lhs
            rh, rl = rhs
            E("tensor", lambda e, a=psum_ap, x=lh, y=rh, st=first:
              e.matmul(a, x, y, start=st, stop=False, skip_group_check=skip),
              res_r, res_w if not last else ())
            E("tensor", lambda e, a=psum_ap, x=lh, y=rl:
              e.matmul(a, x, y, start=False, stop=False, skip_group_check=skip),
              res_r, ())
            E("tensor", lambda e, a=psum_ap, x=ll, y=rh, sp=last:
              e.matmul(a, x, y, start=False, stop=sp, skip_group_check=skip),
              res_r, res_w if last else ())

        def split_pair(src_ap, dst_h, dst_l, rd, wr, hi_eng="vector"):
            E(hi_eng, lambda e, s=src_ap, d=dst_h: e.tensor_copy(out=d, in_=s)
              if hi_eng == "vector" else e.copy(out=d, in_=s), rd, wr)
            E("vector", lambda e, s=src_ap, h=dst_h, d=dst_l: e.tensor_tensor(
                out=d, in0=s, in1=h, op=AL.subtract), rd + wr, wr)

        for dst, src, tg in ((sel16_t, sel16, "sel16"), (sel2_t, sel2, "sel2"),
                             (ones_t, ones_c, "ones"),
                             (convw_t, conv_w, "convw"), (convb_t, conv_b, "convb"),
                             (dtpb_t, dtp_b, "dtpb"), (dvec_t, d_vec, "dvec"),
                             (bup_t, b_up, "bup"), (bdn_t, b_dn, "bdn")):
            dma(dst[:], src[:], writes=(tg,))

        pp_ctr = [0]

        def pp_next():
            p = pp_ctr[0] & 1
            pp_ctr[0] += 1
            return p, ("sync", "scalar")[p]

        def pp_split(p, cols, rows=128):
            split_pair(pp_f[p][0:rows, 0:cols], pp_h[p][0:rows, 0:cols],
                       pp_l[p][0:rows, 0:cols], (ppfr[p],), (pphr[p],))

        for dc in range(NDC):
            for piece in range(4):
                p, q = pp_next()
                dma(pp_f[p][:, 0:2048],
                    w_in[dc * 128:(dc + 1) * 128, piece * 2048:(piece + 1) * 2048],
                    writes=(ppfr[p],), q=q)
                pp_split(p, 2048)
                for jj in range(16):
                    jc = piece * 16 + jj
                    for h in range(2):
                        dma(s_win[h][jc][:, dc * 128:(dc + 1) * 128],
                            [pp_h, pp_l][h][p][:, jj * 128:(jj + 1) * 128],
                            reads=(pphr[p],), writes=("s_win",), q=q)
        for ic in range(NIC):
            p, q = pp_next()
            dma(pp_f[p][:, 0:2048], op_w[ic * 128:(ic + 1) * 128, :],
                writes=(ppfr[p],), q=q)
            pp_split(p, 2048)
            for h in range(2):
                dma(s_opw[h][ic], [pp_h, pp_l][h][p][:, 0:2048],
                    reads=(pphr[p],), writes=("s_opw",), q=q)
        for piece in range(2):
            p, q = pp_next()
            dma(pp_f[p][:, 0:2048], dtp_w[:, piece * 2048:(piece + 1) * 2048],
                writes=(ppfr[p],), q=q)
            pp_split(p, 2048)
            for jj in range(16):
                ic = piece * 16 + jj
                for h in range(2):
                    dma(s_dtp[h][ic],
                        [pp_h, pp_l][h][p][:, jj * 128:(jj + 1) * 128],
                        reads=(pphr[p],), writes=("s_dtp",), q=q)
        for ic in range(NIC):
            p, q = pp_next()
            dma(pp_f[p][:, 0:R], xp_dt[ic * 128:(ic + 1) * 128, :],
                writes=(ppfr[p],), q=q)
            dma(pp_f[p][:, R:R + NS], xp_b[ic * 128:(ic + 1) * 128, :],
                writes=(ppfr[p],), q=q)
            dma(pp_f[p][:, R + NS:R + 2 * NS], xp_c[ic * 128:(ic + 1) * 128, :],
                writes=(ppfr[p],), q=q)
            pp_split(p, R + 2 * NS)
            for h in range(2):
                dma(s_xp[h][ic], [pp_h, pp_l][h][p][:, 0:R + 2 * NS],
                    reads=(pphr[p],), writes=("s_xp",), q=q)
        p, q = pp_next()
        for dc in range(NDC):
            dma(pp_f[p][:, dc * 16:(dc + 1) * 16],
                lau[dc * 128:(dc + 1) * 128, :], writes=(ppfr[p],), q=q)
        pp_split(p, NDC * 16)
        for h in range(2):
            dma(s_lau[h][:, :], [pp_h, pp_l][h][p][:, 0:NDC * 16],
                reads=(pphr[p],), writes=("s_lau",), q=q)
        for half in range(2):
            p, q = pp_next()
            for j in range(NHC // 2):
                hc = half * (NHC // 2) + j
                dma(pp_f[p][:, j * 16:(j + 1) * 16],
                    lad[hc * 128:(hc + 1) * 128, :], writes=(ppfr[p],), q=q)
            pp_split(p, (NHC // 2) * 16)
            for h in range(2):
                dma(s_lad[h][:, half * (NHC // 2) * 16:
                             (half + 1) * (NHC // 2) * 16],
                    [pp_h, pp_l][h][p][:, 0:(NHC // 2) * 16],
                    reads=(pphr[p],), writes=("s_lad",), q=q)
        for piece in range(4):
            p, q = pp_next()
            dma(pp_f[p][0:16, 0:2048], lbu[:, piece * 2048:(piece + 1) * 2048],
                writes=(ppfr[p],), q=q)
            pp_split(p, 2048, rows=16)
            for h in range(2):
                dma(s_lbu[h][:, piece * 2048:(piece + 1) * 2048],
                    [pp_h, pp_l][h][p][0:16, 0:2048],
                    reads=(pphr[p],), writes=("s_lbu",), q=q)
        p, q = pp_next()
        dma(pp_f[p][0:16, 0:2048], lbd[:, :], writes=(ppfr[p],), q=q)
        pp_split(p, 2048, rows=16)
        for h in range(2):
            dma(s_lbd[h][:, :], [pp_h, pp_l][h][p][0:16, 0:2048],
                reads=(pphr[p],), writes=("s_lbd",), q=q)

        def dequant(qsrc, ssrc, dst, n_stripes, n_cols, tile_res):
            for si in range(n_stripes):
                for half in range(n_cols // 512):
                    c0 = half * 512
                    p, q = pp_next()
                    dma(sc_r[p][:, 0:512],
                        ssrc[2 * si:2 * si + 2, c0:c0 + 512],
                        writes=(scrr[p],), q=q)
                    dma(pp_q[p][:, 0:512],
                        qsrc[si * 128:(si + 1) * 128, c0:c0 + 512],
                        writes=(ppqr[p],), q=q)
                    E("tensor", lambda e, p=p: e.matmul(
                        ps[6 + p][:, 0:512], sel2_t[:], sc_r[p][:, 0:512],
                        start=True, stop=True, skip_group_check=True),
                      (scrr[p], "sel2", f"ps{6 + p}"), (f"ps{6 + p}",))
                    E("vector", lambda e, p=p: e.scalar_tensor_tensor(
                        out=pp_f[p][:, 0:512], in0=pp_q[p][:, 0:512],
                        scalar=-7.5, in1=ps[6 + p][:, 0:512],
                        op0=AL.add, op1=AL.mult),
                      (ppqr[p], f"ps{6 + p}"), (ppfr[p], f"ps{6 + p}"))
                    pp_split(p, 512)
                    for jj in range(4):
                        j = half * 4 + jj
                        for h in range(2):
                            dma(dst[h][j][:, si * 128:(si + 1) * 128],
                                [pp_h, pp_l][h][p][:, jj * 128:(jj + 1) * 128],
                                reads=(pphr[p],), writes=(tile_res,), q=q)
        dequant(wup_q, wup_s, s_wup, NDC, HF, "s_wup")
        dequant(wdn_q, wdn_s, s_wdn, NHC, D, "s_wdn")

        p, q = pp_next()
        dma(pp_f[p][:, 0:NIC * NS], a_log[:], writes=(ppfr[p],), q=q)
        E("scalar", lambda e, p=p: e.activation(out=pp_s[p][:, 0:NIC * NS],
                                                in_=pp_f[p][:, 0:NIC * NS],
                                                func=AF.Exp),
          (ppfr[p],), (ppsr[p],))
        E("vector", lambda e, p=p: e.tensor_scalar_mul(
            out=a_buf[:], in0=pp_s[p][:, 0:NIC * NS], scalar1=-1.0),
          (ppsr[p],), ("abuf",))
        E("vector", lambda e: e.memset(state[:], 0.0), (), ("state",))
        E("vector", lambda e: e.memset(tails[:], 0.0), (), ("tails",))
        E("vector", lambda e: e.engine_nop(),
          ("res", "res1", "x2p0", "x2p1", "bb", "cb", "xst0", "xst1"),
          ("res", "res1", "x2p0", "x2p1", "bb", "cb", "xst0", "xst1"))

        def wa_slot(par):
            o = par * NDC * 128
            return ((wa_h[:, o:o + NDC * 128], wa_l[:, o:o + NDC * 128]),
                    f"wa{par}")

        def wb_slot(par, cols=8 * 128):
            o = par * 8 * 128
            return ((wb_h[:, o:o + cols], wb_l[:, o:o + cols]), f"wb{par}")

        for c in range(nch):
            tc0 = c * Q

            for dc in range(NDC):
                xp_ = dc % 2
                dma(xst[:, xp_ * Q:(xp_ + 1) * Q],
                    x_t[dc * 128:(dc + 1) * 128, tc0:tc0 + Q],
                    writes=(f"xst{xp_}",))
                split_pair(xst[:, xp_ * Q:(xp_ + 1) * Q],
                           xh_buf[:, dc * Q:(dc + 1) * Q],
                           xl_buf[:, dc * Q:(dc + 1) * Q],
                           (f"xst{xp_}",), ("xpair",), hi_eng="scalar")

            for ic in range(NIC):
                par = ic % 2
                (wtiles, wres) = wa_slot(par)
                for h in range(2):
                    dma(wtiles[h], s_win[h][ic], reads=("s_win",), writes=(wres,))
                pb = ps[par][:, 0:Q]
                for dc in range(NDC):
                    mm3(pb, (wtiles[0][:, dc * 128:(dc + 1) * 128],
                             wtiles[1][:, dc * 128:(dc + 1) * 128]),
                        (xh_buf[:, dc * Q:(dc + 1) * Q],
                         xl_buf[:, dc * Q:(dc + 1) * Q]),
                        first=(dc == 0), last=(dc == NDC - 1),
                        res_r=(wres, "xpair", f"ps{par}"), res_w=(f"ps{par}",))
                hb = hidt[par]
                E("scalar", lambda e, ic=ic, hb=hb: e.copy(
                    out=hb[:, 0:3], in_=tails[:, ic * 4:ic * 4 + 3]),
                  ("tails",), (f"hid{par}",))
                E("scalar", lambda e, pb=pb, hb=hb: e.copy(
                    out=hb[:, 3:3 + Q], in_=pb),
                  (f"ps{par}",), (f"hid{par}", f"ps{par}"))
                E("scalar", lambda e, ic=ic, hb=hb: e.copy(
                    out=tails[:, ic * 4:ic * 4 + 3], in_=hb[:, Q:Q + 3]),
                  (f"hid{par}",), ("tails",))
                E("vector", lambda e, ic=ic, hb=hb, par=par: e.tensor_scalar(
                    out=tB[par][:], in0=hb[:, 0:Q],
                    scalar1=convw_t[:, ic * 4:ic * 4 + 1],
                    scalar2=convb_t[:, ic:ic + 1], op0=AL.mult, op1=AL.add),
                  (f"hid{par}", "convw", "convb", f"tB{par}"), (f"tB{par}",))
                E("vector", lambda e, ic=ic, hb=hb, par=par:
                  e.scalar_tensor_tensor(
                      out=tC[par][:], in0=hb[:, 1:1 + Q],
                      scalar=convw_t[:, ic * 4 + 1:ic * 4 + 2], in1=tB[par][:],
                      op0=AL.mult, op1=AL.add),
                  (f"hid{par}", f"tB{par}", "convw", f"tC{par}"), (f"tC{par}",))
                E("vector", lambda e, ic=ic, hb=hb, par=par:
                  e.scalar_tensor_tensor(
                      out=tB[par][:], in0=hb[:, 2:2 + Q],
                      scalar=convw_t[:, ic * 4 + 2:ic * 4 + 3], in1=tC[par][:],
                      op0=AL.mult, op1=AL.add),
                  (f"hid{par}", f"tC{par}", "convw", f"tB{par}"), (f"tB{par}",))
                E("vector", lambda e, ic=ic, hb=hb, par=par:
                  e.scalar_tensor_tensor(
                      out=tC[par][:], in0=hb[:, 3:3 + Q],
                      scalar=convw_t[:, ic * 4 + 3:ic * 4 + 4], in1=tB[par][:],
                      op0=AL.mult, op1=AL.add),
                  (f"hid{par}", f"tB{par}", "convw", f"tC{par}"), (f"tC{par}",))
                E("scalar", lambda e, ic=ic, par=par: e.activation(
                    out=u_buf[:, ic * Q:(ic + 1) * Q], in_=tC[par][:],
                    func=AF.Silu), (f"tC{par}",), ("u",))
            if c == tapc:
                tap("u", u_buf[:], (128, NIC * Q), "u")

            for ic in range(NIC):
                par = ic % 2
                (wtiles, wres) = wb_slot(par, cols=R + 2 * NS)
                for h in range(2):
                    dma(wtiles[h], s_xp[h][ic], reads=("s_xp",), writes=(wres,))
                split_pair(u_buf[:, ic * Q:(ic + 1) * Q],
                           uht[par][:], ult[par][:], ("u",), (f"upair{par}",),
                           hi_eng="scalar")
                upair = (uht[par][:], ult[par][:])
                mm3(ps[2][:, 0:Q], (wtiles[0][:, 0:R], wtiles[1][:, 0:R]),
                    upair, first=(ic == 0), last=(ic == NIC - 1),
                    res_r=(wres, f"upair{par}", "ps2"), res_w=("ps2",))
                mm3(ps[3][0:NS, 0:Q],
                    (wtiles[0][:, R:R + NS], wtiles[1][:, R:R + NS]),
                    upair, first=(ic == 0), last=(ic == NIC - 1),
                    res_r=(wres, f"upair{par}", "ps3"), res_w=("ps3",))
                mm3(ps[4][0:NS, 0:Q],
                    (wtiles[0][:, R + NS:R + 2 * NS],
                     wtiles[1][:, R + NS:R + 2 * NS]),
                    upair, first=(ic == 0), last=(ic == NIC - 1),
                    res_r=(wres, f"upair{par}", "ps4"), res_w=("ps4",))
            E("scalar", lambda e: e.copy(out=tG[:], in_=ps[2][:, 0:Q]),
              ("ps2",), ("tG", "ps2"))
            E("scalar", lambda e: e.copy(out=bt_t[:], in_=ps[3][0:NS, 0:Q]),
              ("ps3",), ("btct", "ps3"))
            E("scalar", lambda e: e.copy(out=ct_t[:], in_=ps[4][0:NS, 0:Q]),
              ("ps4",), ("btct", "ps4"))
            if c == tapc:
                tap("dtin", tG[:], (128, Q), "tG")
                tap("bt", bt_t[:], (NS, Q), "btct")
                tap("ct", ct_t[:], (NS, Q), "btct")
            split_pair(tG[:], dtih[:], dtil[:], ("tG",), ("dtipair",),
                       hi_eng="scalar")

            for which, src_t, dst_buf, dres in ((0, bt_t, bb_buf, "bb"),
                                                (1, ct_t, cb_buf, "cb")):
                for n in range(NS):
                    p2 = n % 2
                    g, j = n // GN, n % GN
                    E("tensor", lambda e, n=n, p2=p2, s=src_t: e.matmul(
                        ps[5 + p2][:, 0:Q], sel16_t[:, n * 128:(n + 1) * 128],
                        s[:], start=True, stop=True, skip_group_check=True),
                      ("btct", "sel16", f"ps{5 + p2}"), (f"ps{5 + p2}",))
                    E("scalar", lambda e, g=g, j=j, p2=p2, d=dst_buf: e.copy(
                        out=d[:, g * GN * Q:(g + 1) * GN * Q].rearrange(
                            "p (q n) -> p n q", n=GN)[:, j, :],
                        in_=ps[5 + p2][:, 0:Q]),
                      (f"ps{5 + p2}",), (dres, f"ps{5 + p2}"))
            if c == tapc:
                tap("bb", bb_buf[:], (128, NS * Q), "bb")

            if c == tapc:
                tap("state_in", state[:], (128, NIC * NS), "state")
            E("vector", lambda e: e.engine_nop(),
              ("res", "z0", "z1", "e0", "e1"),
              ("res", "z0", "z1", "e0", "e1"))
            for ic in range(NIC):
                par = ic % 2
                (wtiles, wres) = wb_slot(par, cols=128)
                for h in range(2):
                    dma(wtiles[h], s_dtp[h][ic], reads=("s_dtp",), writes=(wres,))
                pdt = [ps[7], ps[4]][par]
                pdtr = ["ps7", "ps4"][par]
                mm3(pdt[:, 0:Q], (wtiles[0], wtiles[1]), (dtih[:], dtil[:]),
                    first=True, last=True,
                    res_r=(wres, "dtipair", pdtr), res_w=(pdtr,))
                E("scalar", lambda e, ic=ic, pdt=pdt, par=par: e.activation(
                    out=tt1[par][:], in_=pdt[:, 0:Q], func=AF.Exp,
                    bias=dtpb_t[:, ic:ic + 1]),
                  (pdtr, "dtpb", f"tt1{par}"), (f"tt1{par}", pdtr))
                E("scalar", lambda e, par=par: e.activation(
                    out=dtt[par], in_=tt1[par][:], func=AF.Ln, bias=1.0),
                  (f"tt1{par}", f"dtt{par}"), (f"dtt{par}",))
                if c == tapc and ic == 0:
                    tap("dt0", dtt[0], (128, Q), "dtt0")
                E("vector", lambda e, ic=ic, par=par: e.tensor_tensor(
                    out=dtut[par], in0=dtt[par],
                    in1=u_buf[:, ic * Q:(ic + 1) * Q],
                    op=AL.mult), (f"dtt{par}", "u", f"dtu{par}"), (f"dtu{par}",))
                (gtiles, gres) = wa_slot(par)
                for h in range(2):
                    dma(gtiles[h], s_win[h][NIC + ic], reads=("s_win",),
                        writes=(gres,))
                pbg = ps[5 + par][:, 0:Q]
                for dc in range(NDC):
                    mm3(pbg, (gtiles[0][:, dc * 128:(dc + 1) * 128],
                              gtiles[1][:, dc * 128:(dc + 1) * 128]),
                        (xh_buf[:, dc * Q:(dc + 1) * Q],
                         xl_buf[:, dc * Q:(dc + 1) * Q]),
                        first=(dc == 0), last=(dc == NDC - 1),
                        res_r=(gres, "xpair", f"ps{5 + par}"),
                        res_w=(f"ps{5 + par}",))
                E("scalar", lambda e, p=pbg, par=par: e.activation(
                    out=get_[par], in_=p, func=AF.Exp, scale=-1.0),
                  (f"ps{5 + par}", f"ge{par}"), (f"ge{par}",))
                E("scalar", lambda e, p=pbg, par=par: e.copy(
                    out=ggt[par], in_=p),
                  (f"ps{5 + par}", f"gg{par}"), (f"gg{par}", f"ps{5 + par}"))

                gq = GN * Q

                def emit_da(g, par=par, ic=ic):
                    pg = g % 2
                    for j in range(GN):
                        n = g * GN + j
                        dst = sc1p[pg].rearrange("p (q n) -> p n q",
                                                 n=GN)[:, j, :]
                        E("scalar", lambda e, d=dst, n=n: e.activation(
                            out=d, in_=dtt[par], func=AF.Exp,
                            scale=a_buf[:, ic * NS + n:ic * NS + n + 1]),
                          (f"dtt{par}", "abuf", f"e{pg}"), (f"e{pg}",))

                emit_da(0)
                emit_da(1)
                for g in range(NG):
                    pg = g % 2
                    bb_g = bb_buf[:, g * gq:(g + 1) * gq]
                    cb_g = cb_buf[:, g * gq:(g + 1) * gq]
                    dtu_b = dtut[par].unsqueeze(2).broadcast_to([128, Q, GN])
                    E("vector", lambda e, db=dtu_b, b=bb_g, pg=pg:
                      e.tensor_tensor(out=sc0p[pg].rearrange(
                          "p (q n) -> p q n", n=GN), in0=db,
                          in1=b.rearrange("p (q n) -> p q n", n=GN),
                          op=AL.mult),
                      (f"dtu{par}", "bb", f"z{pg}", f"e{pg}"), (f"z{pg}",))
                    for j in range(GN):
                        n = g * GN + j
                        da_ap = sc1p[pg].rearrange("p (q n) -> p n q",
                                                   n=GN)[:, j, :]
                        w_ap = sc0p[pg].rearrange("p (q n) -> p n q",
                                                  n=GN)[:, j, :]
                        h_ap = scanh[:, 0:gq].rearrange("p (q n) -> p n q",
                                                        n=GN)[:, j, :]
                        E("vector", lambda e, a=da_ap, w=w_ap, o=h_ap,
                          ini=state[:, ic * NS + n:ic * NS + n + 1]:
                          e.tensor_tensor_scan(o, a, w, ini, AL.mult, AL.add),
                          (f"z{pg}", f"e{pg}", "state", "sc2"), ("sc2",))
                    E("vector", lambda e, cb=cb_g, gq=gq, pg=pg: e.tensor_tensor(
                        out=sc1p[pg], in0=scanh[:, 0:gq], in1=cb,
                        op=AL.mult), ("sc2", "cb", f"e{pg}"), (f"e{pg}",))
                    E("vector", lambda e, g=g, ic=ic, gq=gq: e.tensor_copy(
                        out=state[:, ic * NS + g * GN:ic * NS + (g + 1) * GN],
                        in_=scanh[:, gq - GN:gq]), ("sc2", "state"), ("state",))
                    E("gpsimd", lambda e, g=g, gq=gq, pg=pg: e.tensor_tensor(
                        out=sc0p[pg][:, 0:gq // 2],
                        in0=sc1p[pg].rearrange("p (i k) -> p i k", k=2)[:, :, 0],
                        in1=sc1p[pg].rearrange("p (i k) -> p i k", k=2)[:, :, 1],
                        op=AL.add), (f"e{pg}", f"z{pg}"), (f"z{pg}",))
                    E("gpsimd", lambda e, g=g, gq=gq, pg=pg: e.tensor_tensor(
                        out=y4[:, g * Q:(g + 1) * Q],
                        in0=sc0p[pg][:, 0:gq // 2].rearrange(
                            "p (i k) -> p i k", k=2)[:, :, 0],
                        in1=sc0p[pg][:, 0:gq // 2].rearrange(
                            "p (i k) -> p i k", k=2)[:, :, 1],
                        op=AL.add), (f"z{pg}", "y4"), ("y4",))
                    if g + 2 < NG:
                        emit_da(g + 2)
                E("vector", lambda e, par=par: e.tensor_tensor(
                    out=tt2[par][:], in0=y4[:, 0:Q], in1=y4[:, Q:2 * Q],
                    op=AL.add), ("y4", f"tt2{par}"), (f"tt2{par}",))
                E("vector", lambda e, par=par: e.tensor_tensor(
                    out=tt1[par][:], in0=y4[:, 2 * Q:3 * Q],
                    in1=y4[:, 3 * Q:4 * Q],
                    op=AL.add), ("y4", f"tt1{par}"), (f"tt1{par}",),
                  sfence=True)
                E("scalar", lambda e, par=par: e.activation(
                    out=dtt[par], in_=get_[par], func=AF.Copy, bias=1.0),
                  (f"ge{par}", f"dtt{par}"), (f"dtt{par}",))
                E("vector", lambda e, par=par: e.reciprocal(
                    out=get_[par], in_=dtt[par]),
                  (f"dtt{par}", f"ge{par}"), (f"ge{par}",))
                E("vector", lambda e, par=par: e.tensor_tensor(
                    out=dtt[par], in0=ggt[par], in1=get_[par], op=AL.mult),
                  (f"gg{par}", f"ge{par}", f"dtt{par}"), (f"dtt{par}",))
                E("vector", lambda e, par=par: e.tensor_tensor(
                    out=dtut[par], in0=tt2[par][:], in1=tt1[par][:],
                    op=AL.add),
                  (f"tt2{par}", f"tt1{par}", f"dtu{par}"), (f"dtu{par}",))
                E("vector", lambda e, ic=ic, par=par: e.scalar_tensor_tensor(
                    out=tt1[par][:], in0=u_buf[:, ic * Q:(ic + 1) * Q],
                    scalar=dvec_t[:, ic:ic + 1], in1=dtut[par],
                    op0=AL.mult, op1=AL.add),
                  ("u", f"dtu{par}", "dvec", f"tt1{par}"), (f"tt1{par}",))
                E("vector", lambda e, ic=ic, par=par: e.tensor_tensor(
                    out=u_buf[:, ic * Q:(ic + 1) * Q], in0=tt1[par][:],
                    in1=dtt[par], op=AL.mult),
                  (f"tt1{par}", f"dtt{par}", "u"), ("u",))
                if c == tapc and ic == 0:
                    tap("yfin", u_buf[:, 0:Q], (128, Q), "u")
            if c == tapc:
                tap("ymall", u_buf[:], (128, NIC * Q), "u")
            E("vector", lambda e: e.engine_nop(),
              ("res", "z0", "z1", "e0", "e1"),
              ("res", "z0", "z1", "e0", "e1"))

            for grp in range(2):
                for ic in range(NIC):
                    par = ic % 2
                    split_pair(u_buf[:, ic * Q:(ic + 1) * Q],
                               yph[par][:], ypl[par][:], ("u",),
                               (f"upair{par}",), hi_eng="scalar")
                    (wtiles, wres) = wb_slot(par)
                    for h in range(2):
                        dma(wtiles[h],
                            s_opw[h][ic][:, grp * 8 * 128:(grp + 1) * 8 * 128],
                            reads=("s_opw",), writes=(wres,))
                    for j in range(8):
                        pa = ps[j][:, 0:Q]
                        mm3(pa, (wtiles[0][:, j * 128:(j + 1) * 128],
                                 wtiles[1][:, j * 128:(j + 1) * 128]),
                            (yph[par][:], ypl[par][:]),
                            first=(ic == 0), last=(ic == NIC - 1),
                            res_r=(wres, f"upair{par}", f"ps{j}"),
                            res_w=(f"ps{j}",))
                for j in range(8):
                    dc = grp * 8 + j
                    pa = ps[j][:, 0:Q]
                    E("vector", lambda e, pa=pa, dc=dc: e.tensor_tensor(
                        out=tA[0][:], in0=pa, in1=xh_buf[:, dc * Q:(dc + 1) * Q],
                        op=AL.add),
                      (f"ps{j}", "xpair", "tA0"),
                      ("tA0", f"ps{j}"))
                    E("vector", lambda e, dc=dc: e.tensor_tensor(
                        out=res_buf[:, dc * Q:(dc + 1) * Q], in0=tA[0][:],
                        in1=xl_buf[:, dc * Q:(dc + 1) * Q], op=AL.add),
                      ("tA0", "xpair"), ("res",))
            if c == tapc:
                tap("attnres", res_buf[:], (128, NDC * Q), "res")

            def layer_norm(src_buf, src_res, out_cb, psA, psB, psC):
                for dc in range(NDC):
                    dp = dc % 2
                    E("scalar", lambda e, dc=dc, dp=dp: e.activation(
                        out=tA[dp][:], in_=src_buf[:, dc * Q:(dc + 1) * Q],
                        func=AF.Square), (src_res, f"tA{dp}"), (f"tA{dp}",))
                    E("tensor", lambda e, dc=dc: e.matmul(
                        ps[psA][0:1, 0:Q], ones_t[:],
                        src_buf[:, dc * Q:(dc + 1) * Q],
                        start=(dc == 0), stop=(dc == NDC - 1),
                        skip_group_check=True),
                      (src_res, "ones", f"ps{psA}"), (f"ps{psA}",))
                    E("tensor", lambda e, dc=dc, dp=dp: e.matmul(
                        ps[psB][0:1, 0:Q], ones_t[:], tA[dp][:],
                        start=(dc == 0), stop=(dc == NDC - 1),
                        skip_group_check=True),
                      (f"tA{dp}", "ones", f"ps{psB}"), (f"ps{psB}",))
                E("scalar", lambda e: e.activation(
                    out=st1[0:1, 0:Q], in_=ps[psA][0:1, 0:Q], func=AF.Copy,
                    scale=1.0 / D), (f"ps{psA}", "y4"), ("y4", f"ps{psA}"))
                E("scalar", lambda e: e.activation(
                    out=st1[0:1, Q:2 * Q], in_=st1[0:1, 0:Q], func=AF.Square),
                  ("y4",), ("y4",))
                E("scalar", lambda e: e.activation(
                    out=st1[0:1, 2 * Q:3 * Q], in_=ps[psB][0:1, 0:Q],
                    func=AF.Copy, scale=1.0 / D),
                  (f"ps{psB}", "y4"), ("y4", f"ps{psB}"))
                E("vector", lambda e: e.tensor_tensor(
                    out=st1[0:1, 3 * Q:4 * Q], in0=st1[0:1, 2 * Q:3 * Q],
                    in1=st1[0:1, Q:2 * Q], op=AL.subtract), ("y4",), ("y4",))
                E("vector", lambda e: e.tensor_scalar_add(
                    out=st1[0:1, Q:2 * Q], in0=st1[0:1, 3 * Q:4 * Q],
                    scalar1=EPS), ("y4",), ("y4",))
                E("scalar", lambda e: e.activation(
                    out=st1[0:1, 2 * Q:3 * Q], in_=st1[0:1, Q:2 * Q],
                    func=AF.Ln), ("y4",), ("y4",))
                E("scalar", lambda e: e.activation(
                    out=st1[0:1, 3 * Q:4 * Q], in_=st1[0:1, 2 * Q:3 * Q],
                    func=AF.Exp, scale=-0.5), ("y4",), ("y4",))
                E("tensor", lambda e: e.matmul(
                    ps[psC][:, 0:Q], bc1_t[:], st1[0:1, 0:Q],
                    start=True, stop=True, skip_group_check=True),
                  ("y4", "sel16", f"ps{psC}"), (f"ps{psC}",))
                E("tensor", lambda e: e.matmul(
                    ps[psC][:, Q:2 * Q], bc1_t[:], st1[0:1, 3 * Q:4 * Q],
                    start=True, stop=True, skip_group_check=True),
                  ("y4", "sel16", f"ps{psC}"), (f"ps{psC}",))
                E("scalar", lambda e: e.copy(
                    out=mb_t, in_=ps[psC][:, 0:Q]),
                  (f"ps{psC}", "y4"), ("y4",))
                E("scalar", lambda e: e.copy(
                    out=rb_t, in_=ps[psC][:, Q:2 * Q]),
                  (f"ps{psC}", "y4"), ("y4", f"ps{psC}"))
                for dc in range(NDC):
                    dp = dc % 2
                    E("vector", lambda e, dc=dc, dp=dp: e.tensor_tensor(
                        out=tA[dp][:], in0=src_buf[:, dc * Q:(dc + 1) * Q],
                        in1=mb_t, op=AL.subtract),
                      (src_res, "y4", f"tA{dp}"), (f"tA{dp}",))
                    E("vector", lambda e, dc=dc, dp=dp: e.tensor_tensor(
                        out=tB[dp][:], in0=tA[dp][:], in1=rb_t, op=AL.mult),
                      (f"tA{dp}", "y4", f"tB{dp}"), (f"tB{dp}",))
                    out_cb(dc)

            def ln1_out(dc):
                split_pair(tB[dc % 2][:],
                           x1h_buf[:, dc * Q:(dc + 1) * Q],
                           x1l_buf[:, dc * Q:(dc + 1) * Q],
                           (f"tB{dc % 2}",), ("x1pair",), hi_eng="scalar")
            layer_norm(res_buf, "res", ln1_out, 4, 5, 6)
            if c == tapc:
                tap("x1", x1h_buf[:], (128, NDC * Q), "x1pair")

            if c == 0:
                for h in range(2):
                    dma(laut[h][:], s_lau[h][:, :], reads=("s_lau",),
                        writes=("laut",))

            for dc in range(NDC):
                mm3(ps[2][0:16, 0:Q],
                    (laut[0][:, dc * 16:(dc + 1) * 16],
                     laut[1][:, dc * 16:(dc + 1) * 16]),
                    (x1h_buf[:, dc * Q:(dc + 1) * Q],
                     x1l_buf[:, dc * Q:(dc + 1) * Q]),
                    first=(dc == 0), last=(dc == NDC - 1),
                    res_r=("laut", "x1pair", "ps2"), res_w=("ps2",))
            E("scalar", lambda e: e.copy(out=tG[0:16, :], in_=ps[2][0:16, 0:Q]),
              ("ps2", "tG"), ("tG", "ps2"))
            split_pair(tG[0:16, :], t1h[:], t1l[:], ("tG",), ("dtipair",),
                       hi_eng="scalar")

            for hbk in range(8):
                hpar = hbk % 2
                for jj in range(8):
                    hc = hbk * 8 + jj
                    par = jj % 2
                    (wtiles, wres) = wa_slot(par)
                    for h in range(2):
                        dma(wtiles[h], s_wup[h][hc], reads=("s_wup",),
                            writes=(wres,))
                    pb = ps[par][:, 0:Q]
                    for dc in range(NDC):
                        mm3(pb, (wtiles[0][:, dc * 128:(dc + 1) * 128],
                                 wtiles[1][:, dc * 128:(dc + 1) * 128]),
                            (x1h_buf[:, dc * Q:(dc + 1) * Q],
                             x1l_buf[:, dc * Q:(dc + 1) * Q]),
                            first=(dc == 0), last=False,
                            res_r=(wres, "x1pair", f"ps{par}"),
                            res_w=(f"ps{par}",))
                    for h in range(2):
                        dma(lbut[h][:, par * 128:(par + 1) * 128],
                            s_lbu[h][:, hc * 128:(hc + 1) * 128],
                            reads=("s_lbu",), writes=(f"lbu{par}",))
                    mm3(pb, (lbut[0][:, par * 128:(par + 1) * 128],
                             lbut[1][:, par * 128:(par + 1) * 128]),
                        (t1h[:], t1l[:]), first=False, last=True,
                        res_r=(f"lbu{par}", "dtipair", f"ps{par}"),
                        res_w=(f"ps{par}",))
                    E("scalar", lambda e, pb=pb, hc=hc, par=par: e.activation(
                        out=tA[par][:], in_=pb, func=AF.Relu,
                        bias=bup_t[:, hc:hc + 1]),
                      (f"ps{par}", "bup", f"tA{par}"), (f"tA{par}", f"ps{par}"))
                    o = (hpar * 8 + jj) * Q
                    split_pair(tA[par][:], x2h[:, o:o + Q], x2l[:, o:o + Q],
                               (f"tA{par}",), (f"x2p{hpar}",), hi_eng="scalar")
                    for h in range(2):
                        dma(ladt[h][:, par * 16:(par + 1) * 16],
                            s_lad[h][:, hc * 16:(hc + 1) * 16],
                            reads=("s_lad",), writes=(f"lad{par}",))
                    mm3(ps[3][0:16, 0:Q],
                        (ladt[0][:, par * 16:(par + 1) * 16],
                         ladt[1][:, par * 16:(par + 1) * 16]),
                        (x2h[:, o:o + Q], x2l[:, o:o + Q]),
                        first=(hc == 0), last=(hc == NHC - 1),
                        res_r=(f"lad{par}", f"x2p{hpar}", "ps3"), res_w=("ps3",))
                for dc in range(NDC):
                    par = dc % 2
                    (wtiles, wres) = wb_slot(par)
                    for h in range(2):
                        dma(wtiles[h],
                            s_wdn[h][dc][:, hbk * 8 * 128:(hbk + 1) * 8 * 128],
                            reads=("s_wdn",), writes=(wres,))
                    pb = ps[4 + par][:, 0:Q]
                    for jj in range(8):
                        o = (hpar * 8 + jj) * Q
                        mm3(pb, (wtiles[0][:, jj * 128:(jj + 1) * 128],
                                 wtiles[1][:, jj * 128:(jj + 1) * 128]),
                            (x2h[:, o:o + Q], x2l[:, o:o + Q]),
                            first=(jj == 0), last=(jj == 7 and hbk != 7),
                            res_r=(wres, f"x2p{hpar}", f"ps{4 + par}"),
                            res_w=(f"ps{4 + par}",))
                    if hbk == 7:
                        if dc == 0:
                            E("scalar", lambda e: e.copy(
                                out=tG[0:16, :], in_=ps[3][0:16, 0:Q]),
                              ("ps3", "tG"), ("tG", "ps3"))
                            split_pair(tG[0:16, :], t2h[:], t2l[:], ("tG",),
                                       ("dtipair",), hi_eng="scalar")
                        for h in range(2):
                            dma(lbdt[h][:, par * 128:(par + 1) * 128],
                                s_lbd[h][:, dc * 128:(dc + 1) * 128],
                                reads=("s_lbd",), writes=(f"lbu{par}",))
                        mm3(pb, (lbdt[0][:, par * 128:(par + 1) * 128],
                                 lbdt[1][:, par * 128:(par + 1) * 128]),
                            (t2h[:], t2l[:]), first=False, last=True,
                            res_r=(f"lbu{par}", "dtipair", f"ps{4 + par}"),
                            res_w=(f"ps{4 + par}",))
                    cur = hbk % 2
                    prv = 1 - cur
                    dst = u_buf[:, (cur * NDC + dc) * Q:(cur * NDC + dc + 1) * Q]
                    srcp = u_buf[:, (prv * NDC + dc) * Q:(prv * NDC + dc + 1) * Q]
                    if hbk == 0:
                        E("scalar", lambda e, pb=pb, d=dst: e.copy(
                            out=d, in_=pb),
                          (f"ps{4 + par}", "u"), ("u", f"ps{4 + par}"))
                    else:
                        E("vector", lambda e, pb=pb, d=dst, s=srcp:
                          e.tensor_tensor(out=d, in0=pb, in1=s, op=AL.add),
                          (f"ps{4 + par}", "u"), ("u", f"ps{4 + par}"))
            for dc in range(NDC):
                y2fin = u_buf[:, (NDC + dc) * Q:(NDC + dc + 1) * Q]
                E("vector", lambda e, s=y2fin, dc=dc: e.tensor_scalar_add(
                    out=tA[0][:], in0=s, scalar1=bdn_t[:, dc:dc + 1]),
                  ("u", "bdn", "tA0"), ("tA0",))
                E("vector", lambda e, dc=dc: e.tensor_tensor(
                    out=tB[1][:], in0=tA[0][:],
                    in1=x1h_buf[:, dc * Q:(dc + 1) * Q], op=AL.add),
                  ("tA0", "x1pair", "tB1"), ("tB1",))
                E("vector", lambda e, dc=dc: e.tensor_tensor(
                    out=res_buf[:, dc * Q:(dc + 1) * Q], in0=tB[1][:],
                    in1=x1l_buf[:, dc * Q:(dc + 1) * Q], op=AL.add),
                  ("tB1", "x1pair"), ("res",))
            if c == tapc:
                tap("ffnres", res_buf[:], (128, NDC * Q), "res")

            def ln2_out(dc, c=c):
                E("sync", lambda e, dc=dc, c=c: e.dma_start(
                    out=out_t[dc * 128:(dc + 1) * 128, c * Q:(c + 1) * Q],
                    in_=tB[dc % 2][:]), (f"tB{dc % 2}",), ("outdma",))
            layer_norm(res_buf, "res", ln2_out, 4, 5, 6)

        A.final_waits = ["sync", "gpsimd"]
        A.assemble(nc, block)

    return nc, tap_names


def prep_inputs(inputs):
    f32 = np.float32
    g = {k: np.asarray(v) for k, v in inputs.items()}
    x = g["L0_attn_input_states"]
    base = {}
    base["w_in"] = np.ascontiguousarray(g["L0_attn_in_proj_w"], f32)
    cw = np.asarray(g["L0_attn_conv_w"], f32).reshape(IN, 4)
    base["conv_w"] = np.ascontiguousarray(
        cw.reshape(NIC, 128, 4).transpose(1, 0, 2).reshape(128, NIC * 4))
    base["conv_b"] = np.ascontiguousarray(
        np.asarray(g["L0_attn_conv_b"], f32).reshape(NIC, 128).T)
    xp = np.asarray(g["L0_attn_x_proj_w"], f32)
    base["xp_dt"] = np.ascontiguousarray(xp[:, 0:R])
    base["xp_b"] = np.ascontiguousarray(xp[:, R:R + NS])
    base["xp_c"] = np.ascontiguousarray(xp[:, R + NS:R + 2 * NS])
    base["dtp_w"] = np.ascontiguousarray(g["L0_attn_dt_proj_w"], f32)
    base["dtp_b"] = np.ascontiguousarray(
        np.asarray(g["L0_attn_dt_proj_b"], f32).reshape(NIC, 128).T)
    base["a_log"] = np.ascontiguousarray(
        np.asarray(g["L0_attn_A_log"], f32).reshape(NIC, 128, NS)
        .transpose(1, 0, 2).reshape(128, NIC * NS))
    base["d_vec"] = np.ascontiguousarray(
        np.asarray(g["L0_attn_D"], f32).reshape(NIC, 128).T)
    base["op_w"] = np.ascontiguousarray(g["L0_attn_out_proj_w"], f32)
    base["wup_q"] = np.ascontiguousarray(np.asarray(g["L0_ffn_w_up_q"]).T)
    base["wup_s"] = np.ascontiguousarray(np.asarray(g["L0_ffn_w_up_scale"], f32).T)
    base["b_up"] = np.ascontiguousarray(
        np.asarray(g["L0_ffn_b_up"], f32).reshape(NHC, 128).T)
    base["lau"] = np.ascontiguousarray(g["L0_ffn_w_up_lora_a"], f32)
    base["lbu"] = np.ascontiguousarray(g["L0_ffn_w_up_lora_b"], f32)
    base["wdn_q"] = np.ascontiguousarray(np.asarray(g["L0_ffn_w_down_q"]).T)
    base["wdn_s"] = np.ascontiguousarray(
        np.asarray(g["L0_ffn_w_down_scale"], f32).T)
    base["b_dn"] = np.ascontiguousarray(
        np.asarray(g["L0_ffn_b_down"], f32).reshape(NDC, 128).T)
    base["lad"] = np.ascontiguousarray(g["L0_ffn_w_down_lora_a"], f32)
    base["lbd"] = np.ascontiguousarray(g["L0_ffn_w_down_lora_b"], f32)
    sel16v = np.zeros((NS, NS * 128), f32)
    for n in range(NS):
        sel16v[n, n * 128:(n + 1) * 128] = 1.0
    base["sel16"] = sel16v
    sel2v = np.zeros((2, 128), f32)
    sel2v[0, 0:64] = 1.0
    sel2v[1, 64:128] = 1.0
    base["sel2"] = sel2v
    base["ones_c"] = np.ones((128, 1), f32)

    in_maps = []
    for cid in range(NCORES):
        m = dict(base)
        m["x_t"] = np.ascontiguousarray(x[cid].T.astype(f32))
        in_maps.append(m)
    return in_maps


def kernel(**inputs):
    in_maps = prep_inputs(inputs)
    nc, _ = build_program()
    res = run_bass_kernel_spmd(nc, in_maps, core_ids=list(range(NCORES)))
    out = np.stack([np.asarray(r["out_t"]).T for r in res.results])
    return np.ascontiguousarray(out).astype(np.float32)


# version 10
# speedup vs baseline: 1.2456x; 1.0177x over previous
import numpy as np
from contextlib import ExitStack
import concourse.bass as bass
import concourse.mybir as mybir
from concourse.bass_utils import run_bass_kernel_spmd

F32 = mybir.dt.float32
BF16 = mybir.dt.bfloat16
I32 = mybir.dt.int32
AL = mybir.AluOpType
AF = mybir.ActivationFunctionType
AX = mybir.AxisListType

T, D, IN, NS, R, HF = 2048, 2048, 4096, 16, 128, 8192
Q = 256
NCH = T // Q
NDC = D // 128
NIC = IN // 128
NHC = HF // 128
NG = 4
GN = NS // NG
EPS = 1e-5
NCORES = 8

ENGS = ("sync", "tensor", "vector", "scalar", "gpsimd")
DMA_ENGS = {"sync"}


class Asm:

    def __init__(self):
        self.ops = []
        self.final_waits = []

    def emit(self, eng, build, reads=(), writes=(), sfence=False, is_dma=False):
        self.ops.append([eng, build, tuple(reads), tuple(writes), False, sfence,
                         is_dma or eng in DMA_ENGS])

    def assemble(self, nc, block):
        ops = self.ops
        n = len(ops)
        last_w = {}
        readers = {}
        deps = [None] * n
        for i, (eng, _b, rd, wr, _f, sfence, _d) in enumerate(ops):
            dd = set()
            for r in rd:
                j = last_w.get(r)
                if j is not None and (ops[j][0] != eng or sfence):
                    dd.add(j)
            for r in wr:
                j = last_w.get(r)
                if j is not None and (ops[j][0] != eng or sfence):
                    dd.add(j)
                for j in readers.get(r, ()):
                    if ops[j][0] != eng or sfence:
                        dd.add(j)
            for r in rd:
                readers.setdefault(r, []).append(i)
            for r in wr:
                last_w[r] = i
                readers[r] = []
            deps[i] = dd
        for i in range(n):
            for j in deps[i]:
                ops[j][4] = True
        def key_of(i):
            eng, _b, _r, wr, _m, _sf, isd = ops[i]
            if isd:
                assert wr, f"dma op {i} needs a write resource"
                return "d:" + wr[0]
            return eng
        for op in ops:
            if op[6]:
                op[4] = True
        last_of = {}
        for i, op in enumerate(ops):
            last_of[op[0]] = i
        for e in ENGS:
            if e in last_of:
                ops[last_of[e]][4] = True
        kcount = {}
        inc_val = [0] * n
        for i in range(n):
            k = key_of(i)
            if ops[i][4]:
                kcount[k] = kcount.get(k, 0) + (16 if ops[i][6] else 1)
            inc_val[i] = kcount.get(k, 0)
        need_keys = [None] * n
        for i in range(n):
            ks = set()
            for j in deps[i]:
                ks.add(key_of(j))
            need_keys[i] = ks
        run_cnt = {}
        wait_val = [None] * n
        for i in range(n):
            if need_keys[i]:
                wv = {}
                for j in deps[i]:
                    k = key_of(j)
                    if k.startswith("d:"):
                        wv[k] = run_cnt.get(k, 0)
                    else:
                        wv[k] = max(wv.get(k, 0), inc_val[j])
                wait_val[i] = wv
            k = key_of(i)
            if ops[i][4]:
                run_cnt[k] = run_cnt.get(k, 0) + (16 if ops[i][6] else 1)
        finals = dict(run_cnt)
        all_keys = sorted(finals)
        sems = {k: nc.alloc_semaphore(f"s_{k.replace(':', '_')}")
                for k in all_keys}
        watermark = {e: {} for e in ENGS}

        def run_engine(e_obj, eng):
            wm = watermark[eng]
            for i in range(n):
                oe, build, _r, _w, marked, _sf, isd = ops[i]
                if oe != eng:
                    continue
                if wait_val[i]:
                    for k in sorted(wait_val[i]):
                        v = wait_val[i][k]
                        if v > wm.get(k, 0):
                            e_obj.wait_ge(sems[k], v)
                            wm[k] = v
                ins = build(e_obj)
                if marked:
                    ins.then_inc(sems[key_of(i)], 16 if isd else 1)
            if eng in self.final_waits:
                for k in all_keys:
                    if k != eng and finals[k] > watermark[eng].get(k, 0):
                        e_obj.wait_ge(sems[k], finals[k])
                        watermark[eng][k] = finals[k]

        for eng, starter in (("sync", block.sync), ("tensor", block.tensor),
                             ("vector", block.vector), ("scalar", block.scalar),
                             ("gpsimd", block.gpsimd)):
            starter(lambda e, eng=eng: run_engine(e, eng))


def build_program(taps=(), nch=NCH, tapc=0):
    nc = bass.Bass("TRN2", target_bir_lowering=False,
                   detect_race_conditions=False)
    A = Asm()
    tap_names = []

    def din(name, shape, dt=F32):
        return nc.dram_tensor(name, list(shape), dt, kind="ExternalInput").ap()

    x_t = din("x_t", (D, T))
    w_in = din("w_in", (D, HF))
    conv_w = din("conv_w", (128, NIC * 4))
    conv_b = din("conv_b", (128, NIC))
    xp_dt = din("xp_dt", (IN, R))
    xp_b = din("xp_b", (IN, NS))
    xp_c = din("xp_c", (IN, NS))
    dtp_w = din("dtp_w", (R, IN))
    dtp_b = din("dtp_b", (128, NIC))
    a_log = din("a_log", (128, NIC * NS))
    d_vec = din("d_vec", (128, NIC))
    op_w = din("op_w", (IN, D))
    wup_q = din("wup_q", (D, HF), I32)
    wup_s = din("wup_s", (2 * NDC, HF))
    b_up = din("b_up", (128, NHC))
    lau = din("lau", (D, 16))
    lbu = din("lbu", (16, HF))
    wdn_q = din("wdn_q", (HF, D), I32)
    wdn_s = din("wdn_s", (2 * NHC, D))
    b_dn = din("b_dn", (128, NDC))
    lad = din("lad", (HF, 16))
    lbd = din("lbd", (16, D))
    sel16 = din("sel16", (NS, NS * 128))
    sel2 = din("sel2", (2, 128))
    ones_c = din("ones_c", (128, 1))

    out_t = nc.dram_tensor("out_t", [D, T], F32, kind="ExternalOutput").ap()

    def scr(name, shape, dt=BF16):
        return nc.dram_tensor(name, list(shape), dt, kind="Internal").ap()

    s_win = [scr("s_win_h", (2 * NIC, 128, NDC * 128)),
             scr("s_win_l", (2 * NIC, 128, NDC * 128))]
    s_opw = [scr("s_opw_h", (NIC, 128, NDC * 128)),
             scr("s_opw_l", (NIC, 128, NDC * 128))]
    s_dtp = [scr("s_dtp_h", (NIC, 128, 128)), scr("s_dtp_l", (NIC, 128, 128))]
    s_xp = [scr("s_xp_h", (NIC, 128, R + 2 * NS)),
            scr("s_xp_l", (NIC, 128, R + 2 * NS))]
    s_wup = [scr("s_wup_h", (NHC, 128, NDC * 128)),
             scr("s_wup_l", (NHC, 128, NDC * 128))]
    s_wdn = [scr("s_wdn_h", (NDC, 128, NHC * 128)),
             scr("s_wdn_l", (NDC, 128, NHC * 128))]
    s_lau = [scr("s_lau_h", (128, NDC * 16)), scr("s_lau_l", (128, NDC * 16))]
    s_lbu = [scr("s_lbu_h", (16, HF)), scr("s_lbu_l", (16, HF))]
    s_lad = [scr("s_lad_h", (128, NHC * 16)), scr("s_lad_l", (128, NHC * 16))]
    s_lbd = [scr("s_lbd_h", (16, D)), scr("s_lbd_l", (16, D))]

    es = ExitStack()

    def sbt(name, shape, dt=F32):
        return es.enter_context(nc.sbuf_tensor(name, list(shape), dt))

    with es:
        u_buf = sbt("u_buf", (128, NIC * Q))
        bb_buf = sbt("bb_buf", (128, NS * Q))
        cb_buf = sbt("cb_buf", (128, NS * Q))
        scanh = sbt("scanh", (128, GN * Q))
        tmpP = sbt("tmpP", (128, GN * Q))
        tmpQ = sbt("tmpQ", (128, GN * Q))
        res_buf = sbt("res_buf", (128, NDC * Q))
        sc0p = [res_buf[:, 0:GN * Q], res_buf[:, GN * Q:2 * GN * Q]]
        sc1p = [res_buf[:, 2 * GN * Q:3 * GN * Q],
                res_buf[:, 3 * GN * Q:4 * GN * Q]]
        dtt = [tmpP[:, 0:Q], tmpP[:, Q:2 * Q]]
        dtut = [tmpP[:, 2 * Q:3 * Q], tmpP[:, 3 * Q:4 * Q]]
        get_ = [tmpQ[:, 0:Q], tmpQ[:, Q:2 * Q]]
        ggt = [tmpQ[:, 2 * Q:3 * Q], tmpQ[:, 3 * Q:4 * Q]]
        y4 = sbt("y4", (128, NG * Q))
        st1 = y4[0:1, :]
        mb_t = y4[:, 0:Q]
        rb_t = y4[:, Q:2 * Q]
        hidt = [sbt(f"hid{i}", (128, Q + 4)) for i in range(2)]
        tA = [sbt(f"tA{i}", (128, Q)) for i in range(2)]
        tB = [sbt(f"tB{i}", (128, Q)) for i in range(2)]
        tC = [sbt(f"tC{i}", (128, Q)) for i in range(2)]
        tD = sbt("tD", (128, Q))
        tE = sbt("tE", (128, Q))
        tF = sbt("tF", (128, Q))
        tG = sbt("tG", (128, Q))
        tH = sbt("tH", (128, Q))
        tt1 = [tD, tE]
        tt2 = [tF, tH]
        a_buf = sbt("a_buf", (128, NIC * NS))
        state = sbt("state", (128, NIC * NS))
        tails = sbt("tails", (128, NIC * 4))
        sel16_t = sbt("sel16_t", (NS, NS * 128))
        sel2_t = sbt("sel2_t", (2, 128))
        bc1_t = sel16_t[0:1, 0:128]
        ones_t = sbt("ones_t", (128, 1))
        convw_t = sbt("convw_t", (128, NIC * 4))
        convb_t = sbt("convb_t", (128, NIC))
        dtpb_t = sbt("dtpb_t", (128, NIC))
        dvec_t = sbt("dvec_t", (128, NIC))
        bup_t = sbt("bup_t", (128, NHC))
        bdn_t = sbt("bdn_t", (128, NDC))
        bt_t = sbt("bt_t", (NS, Q))
        ct_t = sbt("ct_t", (NS, Q))
        xst = sbt("xst", (128, 2 * Q))
        xh_buf = sbt("xh_buf", (128, NDC * Q), BF16)
        xl_buf = sbt("xl_buf", (128, NDC * Q), BF16)
        x1h_buf = sbt("x1h_buf", (128, NDC * Q), BF16)
        x1l_buf = sbt("x1l_buf", (128, NDC * Q), BF16)
        x2h = sbt("x2h", (128, 16 * Q), BF16)
        x2l = sbt("x2l", (128, 16 * Q), BF16)
        wa_h = sbt("wa_h", (128, 2 * NDC * 128), BF16)
        wa_l = sbt("wa_l", (128, 2 * NDC * 128), BF16)
        wb_h = sbt("wb_h", (128, 2 * 8 * 128), BF16)
        wb_l = sbt("wb_l", (128, 2 * 8 * 128), BF16)
        dtih = sbt("dtih", (128, Q), BF16)
        dtil = sbt("dtil", (128, Q), BF16)
        uht = [sbt(f"uh{i}", (128, Q), BF16) for i in range(2)]
        ult = [sbt(f"ul{i}", (128, Q), BF16) for i in range(2)]
        yph = uht
        ypl = ult
        t1h = dtih[0:16, :]
        t1l = dtil[0:16, :]
        t2h = dtih[0:16, :]
        t2l = dtil[0:16, :]
        laut = [sbt("laut_h", (128, NDC * 16), BF16),
                sbt("laut_l", (128, NDC * 16), BF16)]
        lbut = [sbt("lbut_h", (16, 2 * 128), BF16),
                sbt("lbut_l", (16, 2 * 128), BF16)]
        ladt = [sbt("ladt_h", (128, 2 * 16), BF16),
                sbt("ladt_l", (128, 2 * 16), BF16)]
        lbdt = lbut
        pp_f = [res_buf[:, 0:2048], res_buf[:, 2048:4096]]
        pp_h = [x2h[:, 0:2048], x2h[:, 2048:4096]]
        pp_l = [x2l[:, 0:2048], x2l[:, 2048:4096]]
        pp_s = [xst[:, 0:512], xst[:, 0:512]]
        pp_q0 = sbt("pp_q0", (128, 512), I32)
        pp_q = [pp_q0, pp_q0]
        sc_r = [bb_buf[0:2, 0:1024], cb_buf[0:2, 0:1024]]
        ppfr = ["res", "res1"]
        pphr = ["x2p0", "x2p1"]
        ppsr = ["xst0", "xst0"]
        ppqr = ["ppq0", "ppq0"]
        scrr = ["bb", "cb"]

        ps = [es.enter_context(nc.psum_tensor(f"pb{i}", [128, 512], F32))
              for i in range(8)]

        block = es.enter_context(nc.Block())

        def E(eng, build, reads=(), writes=(), sfence=False, is_dma=False):
            A.emit(eng, build, reads, writes, sfence, is_dma)

        def dma(dst, src, reads=(), writes=(), q="sync"):
            E(q, lambda e, d=dst, s=src: e.dma_start(out=d, in_=s),
              reads, writes, is_dma=True)

        def tap(name, ap, shape, res):
            if name not in taps:
                return
            h = nc.dram_tensor("tap_" + name, list(shape), ap.dtype,
                               kind="ExternalOutput").ap()
            tap_names.append(name)
            dma(h[:], ap, reads=(res,), writes=("tapdma",))

        def mm3(psum_ap, lhs, rhs, first, last, res_r, res_w, skip=True):
            lh, ll = lhs
            rh, rl = rhs
            E("tensor", lambda e, a=psum_ap, x=lh, y=rh, st=first:
              e.matmul(a, x, y, start=st, stop=False, skip_group_check=skip),
              res_r, res_w if not last else ())
            E("tensor", lambda e, a=psum_ap, x=lh, y=rl:
              e.matmul(a, x, y, start=False, stop=False, skip_group_check=skip),
              res_r, ())
            E("tensor", lambda e, a=psum_ap, x=ll, y=rh, sp=last:
              e.matmul(a, x, y, start=False, stop=sp, skip_group_check=skip),
              res_r, res_w if last else ())

        def split_pair(src_ap, dst_h, dst_l, rd, wr, hi_eng="vector"):
            E(hi_eng, lambda e, s=src_ap, d=dst_h: e.tensor_copy(out=d, in_=s)
              if hi_eng == "vector" else e.copy(out=d, in_=s), rd, wr)
            E("vector", lambda e, s=src_ap, h=dst_h, d=dst_l: e.tensor_tensor(
                out=d, in0=s, in1=h, op=AL.subtract), rd + wr, wr)

        for dst, src, tg in ((sel16_t, sel16, "sel16"), (sel2_t, sel2, "sel2"),
                             (ones_t, ones_c, "ones"),
                             (convw_t, conv_w, "convw"), (convb_t, conv_b, "convb"),
                             (dtpb_t, dtp_b, "dtpb"), (dvec_t, d_vec, "dvec"),
                             (bup_t, b_up, "bup"), (bdn_t, b_dn, "bdn")):
            dma(dst[:], src[:], writes=(tg,))

        pp_ctr = [0]

        def pp_next():
            p = pp_ctr[0] & 1
            pp_ctr[0] += 1
            return p, ("sync", "scalar")[p]

        def pp_split(p, cols, rows=128):
            split_pair(pp_f[p][0:rows, 0:cols], pp_h[p][0:rows, 0:cols],
                       pp_l[p][0:rows, 0:cols], (ppfr[p],), (pphr[p],))

        for dc in range(NDC):
            for piece in range(4):
                p, q = pp_next()
                dma(pp_f[p][:, 0:2048],
                    w_in[dc * 128:(dc + 1) * 128, piece * 2048:(piece + 1) * 2048],
                    writes=(ppfr[p],), q=q)
                pp_split(p, 2048)
                for jj in range(16):
                    jc = piece * 16 + jj
                    for h in range(2):
                        dma(s_win[h][jc][:, dc * 128:(dc + 1) * 128],
                            [pp_h, pp_l][h][p][:, jj * 128:(jj + 1) * 128],
                            reads=(pphr[p],), writes=("s_win",), q=q)
        for ic in range(NIC):
            p, q = pp_next()
            dma(pp_f[p][:, 0:2048], op_w[ic * 128:(ic + 1) * 128, :],
                writes=(ppfr[p],), q=q)
            pp_split(p, 2048)
            for h in range(2):
                dma(s_opw[h][ic], [pp_h, pp_l][h][p][:, 0:2048],
                    reads=(pphr[p],), writes=("s_opw",), q=q)
        for piece in range(2):
            p, q = pp_next()
            dma(pp_f[p][:, 0:2048], dtp_w[:, piece * 2048:(piece + 1) * 2048],
                writes=(ppfr[p],), q=q)
            pp_split(p, 2048)
            for jj in range(16):
                ic = piece * 16 + jj
                for h in range(2):
                    dma(s_dtp[h][ic],
                        [pp_h, pp_l][h][p][:, jj * 128:(jj + 1) * 128],
                        reads=(pphr[p],), writes=("s_dtp",), q=q)
        for ic in range(NIC):
            p, q = pp_next()
            dma(pp_f[p][:, 0:R], xp_dt[ic * 128:(ic + 1) * 128, :],
                writes=(ppfr[p],), q=q)
            dma(pp_f[p][:, R:R + NS], xp_b[ic * 128:(ic + 1) * 128, :],
                writes=(ppfr[p],), q=q)
            dma(pp_f[p][:, R + NS:R + 2 * NS], xp_c[ic * 128:(ic + 1) * 128, :],
                writes=(ppfr[p],), q=q)
            pp_split(p, R + 2 * NS)
            for h in range(2):
                dma(s_xp[h][ic], [pp_h, pp_l][h][p][:, 0:R + 2 * NS],
                    reads=(pphr[p],), writes=("s_xp",), q=q)
        p, q = pp_next()
        for dc in range(NDC):
            dma(pp_f[p][:, dc * 16:(dc + 1) * 16],
                lau[dc * 128:(dc + 1) * 128, :], writes=(ppfr[p],), q=q)
        pp_split(p, NDC * 16)
        for h in range(2):
            dma(s_lau[h][:, :], [pp_h, pp_l][h][p][:, 0:NDC * 16],
                reads=(pphr[p],), writes=("s_lau",), q=q)
        for half in range(2):
            p, q = pp_next()
            for j in range(NHC // 2):
                hc = half * (NHC // 2) + j
                dma(pp_f[p][:, j * 16:(j + 1) * 16],
                    lad[hc * 128:(hc + 1) * 128, :], writes=(ppfr[p],), q=q)
            pp_split(p, (NHC // 2) * 16)
            for h in range(2):
                dma(s_lad[h][:, half * (NHC // 2) * 16:
                             (half + 1) * (NHC // 2) * 16],
                    [pp_h, pp_l][h][p][:, 0:(NHC // 2) * 16],
                    reads=(pphr[p],), writes=("s_lad",), q=q)
        for piece in range(4):
            p, q = pp_next()
            dma(pp_f[p][0:16, 0:2048], lbu[:, piece * 2048:(piece + 1) * 2048],
                writes=(ppfr[p],), q=q)
            pp_split(p, 2048, rows=16)
            for h in range(2):
                dma(s_lbu[h][:, piece * 2048:(piece + 1) * 2048],
                    [pp_h, pp_l][h][p][0:16, 0:2048],
                    reads=(pphr[p],), writes=("s_lbu",), q=q)
        p, q = pp_next()
        dma(pp_f[p][0:16, 0:2048], lbd[:, :], writes=(ppfr[p],), q=q)
        pp_split(p, 2048, rows=16)
        for h in range(2):
            dma(s_lbd[h][:, :], [pp_h, pp_l][h][p][0:16, 0:2048],
                reads=(pphr[p],), writes=("s_lbd",), q=q)

        def dequant(qsrc, ssrc, dst, n_stripes, n_cols, tile_res):
            for si in range(n_stripes):
                for half in range(n_cols // 512):
                    c0 = half * 512
                    p, q = pp_next()
                    dma(sc_r[p][:, 0:512],
                        ssrc[2 * si:2 * si + 2, c0:c0 + 512],
                        writes=(scrr[p],), q=q)
                    dma(pp_q[p][:, 0:512],
                        qsrc[si * 128:(si + 1) * 128, c0:c0 + 512],
                        writes=(ppqr[p],), q=q)
                    E("tensor", lambda e, p=p: e.matmul(
                        ps[6 + p][:, 0:512], sel2_t[:], sc_r[p][:, 0:512],
                        start=True, stop=True, skip_group_check=True),
                      (scrr[p], "sel2", f"ps{6 + p}"), (f"ps{6 + p}",))
                    E("vector", lambda e, p=p: e.scalar_tensor_tensor(
                        out=pp_f[p][:, 0:512], in0=pp_q[p][:, 0:512],
                        scalar=-7.5, in1=ps[6 + p][:, 0:512],
                        op0=AL.add, op1=AL.mult),
                      (ppqr[p], f"ps{6 + p}"), (ppfr[p], f"ps{6 + p}"))
                    pp_split(p, 512)
                    for jj in range(4):
                        j = half * 4 + jj
                        for h in range(2):
                            dma(dst[h][j][:, si * 128:(si + 1) * 128],
                                [pp_h, pp_l][h][p][:, jj * 128:(jj + 1) * 128],
                                reads=(pphr[p],), writes=(tile_res,), q=q)
        dequant(wup_q, wup_s, s_wup, NDC, HF, "s_wup")
        dequant(wdn_q, wdn_s, s_wdn, NHC, D, "s_wdn")

        p, q = pp_next()
        dma(pp_f[p][:, 0:NIC * NS], a_log[:], writes=(ppfr[p],), q=q)
        E("scalar", lambda e, p=p: e.activation(out=pp_s[p][:, 0:NIC * NS],
                                                in_=pp_f[p][:, 0:NIC * NS],
                                                func=AF.Exp),
          (ppfr[p],), (ppsr[p],))
        E("vector", lambda e, p=p: e.tensor_scalar_mul(
            out=a_buf[:], in0=pp_s[p][:, 0:NIC * NS], scalar1=-1.0),
          (ppsr[p],), ("abuf",))
        E("vector", lambda e: e.memset(state[:], 0.0), (), ("state",))
        E("vector", lambda e: e.memset(tails[:], 0.0), (), ("tails",))
        E("vector", lambda e: e.engine_nop(),
          ("res", "res1", "x2p0", "x2p1", "bb", "cb", "xst0", "xst1"),
          ("res", "res1", "x2p0", "x2p1", "bb", "cb", "xst0", "xst1"))

        def wa_slot(par):
            o = par * NDC * 128
            return ((wa_h[:, o:o + NDC * 128], wa_l[:, o:o + NDC * 128]),
                    f"wa{par}")

        def wb_slot(par, cols=8 * 128):
            o = par * 8 * 128
            return ((wb_h[:, o:o + cols], wb_l[:, o:o + cols]), f"wb{par}")

        for c in range(nch):
            tc0 = c * Q

            for dc in range(NDC):
                xp_ = dc % 2
                dma(xst[:, xp_ * Q:(xp_ + 1) * Q],
                    x_t[dc * 128:(dc + 1) * 128, tc0:tc0 + Q],
                    writes=(f"xst{xp_}",))
                split_pair(xst[:, xp_ * Q:(xp_ + 1) * Q],
                           xh_buf[:, dc * Q:(dc + 1) * Q],
                           xl_buf[:, dc * Q:(dc + 1) * Q],
                           (f"xst{xp_}",), ("xpair",), hi_eng="scalar")

            for ic in range(NIC):
                par = ic % 2
                (wtiles, wres) = wa_slot(par)
                for h in range(2):
                    dma(wtiles[h], s_win[h][ic], reads=("s_win",), writes=(wres,))
                pb = ps[par][:, 0:Q]
                for dc in range(NDC):
                    mm3(pb, (wtiles[0][:, dc * 128:(dc + 1) * 128],
                             wtiles[1][:, dc * 128:(dc + 1) * 128]),
                        (xh_buf[:, dc * Q:(dc + 1) * Q],
                         xl_buf[:, dc * Q:(dc + 1) * Q]),
                        first=(dc == 0), last=(dc == NDC - 1),
                        res_r=(wres, "xpair", f"ps{par}"), res_w=(f"ps{par}",))
                hb = hidt[par]
                E("scalar", lambda e, ic=ic, hb=hb: e.copy(
                    out=hb[:, 0:3], in_=tails[:, ic * 4:ic * 4 + 3]),
                  ("tails",), (f"hid{par}",))
                E("scalar", lambda e, pb=pb, hb=hb: e.copy(
                    out=hb[:, 3:3 + Q], in_=pb),
                  (f"ps{par}",), (f"hid{par}", f"ps{par}"))
                E("scalar", lambda e, ic=ic, hb=hb: e.copy(
                    out=tails[:, ic * 4:ic * 4 + 3], in_=hb[:, Q:Q + 3]),
                  (f"hid{par}",), ("tails",))
                E("vector", lambda e, ic=ic, hb=hb, par=par: e.tensor_scalar(
                    out=tB[par][:], in0=hb[:, 0:Q],
                    scalar1=convw_t[:, ic * 4:ic * 4 + 1],
                    scalar2=convb_t[:, ic:ic + 1], op0=AL.mult, op1=AL.add),
                  (f"hid{par}", "convw", "convb", f"tB{par}"), (f"tB{par}",))
                E("vector", lambda e, ic=ic, hb=hb, par=par:
                  e.scalar_tensor_tensor(
                      out=tC[par][:], in0=hb[:, 1:1 + Q],
                      scalar=convw_t[:, ic * 4 + 1:ic * 4 + 2], in1=tB[par][:],
                      op0=AL.mult, op1=AL.add),
                  (f"hid{par}", f"tB{par}", "convw", f"tC{par}"), (f"tC{par}",))
                E("vector", lambda e, ic=ic, hb=hb, par=par:
                  e.scalar_tensor_tensor(
                      out=tB[par][:], in0=hb[:, 2:2 + Q],
                      scalar=convw_t[:, ic * 4 + 2:ic * 4 + 3], in1=tC[par][:],
                      op0=AL.mult, op1=AL.add),
                  (f"hid{par}", f"tC{par}", "convw", f"tB{par}"), (f"tB{par}",))
                E("vector", lambda e, ic=ic, hb=hb, par=par:
                  e.scalar_tensor_tensor(
                      out=tC[par][:], in0=hb[:, 3:3 + Q],
                      scalar=convw_t[:, ic * 4 + 3:ic * 4 + 4], in1=tB[par][:],
                      op0=AL.mult, op1=AL.add),
                  (f"hid{par}", f"tB{par}", "convw", f"tC{par}"), (f"tC{par}",))
                E("scalar", lambda e, ic=ic, par=par: e.activation(
                    out=u_buf[:, ic * Q:(ic + 1) * Q], in_=tC[par][:],
                    func=AF.Silu), (f"tC{par}",), ("u",))
            if c == tapc:
                tap("u", u_buf[:], (128, NIC * Q), "u")

            for ic in range(NIC):
                par = ic % 2
                (wtiles, wres) = wb_slot(par, cols=R + 2 * NS)
                for h in range(2):
                    dma(wtiles[h], s_xp[h][ic], reads=("s_xp",), writes=(wres,))
                split_pair(u_buf[:, ic * Q:(ic + 1) * Q],
                           uht[par][:], ult[par][:], ("u",), (f"upair{par}",),
                           hi_eng="scalar")
                upair = (uht[par][:], ult[par][:])
                mm3(ps[2][:, 0:Q], (wtiles[0][:, 0:R], wtiles[1][:, 0:R]),
                    upair, first=(ic == 0), last=(ic == NIC - 1),
                    res_r=(wres, f"upair{par}", "ps2"), res_w=("ps2",))
                mm3(ps[3][0:NS, 0:Q],
                    (wtiles[0][:, R:R + NS], wtiles[1][:, R:R + NS]),
                    upair, first=(ic == 0), last=(ic == NIC - 1),
                    res_r=(wres, f"upair{par}", "ps3"), res_w=("ps3",))
                mm3(ps[4][0:NS, 0:Q],
                    (wtiles[0][:, R + NS:R + 2 * NS],
                     wtiles[1][:, R + NS:R + 2 * NS]),
                    upair, first=(ic == 0), last=(ic == NIC - 1),
                    res_r=(wres, f"upair{par}", "ps4"), res_w=("ps4",))
            E("scalar", lambda e: e.copy(out=tG[:], in_=ps[2][:, 0:Q]),
              ("ps2",), ("tG", "ps2"))
            E("scalar", lambda e: e.copy(out=bt_t[:], in_=ps[3][0:NS, 0:Q]),
              ("ps3",), ("btct", "ps3"))
            E("scalar", lambda e: e.copy(out=ct_t[:], in_=ps[4][0:NS, 0:Q]),
              ("ps4",), ("btct", "ps4"))
            if c == tapc:
                tap("dtin", tG[:], (128, Q), "tG")
                tap("bt", bt_t[:], (NS, Q), "btct")
                tap("ct", ct_t[:], (NS, Q), "btct")
            split_pair(tG[:], dtih[:], dtil[:], ("tG",), ("dtipair",),
                       hi_eng="scalar")

            for which, src_t, dst_buf, dres in ((0, bt_t, bb_buf, "bb"),
                                                (1, ct_t, cb_buf, "cb")):
                for n in range(NS):
                    p2 = n % 2
                    g, j = n // GN, n % GN
                    E("tensor", lambda e, n=n, p2=p2, s=src_t: e.matmul(
                        ps[5 + p2][:, 0:Q], sel16_t[:, n * 128:(n + 1) * 128],
                        s[:], start=True, stop=True, skip_group_check=True),
                      ("btct", "sel16", f"ps{5 + p2}"), (f"ps{5 + p2}",))
                    E("scalar", lambda e, g=g, j=j, p2=p2, d=dst_buf: e.copy(
                        out=d[:, g * GN * Q:(g + 1) * GN * Q].rearrange(
                            "p (q n) -> p n q", n=GN)[:, j, :],
                        in_=ps[5 + p2][:, 0:Q]),
                      (f"ps{5 + p2}",), (dres, f"ps{5 + p2}"))
            if c == tapc:
                tap("bb", bb_buf[:], (128, NS * Q), "bb")

            if c == tapc:
                tap("state_in", state[:], (128, NIC * NS), "state")
            E("vector", lambda e: e.engine_nop(),
              ("res", "z0", "z1", "e0", "e1"),
              ("res", "z0", "z1", "e0", "e1"))
            for ic in range(NIC):
                par = ic % 2
                (wtiles, wres) = wb_slot(par, cols=128)
                for h in range(2):
                    dma(wtiles[h], s_dtp[h][ic], reads=("s_dtp",), writes=(wres,))
                pdt = [ps[7], ps[4]][par]
                pdtr = ["ps7", "ps4"][par]
                mm3(pdt[:, 0:Q], (wtiles[0], wtiles[1]), (dtih[:], dtil[:]),
                    first=True, last=True,
                    res_r=(wres, "dtipair", pdtr), res_w=(pdtr,))
                E("scalar", lambda e, ic=ic, pdt=pdt, par=par: e.activation(
                    out=tt1[par][:], in_=pdt[:, 0:Q], func=AF.Exp,
                    bias=dtpb_t[:, ic:ic + 1]),
                  (pdtr, "dtpb", f"tt1{par}"), (f"tt1{par}", pdtr))
                E("scalar", lambda e, par=par: e.activation(
                    out=dtt[par], in_=tt1[par][:], func=AF.Ln, bias=1.0),
                  (f"tt1{par}", f"dtt{par}"), (f"dtt{par}",))
                if c == tapc and ic == 0:
                    tap("dt0", dtt[0], (128, Q), "dtt0")
                E("vector", lambda e, ic=ic, par=par: e.tensor_tensor(
                    out=dtut[par], in0=dtt[par],
                    in1=u_buf[:, ic * Q:(ic + 1) * Q],
                    op=AL.mult), (f"dtt{par}", "u", f"dtu{par}"), (f"dtu{par}",))
                (gtiles, gres) = wa_slot(par)
                for h in range(2):
                    dma(gtiles[h], s_win[h][NIC + ic], reads=("s_win",),
                        writes=(gres,))
                pbg = ps[5 + par][:, 0:Q]
                for dc in range(NDC):
                    mm3(pbg, (gtiles[0][:, dc * 128:(dc + 1) * 128],
                              gtiles[1][:, dc * 128:(dc + 1) * 128]),
                        (xh_buf[:, dc * Q:(dc + 1) * Q],
                         xl_buf[:, dc * Q:(dc + 1) * Q]),
                        first=(dc == 0), last=(dc == NDC - 1),
                        res_r=(gres, "xpair", f"ps{5 + par}"),
                        res_w=(f"ps{5 + par}",))
                E("scalar", lambda e, p=pbg, par=par: e.activation(
                    out=get_[par], in_=p, func=AF.Exp, scale=-1.0),
                  (f"ps{5 + par}", f"ge{par}"), (f"ge{par}",))
                E("scalar", lambda e, p=pbg, par=par: e.copy(
                    out=ggt[par], in_=p),
                  (f"ps{5 + par}", f"gg{par}"), (f"gg{par}", f"ps{5 + par}"))

                gq = GN * Q

                def emit_da(g, par=par, ic=ic):
                    pg = g % 2
                    for j in range(GN):
                        n = g * GN + j
                        dst = sc1p[pg].rearrange("p (q n) -> p n q",
                                                 n=GN)[:, j, :]
                        E("scalar", lambda e, d=dst, n=n: e.activation(
                            out=d, in_=dtt[par], func=AF.Exp,
                            scale=a_buf[:, ic * NS + n:ic * NS + n + 1]),
                          (f"dtt{par}", "abuf", f"e{pg}"), (f"e{pg}",))

                emit_da(0)
                emit_da(1)
                for g in range(NG):
                    pg = g % 2
                    bb_g = bb_buf[:, g * gq:(g + 1) * gq]
                    cb_g = cb_buf[:, g * gq:(g + 1) * gq]
                    dtu_b = dtut[par].unsqueeze(2).broadcast_to([128, Q, GN])
                    E("vector", lambda e, db=dtu_b, b=bb_g, pg=pg:
                      e.tensor_tensor(out=sc0p[pg].rearrange(
                          "p (q n) -> p q n", n=GN), in0=db,
                          in1=b.rearrange("p (q n) -> p q n", n=GN),
                          op=AL.mult),
                      (f"dtu{par}", "bb", f"z{pg}", f"e{pg}"), (f"z{pg}",))
                    for j in range(GN):
                        n = g * GN + j
                        da_ap = sc1p[pg].rearrange("p (q n) -> p n q",
                                                   n=GN)[:, j, :]
                        w_ap = sc0p[pg].rearrange("p (q n) -> p n q",
                                                  n=GN)[:, j, :]
                        h_ap = scanh[:, 0:gq].rearrange("p (q n) -> p n q",
                                                        n=GN)[:, j, :]
                        E("vector", lambda e, a=da_ap, w=w_ap, o=h_ap,
                          ini=state[:, ic * NS + n:ic * NS + n + 1]:
                          e.tensor_tensor_scan(o, a, w, ini, AL.mult, AL.add),
                          (f"z{pg}", f"e{pg}", "state", "sc2"), ("sc2",))
                    E("vector", lambda e, cb=cb_g, gq=gq, pg=pg: e.tensor_tensor(
                        out=sc1p[pg], in0=scanh[:, 0:gq], in1=cb,
                        op=AL.mult), ("sc2", "cb", f"e{pg}"), (f"e{pg}",))
                    E("vector", lambda e, g=g, ic=ic, gq=gq: e.tensor_copy(
                        out=state[:, ic * NS + g * GN:ic * NS + (g + 1) * GN],
                        in_=scanh[:, gq - GN:gq]), ("sc2", "state"), ("state",))
                    E("vector", lambda e, g=g, gq=gq, pg=pg: e.tensor_reduce(
                        out=y4[:, g * Q:(g + 1) * Q],
                        in_=sc1p[pg].rearrange("p (q n) -> p q n", n=GN),
                        axis=AX.X, op=AL.add), (f"e{pg}", "y4"), ("y4",))
                    if g + 2 < NG:
                        emit_da(g + 2)
                E("vector", lambda e, par=par: e.tensor_tensor(
                    out=tt2[par][:], in0=y4[:, 0:Q], in1=y4[:, Q:2 * Q],
                    op=AL.add), ("y4", f"tt2{par}"), (f"tt2{par}",))
                E("vector", lambda e, par=par: e.tensor_tensor(
                    out=tt1[par][:], in0=y4[:, 2 * Q:3 * Q],
                    in1=y4[:, 3 * Q:4 * Q],
                    op=AL.add), ("y4", f"tt1{par}"), (f"tt1{par}",),
                  sfence=True)
                E("scalar", lambda e, par=par: e.activation(
                    out=dtt[par], in_=get_[par], func=AF.Copy, bias=1.0),
                  (f"ge{par}", f"dtt{par}"), (f"dtt{par}",))
                E("vector", lambda e, par=par: e.reciprocal(
                    out=get_[par], in_=dtt[par]),
                  (f"dtt{par}", f"ge{par}"), (f"ge{par}",))
                E("vector", lambda e, par=par: e.tensor_tensor(
                    out=dtt[par], in0=ggt[par], in1=get_[par], op=AL.mult),
                  (f"gg{par}", f"ge{par}", f"dtt{par}"), (f"dtt{par}",))
                E("vector", lambda e, par=par: e.tensor_tensor(
                    out=dtut[par], in0=tt2[par][:], in1=tt1[par][:],
                    op=AL.add),
                  (f"tt2{par}", f"tt1{par}", f"dtu{par}"), (f"dtu{par}",))
                E("vector", lambda e, ic=ic, par=par: e.scalar_tensor_tensor(
                    out=tt1[par][:], in0=u_buf[:, ic * Q:(ic + 1) * Q],
                    scalar=dvec_t[:, ic:ic + 1], in1=dtut[par],
                    op0=AL.mult, op1=AL.add),
                  ("u", f"dtu{par}", "dvec", f"tt1{par}"), (f"tt1{par}",))
                E("vector", lambda e, ic=ic, par=par: e.tensor_tensor(
                    out=u_buf[:, ic * Q:(ic + 1) * Q], in0=tt1[par][:],
                    in1=dtt[par], op=AL.mult),
                  (f"tt1{par}", f"dtt{par}", "u"), ("u",))
                if c == tapc and ic == 0:
                    tap("yfin", u_buf[:, 0:Q], (128, Q), "u")
            if c == tapc:
                tap("ymall", u_buf[:], (128, NIC * Q), "u")
            E("vector", lambda e: e.engine_nop(),
              ("res", "z0", "z1", "e0", "e1"),
              ("res", "z0", "z1", "e0", "e1"))

            for grp in range(2):
                for ic in range(NIC):
                    par = ic % 2
                    split_pair(u_buf[:, ic * Q:(ic + 1) * Q],
                               yph[par][:], ypl[par][:], ("u",),
                               (f"upair{par}",), hi_eng="scalar")
                    (wtiles, wres) = wb_slot(par)
                    for h in range(2):
                        dma(wtiles[h],
                            s_opw[h][ic][:, grp * 8 * 128:(grp + 1) * 8 * 128],
                            reads=("s_opw",), writes=(wres,))
                    for j in range(8):
                        pa = ps[j][:, 0:Q]
                        mm3(pa, (wtiles[0][:, j * 128:(j + 1) * 128],
                                 wtiles[1][:, j * 128:(j + 1) * 128]),
                            (yph[par][:], ypl[par][:]),
                            first=(ic == 0), last=(ic == NIC - 1),
                            res_r=(wres, f"upair{par}", f"ps{j}"),
                            res_w=(f"ps{j}",))
                for j in range(8):
                    dc = grp * 8 + j
                    pa = ps[j][:, 0:Q]
                    E("vector", lambda e, pa=pa, dc=dc: e.tensor_tensor(
                        out=tA[0][:], in0=pa, in1=xh_buf[:, dc * Q:(dc + 1) * Q],
                        op=AL.add),
                      (f"ps{j}", "xpair", "tA0"),
                      ("tA0", f"ps{j}"))
                    E("vector", lambda e, dc=dc: e.tensor_tensor(
                        out=res_buf[:, dc * Q:(dc + 1) * Q], in0=tA[0][:],
                        in1=xl_buf[:, dc * Q:(dc + 1) * Q], op=AL.add),
                      ("tA0", "xpair"), ("res",))
            if c == tapc:
                tap("attnres", res_buf[:], (128, NDC * Q), "res")

            def layer_norm(src_buf, src_res, out_cb, psA, psB, psC):
                for dc in range(NDC):
                    dp = dc % 2
                    E("scalar", lambda e, dc=dc, dp=dp: e.activation(
                        out=tA[dp][:], in_=src_buf[:, dc * Q:(dc + 1) * Q],
                        func=AF.Square), (src_res, f"tA{dp}"), (f"tA{dp}",))
                    E("tensor", lambda e, dc=dc: e.matmul(
                        ps[psA][0:1, 0:Q], ones_t[:],
                        src_buf[:, dc * Q:(dc + 1) * Q],
                        start=(dc == 0), stop=(dc == NDC - 1),
                        skip_group_check=True),
                      (src_res, "ones", f"ps{psA}"), (f"ps{psA}",))
                    E("tensor", lambda e, dc=dc, dp=dp: e.matmul(
                        ps[psB][0:1, 0:Q], ones_t[:], tA[dp][:],
                        start=(dc == 0), stop=(dc == NDC - 1),
                        skip_group_check=True),
                      (f"tA{dp}", "ones", f"ps{psB}"), (f"ps{psB}",))
                E("scalar", lambda e: e.activation(
                    out=st1[0:1, 0:Q], in_=ps[psA][0:1, 0:Q], func=AF.Copy,
                    scale=1.0 / D), (f"ps{psA}", "y4"), ("y4", f"ps{psA}"))
                E("scalar", lambda e: e.activation(
                    out=st1[0:1, Q:2 * Q], in_=st1[0:1, 0:Q], func=AF.Square),
                  ("y4",), ("y4",))
                E("scalar", lambda e: e.activation(
                    out=st1[0:1, 2 * Q:3 * Q], in_=ps[psB][0:1, 0:Q],
                    func=AF.Copy, scale=1.0 / D),
                  (f"ps{psB}", "y4"), ("y4", f"ps{psB}"))
                E("vector", lambda e: e.tensor_tensor(
                    out=st1[0:1, 3 * Q:4 * Q], in0=st1[0:1, 2 * Q:3 * Q],
                    in1=st1[0:1, Q:2 * Q], op=AL.subtract), ("y4",), ("y4",))
                E("vector", lambda e: e.tensor_scalar_add(
                    out=st1[0:1, Q:2 * Q], in0=st1[0:1, 3 * Q:4 * Q],
                    scalar1=EPS), ("y4",), ("y4",))
                E("scalar", lambda e: e.activation(
                    out=st1[0:1, 2 * Q:3 * Q], in_=st1[0:1, Q:2 * Q],
                    func=AF.Ln), ("y4",), ("y4",))
                E("scalar", lambda e: e.activation(
                    out=st1[0:1, 3 * Q:4 * Q], in_=st1[0:1, 2 * Q:3 * Q],
                    func=AF.Exp, scale=-0.5), ("y4",), ("y4",))
                E("tensor", lambda e: e.matmul(
                    ps[psC][:, 0:Q], bc1_t[:], st1[0:1, 0:Q],
                    start=True, stop=True, skip_group_check=True),
                  ("y4", "sel16", f"ps{psC}"), (f"ps{psC}",))
                E("tensor", lambda e: e.matmul(
                    ps[psC][:, Q:2 * Q], bc1_t[:], st1[0:1, 3 * Q:4 * Q],
                    start=True, stop=True, skip_group_check=True),
                  ("y4", "sel16", f"ps{psC}"), (f"ps{psC}",))
                E("scalar", lambda e: e.copy(
                    out=mb_t, in_=ps[psC][:, 0:Q]),
                  (f"ps{psC}", "y4"), ("y4",))
                E("scalar", lambda e: e.copy(
                    out=rb_t, in_=ps[psC][:, Q:2 * Q]),
                  (f"ps{psC}", "y4"), ("y4", f"ps{psC}"))
                for dc in range(NDC):
                    dp = dc % 2
                    E("vector", lambda e, dc=dc, dp=dp: e.tensor_tensor(
                        out=tA[dp][:], in0=src_buf[:, dc * Q:(dc + 1) * Q],
                        in1=mb_t, op=AL.subtract),
                      (src_res, "y4", f"tA{dp}"), (f"tA{dp}",))
                    E("vector", lambda e, dc=dc, dp=dp: e.tensor_tensor(
                        out=tB[dp][:], in0=tA[dp][:], in1=rb_t, op=AL.mult),
                      (f"tA{dp}", "y4", f"tB{dp}"), (f"tB{dp}",))
                    out_cb(dc)

            def ln1_out(dc):
                split_pair(tB[dc % 2][:],
                           x1h_buf[:, dc * Q:(dc + 1) * Q],
                           x1l_buf[:, dc * Q:(dc + 1) * Q],
                           (f"tB{dc % 2}",), ("x1pair",), hi_eng="scalar")
            layer_norm(res_buf, "res", ln1_out, 4, 5, 6)
            if c == tapc:
                tap("x1", x1h_buf[:], (128, NDC * Q), "x1pair")

            if c == 0:
                for h in range(2):
                    dma(laut[h][:], s_lau[h][:, :], reads=("s_lau",),
                        writes=("laut",))

            for dc in range(NDC):
                mm3(ps[2][0:16, 0:Q],
                    (laut[0][:, dc * 16:(dc + 1) * 16],
                     laut[1][:, dc * 16:(dc + 1) * 16]),
                    (x1h_buf[:, dc * Q:(dc + 1) * Q],
                     x1l_buf[:, dc * Q:(dc + 1) * Q]),
                    first=(dc == 0), last=(dc == NDC - 1),
                    res_r=("laut", "x1pair", "ps2"), res_w=("ps2",))
            E("scalar", lambda e: e.copy(out=tG[0:16, :], in_=ps[2][0:16, 0:Q]),
              ("ps2", "tG"), ("tG", "ps2"))
            split_pair(tG[0:16, :], t1h[:], t1l[:], ("tG",), ("dtipair",),
                       hi_eng="scalar")

            for hbk in range(8):
                hpar = hbk % 2
                for jj in range(8):
                    hc = hbk * 8 + jj
                    par = jj % 2
                    (wtiles, wres) = wa_slot(par)
                    for h in range(2):
                        dma(wtiles[h], s_wup[h][hc], reads=("s_wup",),
                            writes=(wres,))
                    pb = ps[par][:, 0:Q]
                    for dc in range(NDC):
                        mm3(pb, (wtiles[0][:, dc * 128:(dc + 1) * 128],
                                 wtiles[1][:, dc * 128:(dc + 1) * 128]),
                            (x1h_buf[:, dc * Q:(dc + 1) * Q],
                             x1l_buf[:, dc * Q:(dc + 1) * Q]),
                            first=(dc == 0), last=False,
                            res_r=(wres, "x1pair", f"ps{par}"),
                            res_w=(f"ps{par}",))
                    for h in range(2):
                        dma(lbut[h][:, par * 128:(par + 1) * 128],
                            s_lbu[h][:, hc * 128:(hc + 1) * 128],
                            reads=("s_lbu",), writes=(f"lbu{par}",))
                    mm3(pb, (lbut[0][:, par * 128:(par + 1) * 128],
                             lbut[1][:, par * 128:(par + 1) * 128]),
                        (t1h[:], t1l[:]), first=False, last=True,
                        res_r=(f"lbu{par}", "dtipair", f"ps{par}"),
                        res_w=(f"ps{par}",))
                    E("scalar", lambda e, pb=pb, hc=hc, par=par: e.activation(
                        out=tA[par][:], in_=pb, func=AF.Relu,
                        bias=bup_t[:, hc:hc + 1]),
                      (f"ps{par}", "bup", f"tA{par}"), (f"tA{par}", f"ps{par}"))
                    o = (hpar * 8 + jj) * Q
                    split_pair(tA[par][:], x2h[:, o:o + Q], x2l[:, o:o + Q],
                               (f"tA{par}",), (f"x2p{hpar}",), hi_eng="scalar")
                    for h in range(2):
                        dma(ladt[h][:, par * 16:(par + 1) * 16],
                            s_lad[h][:, hc * 16:(hc + 1) * 16],
                            reads=("s_lad",), writes=(f"lad{par}",))
                    mm3(ps[3][0:16, 0:Q],
                        (ladt[0][:, par * 16:(par + 1) * 16],
                         ladt[1][:, par * 16:(par + 1) * 16]),
                        (x2h[:, o:o + Q], x2l[:, o:o + Q]),
                        first=(hc == 0), last=(hc == NHC - 1),
                        res_r=(f"lad{par}", f"x2p{hpar}", "ps3"), res_w=("ps3",))
                for dc in range(NDC):
                    par = dc % 2
                    (wtiles, wres) = wb_slot(par)
                    for h in range(2):
                        dma(wtiles[h],
                            s_wdn[h][dc][:, hbk * 8 * 128:(hbk + 1) * 8 * 128],
                            reads=("s_wdn",), writes=(wres,))
                    pb = ps[4 + par][:, 0:Q]
                    for jj in range(8):
                        o = (hpar * 8 + jj) * Q
                        mm3(pb, (wtiles[0][:, jj * 128:(jj + 1) * 128],
                                 wtiles[1][:, jj * 128:(jj + 1) * 128]),
                            (x2h[:, o:o + Q], x2l[:, o:o + Q]),
                            first=(jj == 0), last=(jj == 7 and hbk != 7),
                            res_r=(wres, f"x2p{hpar}", f"ps{4 + par}"),
                            res_w=(f"ps{4 + par}",))
                    if hbk == 7:
                        if dc == 0:
                            E("scalar", lambda e: e.copy(
                                out=tG[0:16, :], in_=ps[3][0:16, 0:Q]),
                              ("ps3", "tG"), ("tG", "ps3"))
                            split_pair(tG[0:16, :], t2h[:], t2l[:], ("tG",),
                                       ("dtipair",), hi_eng="scalar")
                        for h in range(2):
                            dma(lbdt[h][:, par * 128:(par + 1) * 128],
                                s_lbd[h][:, dc * 128:(dc + 1) * 128],
                                reads=("s_lbd",), writes=(f"lbu{par}",))
                        mm3(pb, (lbdt[0][:, par * 128:(par + 1) * 128],
                                 lbdt[1][:, par * 128:(par + 1) * 128]),
                            (t2h[:], t2l[:]), first=False, last=True,
                            res_r=(f"lbu{par}", "dtipair", f"ps{4 + par}"),
                            res_w=(f"ps{4 + par}",))
                    cur = hbk % 2
                    prv = 1 - cur
                    dst = u_buf[:, (cur * NDC + dc) * Q:(cur * NDC + dc + 1) * Q]
                    srcp = u_buf[:, (prv * NDC + dc) * Q:(prv * NDC + dc + 1) * Q]
                    if hbk == 0:
                        E("scalar", lambda e, pb=pb, d=dst: e.copy(
                            out=d, in_=pb),
                          (f"ps{4 + par}", "u"), ("u", f"ps{4 + par}"))
                    else:
                        E("vector", lambda e, pb=pb, d=dst, s=srcp:
                          e.tensor_tensor(out=d, in0=pb, in1=s, op=AL.add),
                          (f"ps{4 + par}", "u"), ("u", f"ps{4 + par}"))
            for dc in range(NDC):
                y2fin = u_buf[:, (NDC + dc) * Q:(NDC + dc + 1) * Q]
                E("vector", lambda e, s=y2fin, dc=dc: e.tensor_scalar_add(
                    out=tA[0][:], in0=s, scalar1=bdn_t[:, dc:dc + 1]),
                  ("u", "bdn", "tA0"), ("tA0",))
                E("vector", lambda e, dc=dc: e.tensor_tensor(
                    out=tB[1][:], in0=tA[0][:],
                    in1=x1h_buf[:, dc * Q:(dc + 1) * Q], op=AL.add),
                  ("tA0", "x1pair", "tB1"), ("tB1",))
                E("vector", lambda e, dc=dc: e.tensor_tensor(
                    out=res_buf[:, dc * Q:(dc + 1) * Q], in0=tB[1][:],
                    in1=x1l_buf[:, dc * Q:(dc + 1) * Q], op=AL.add),
                  ("tB1", "x1pair"), ("res",))
            if c == tapc:
                tap("ffnres", res_buf[:], (128, NDC * Q), "res")

            def ln2_out(dc, c=c):
                E("sync", lambda e, dc=dc, c=c: e.dma_start(
                    out=out_t[dc * 128:(dc + 1) * 128, c * Q:(c + 1) * Q],
                    in_=tB[dc % 2][:]), (f"tB{dc % 2}",), ("outdma",))
            layer_norm(res_buf, "res", ln2_out, 4, 5, 6)

        A.final_waits = ["sync", "gpsimd"]
        A.assemble(nc, block)

    return nc, tap_names


def prep_inputs(inputs):
    f32 = np.float32
    g = {k: np.asarray(v) for k, v in inputs.items()}
    x = g["L0_attn_input_states"]
    base = {}
    base["w_in"] = np.ascontiguousarray(g["L0_attn_in_proj_w"], f32)
    cw = np.asarray(g["L0_attn_conv_w"], f32).reshape(IN, 4)
    base["conv_w"] = np.ascontiguousarray(
        cw.reshape(NIC, 128, 4).transpose(1, 0, 2).reshape(128, NIC * 4))
    base["conv_b"] = np.ascontiguousarray(
        np.asarray(g["L0_attn_conv_b"], f32).reshape(NIC, 128).T)
    xp = np.asarray(g["L0_attn_x_proj_w"], f32)
    base["xp_dt"] = np.ascontiguousarray(xp[:, 0:R])
    base["xp_b"] = np.ascontiguousarray(xp[:, R:R + NS])
    base["xp_c"] = np.ascontiguousarray(xp[:, R + NS:R + 2 * NS])
    base["dtp_w"] = np.ascontiguousarray(g["L0_attn_dt_proj_w"], f32)
    base["dtp_b"] = np.ascontiguousarray(
        np.asarray(g["L0_attn_dt_proj_b"], f32).reshape(NIC, 128).T)
    base["a_log"] = np.ascontiguousarray(
        np.asarray(g["L0_attn_A_log"], f32).reshape(NIC, 128, NS)
        .transpose(1, 0, 2).reshape(128, NIC * NS))
    base["d_vec"] = np.ascontiguousarray(
        np.asarray(g["L0_attn_D"], f32).reshape(NIC, 128).T)
    base["op_w"] = np.ascontiguousarray(g["L0_attn_out_proj_w"], f32)
    base["wup_q"] = np.ascontiguousarray(np.asarray(g["L0_ffn_w_up_q"]).T)
    base["wup_s"] = np.ascontiguousarray(np.asarray(g["L0_ffn_w_up_scale"], f32).T)
    base["b_up"] = np.ascontiguousarray(
        np.asarray(g["L0_ffn_b_up"], f32).reshape(NHC, 128).T)
    base["lau"] = np.ascontiguousarray(g["L0_ffn_w_up_lora_a"], f32)
    base["lbu"] = np.ascontiguousarray(g["L0_ffn_w_up_lora_b"], f32)
    base["wdn_q"] = np.ascontiguousarray(np.asarray(g["L0_ffn_w_down_q"]).T)
    base["wdn_s"] = np.ascontiguousarray(
        np.asarray(g["L0_ffn_w_down_scale"], f32).T)
    base["b_dn"] = np.ascontiguousarray(
        np.asarray(g["L0_ffn_b_down"], f32).reshape(NDC, 128).T)
    base["lad"] = np.ascontiguousarray(g["L0_ffn_w_down_lora_a"], f32)
    base["lbd"] = np.ascontiguousarray(g["L0_ffn_w_down_lora_b"], f32)
    sel16v = np.zeros((NS, NS * 128), f32)
    for n in range(NS):
        sel16v[n, n * 128:(n + 1) * 128] = 1.0
    base["sel16"] = sel16v
    sel2v = np.zeros((2, 128), f32)
    sel2v[0, 0:64] = 1.0
    sel2v[1, 64:128] = 1.0
    base["sel2"] = sel2v
    base["ones_c"] = np.ones((128, 1), f32)

    in_maps = []
    for cid in range(NCORES):
        m = dict(base)
        m["x_t"] = np.ascontiguousarray(x[cid].T.astype(f32))
        in_maps.append(m)
    return in_maps


def kernel(**inputs):
    in_maps = prep_inputs(inputs)
    nc, _ = build_program()
    res = run_bass_kernel_spmd(nc, in_maps, core_ids=list(range(NCORES)))
    out = np.stack([np.asarray(r["out_t"]).T for r in res.results])
    return np.ascontiguousarray(out).astype(np.float32)
